# version 01
# baseline (speedup 1.0000x reference)
import numpy as np
import concourse.bass as bass
import concourse.mybir as mybir
from concourse.bass_utils import run_bass_kernel_spmd
from contextlib import ExitStack

F32 = mybir.dt.float32
I32 = mybir.dt.int32
AF = mybir.ActivationFunctionType
ALU = mybir.AluOpType
AX = mybir.AxisListType

S = 4096
DM = 1024
FF = 3072
DI = 2048
DSN = 16
NH = 16
DK = 64
TB = 512
NT = S // TB
LN_EPS = 1e-5
QMAX0 = 32767.0
LN2 = float(np.log(2.0))
NCORES = 8

DEBUG = ()
STOP_AT = None


class Ctx:

    def __init__(self, nc, es):
        self.nc = nc
        self.ch = es.enter_context(nc.semaphore("chain"))
        self.t = 0
        self.block = None
        self.stopped = False
        self._eng = None
        self._dma = False
        self._fns = []

    class _SmallGuard:

        SMALL = 512
        _PASS = ("drain", "wait_ge", "nop", "engine_nop")

        def __init__(self, eng):
            self._e = eng

        def __getattr__(self, name):
            f = getattr(self._e, name)
            if name in Ctx._SmallGuard._PASS or not callable(f):
                return f

            def wrap(*a, **kw):
                r = f(*a, **kw)
                out = a[0] if a else kw.get("out")
                fs = None
                try:
                    fs = out.free_size()
                except Exception:
                    pass
                if fs is not None and fs < Ctx._SmallGuard.SMALL:
                    self._e.drain()
                return r

            return wrap

    class _DmaRec:

        def __init__(self, eng):
            self._eng = eng
            self.insts = []

        def dma_start(self, **kw):
            i = self._eng.dma_start(**kw)
            self.insts.append(i)
            return i

        def wait_ge(self, *a):
            return self._eng.wait_ge(*a)

    def _flush(self):
        if not self._fns:
            return
        dec = getattr(self.block, self._eng)
        t0 = self.t
        fns = self._fns
        self._fns = []
        is_dma = self._dma
        eng_name = self._eng
        n = [0]

        def body(eng):
            if t0 > 0:
                eng.wait_ge(self.ch, t0)
            if is_dma:
                rec = Ctx._DmaRec(eng)
                for fn in fns:
                    fn(rec)
                assert rec.insts
                for i in rec.insts:
                    i.then_inc(self.ch, 16)
                n[0] = 16 * len(rec.insts)
            else:
                geng = eng if eng_name in ("tensor", "sync") \
                    else Ctx._SmallGuard(eng)
                for fn in fns:
                    last = fn(geng)
                assert last is not None, "step fn must return its last instruction"
                last.then_inc(self.ch, 1)
                n[0] = 1

        dec(body)
        assert n[0] > 0
        self.t = t0 + n[0]

    def on(self, eng_name, fn, dma=False, force_wait=False):
        if self.stopped:
            return
        if self._fns and self._eng == eng_name and self._dma == dma                 and not force_wait:
            self._fns.append(fn)
            return
        self._flush()
        self._eng = eng_name
        self._dma = dma
        self._fns = [fn]

    def final_wait(self, eng_name):
        self._flush()
        dec = getattr(self.block, eng_name)
        t0 = self.t

        def body(eng):
            eng.wait_ge(self.ch, t0)

        dec(body)


def _host_constants():
    dk = DK
    inv_freq = (np.float32(10000.0) ** (np.arange(0, dk, 2, dtype=np.float32)
                                        / np.float32(dk))).astype(np.float32)
    ang = (np.arange(S, dtype=np.float32)[:, None] / inv_freq[None, :]).astype(np.float32)
    cos = np.cos(ang).astype(np.float32)
    sin = np.sin(ang).astype(np.float32)
    cos64 = np.repeat(cos.T, 2, axis=0)
    sin64 = np.repeat(sin.T, 2, axis=0)
    cos128 = np.concatenate([cos64, cos64], axis=0)
    sgn = np.where(np.arange(64) % 2 == 0, -1.0, 1.0).astype(np.float32)[:, None]
    sin64s = (sin64 * sgn).astype(np.float32)
    sin128 = np.concatenate([sin64s, sin64s], axis=0)
    masks = []
    for r in range(4):
        kk = np.arange(128)[:, None] + 128 * r
        qq = np.arange(TB)[None, :]
        masks.append((kk <= qq).astype(np.float32))
    return {
        "c_cos": np.ascontiguousarray(cos128),
        "c_sin": np.ascontiguousarray(sin128),
        "c_mask": np.ascontiguousarray(np.concatenate(masks, axis=1)),
        "c_ident": np.eye(128, dtype=np.float32),
    }


def build(debug=(), stop_at=None):
    nc = bass.Bass("TRN2", target_bir_lowering=False, detect_race_conditions=False)
    es = ExitStack()

    def din(name, shape):
        return nc.dram_tensor(name, list(shape), F32, kind="ExternalInput").ap()

    x_in = din("x", (S, DM))
    w_qkv = din("w_qkv", (3 * DM, DM))
    w_o = din("w_o", (DM, DM))
    w_gate = din("w_gate", (FF, DM))
    w_up = din("w_up", (FF, DM))
    w_down = din("w_down", (DM, FF))
    w_ip = din("w_ip", (2 * DI, DM))
    conv_w = din("conv_w", (DI, 4))
    conv_b = din("conv_b", (DI, 1))
    w_xp = din("w_xp", (2 * DSN + 1, DI))
    dt_w = din("dt_w", (DI, 1))
    dt_b = din("dt_b", (DI, 1))
    a_log = din("a_log", (DI, DSN))
    d_vec = din("d_vec", (DI, 1))
    w_op = din("w_op", (DM, DI))
    w_l1 = din("w_l1", (DM, DM))
    b_l1 = din("b_l1", (DM, 1))
    dw_l1 = din("dw_l1", (DM, 32))
    da_l1 = din("da_l1", (1, 1))
    c_cos = din("c_cos", (128, S))
    c_sin = din("c_sin", (128, S))
    c_mask = din("c_mask", (128, 4 * TB))
    c_ident = din("c_ident", (128, 128))

    out_d = nc.dram_tensor("out", [S, DM], F32, kind="ExternalOutput").ap()

    def dscr(name, shape):
        return nc.dram_tensor(name, list(shape), F32).ap()

    sXT = dscr("sXT", (DM, S))
    sQ = dscr("sQ", (NH, DK, S))
    sK = dscr("sK", (NH, DK, S))
    sVT = dscr("sVT", (NH, S // 128, 128, DK))
    sX1 = dscr("sX1", (DM, S))
    sXQ = dscr("sXQ", (DM, S))
    sWUT = dscr("sWUT", (DM // 128, 128, FF))
    sWGT = dscr("sWGT", (DM // 128, 128, FF))
    sWDT = dscr("sWDT", (FF // 128, 128, DM))
    sUP = dscr("sUP", (FF, S))
    sGA = dscr("sGA", (FF, S))
    sUPQ = dscr("sUPQ", (FF, S))
    sG1 = dscr("sG1", (FF, S))
    sPROD = dscr("sPROD", (FF, S))
    sX2 = dscr("sX2", (DM, S))
    sWIPT = dscr("sWIPT", (DM // 128, 128, 2 * DI))
    sXC = dscr("sXC", (DI, S))
    sSZ = dscr("sSZ", (DI, S))
    sG = dscr("sG", (DI, S))
    sX3 = dscr("sX3", (DM, S))
    sXP = dscr("sXP", (33, S))
    sATT = dscr("sATT", (DM, S))
    sSC = dscr("sSC", (128, 4 * TB))
    sEX = dscr("sEX", (128, 4 * TB))
    cc_in = nc.dram_tensor("cc_in", [1, 8], F32).ap()
    cc_outs = [nc.dram_tensor(f"cc_out{i}", [1, 8], F32, addr_space="Shared").ap()
               for i in range(6)]

    dbg_shapes = {"XT": (DM, S), "Q": (NH, DK, S), "K": (NH, DK, S),
                  "VT": (NH, S // 128, 128, DK), "X1": (DM, S), "XQ": (DM, S), "ATT": (DM, S), "SC": (128, 4 * TB), "EX": (128, 4 * TB),
                  "UP": (FF, S), "GA": (FF, S), "UPQ": (FF, S), "G1": (FF, S),
                  "PROD": (FF, S), "X2": (DM, S), "XC": (DI, S), "SZ": (DI, S),
                  "G": (DI, S), "X3": (DM, S), "XP": (33, S)}
    dbg_src = {"XT": sXT, "Q": sQ, "K": sK, "VT": sVT, "X1": sX1, "XQ": sXQ, "ATT": sATT, "SC": sSC, "EX": sEX,
               "UP": sUP, "GA": sGA, "UPQ": sUPQ, "G1": sG1, "PROD": sPROD,
               "X2": sX2, "XC": sXC, "SZ": sSZ, "G": sG, "X3": sX3, "XP": sXP}
    dbg = {}
    for name in debug:
        dbg[name] = nc.dram_tensor("dbg_" + name, list(dbg_shapes[name]), F32,
                                   kind="ExternalOutput").ap()

    ident = nc.alloc_sbuf_tensor("ident", [128, 128], F32)
    pvec = nc.alloc_sbuf_tensor("pvec", [128, 512], F32)
    ones128 = nc.alloc_sbuf_tensor("ones128", [128, 1], F32)
    ones1r = nc.alloc_sbuf_tensor("ones1r", [1, 128], F32)
    smax = nc.alloc_sbuf_tensor("smax", [128, 8], F32)
    srow = nc.alloc_sbuf_tensor("srow", [1, 64], F32)
    sscal = nc.alloc_sbuf_tensor("sscal", [128, 16], F32)
    maskt = nc.alloc_sbuf_tensor("maskt", [128, 4 * TB], F32)
    mrow = nc.alloc_sbuf_tensor("mrow", [1, TB], F32)
    vrow = nc.alloc_sbuf_tensor("vrow", [1, TB], F32)
    m2row = nc.alloc_sbuf_tensor("m2row", [1, TB], F32)
    wrow = nc.alloc_sbuf_tensor("wrow", [1, TB], F32)
    trow = nc.alloc_sbuf_tensor("trow", [1, TB], F32)
    tmp128 = nc.alloc_sbuf_tensor("tmp128", [128, 1], F32)
    constv = nc.alloc_sbuf_tensor("constv", [128, 4], F32)
    ones8 = nc.alloc_sbuf_tensor("ones8", [128, 8], F32)
    sscal8 = nc.alloc_sbuf_tensor("sscal8", [128, 16], F32)

    ARENA0 = 42 * 1024
    ARENA_SZ = 183 * 1024

    class Arena:
        def __init__(self):
            self.off = 0

        def al(self, name, shape, dtype=F32):
            nbytes = int(np.prod(shape[1:])) * mybir.dt.size(dtype)
            nbytes = (nbytes + 31) & ~31
            t = nc.alloc_sbuf_tensor_at(name, list(shape), dtype,
                                        offset=ARENA0 + self.off)
            self.off += nbytes
            assert self.off <= ARENA_SZ, f"arena overflow at {name}: {self.off}"
            return t

    pb = [nc.alloc_psum_tensor(f"pb{i}", [128, TB], F32) for i in range(8)]

    cx = Ctx(nc, es)

    def ckpt(name):
        if stop_at == name:
            cx.stopped = True

    def dma(out, in_):
        return lambda e: e.dma_start(out=out, in_=in_)

    MM = dict(skip_group_check=True)

    with nc.Block("main") as blk:
        cx.block = blk

        def p0_loads(sync):
            sync.dma_start(out=ident[:], in_=c_ident[:])
            sync.dma_start(out=maskt[:], in_=c_mask[:])
            for db in range(16):
                sync.dma_start(out=pvec[:, db * 16:(db + 1) * 16],
                               in_=a_log[db * 128:(db + 1) * 128, :])
                sync.dma_start(out=pvec[:, 256 + db:257 + db],
                               in_=d_vec[db * 128:(db + 1) * 128, :])
                sync.dma_start(out=pvec[:, 272 + db * 4:276 + db * 4],
                               in_=conv_w[db * 128:(db + 1) * 128, :])
                sync.dma_start(out=pvec[:, 336 + db:337 + db],
                               in_=dt_w[db * 128:(db + 1) * 128, :])
                sync.dma_start(out=pvec[:, 352 + db:353 + db],
                               in_=dt_b[db * 128:(db + 1) * 128, :])
                sync.dma_start(out=pvec[:, 368 + db:369 + db],
                               in_=conv_b[db * 128:(db + 1) * 128, :])
            for mb in range(8):
                sync.dma_start(out=pvec[:, 384 + mb:385 + mb],
                               in_=b_l1[mb * 128:(mb + 1) * 128, :])
            return sync.dma_start(out=srow[0:1, 8:9], in_=da_l1[:])

        cx.on("sync", p0_loads, dma=True)

        def p0_memset(v):
            v.memset(ones128[:], 1.0)
            v.memset(ones1r[:], 1.0)
            v.memset(smax[:], 0.0)
            v.memset(constv[:, 0:1], LN_EPS)
            v.memset(ones8[:], 1.0)
            return v.memset(constv[:, 1:2], -2.0 * LN2)

        cx.on("vector", p0_memset)
        cx.on("scalar", lambda s: s.activation(pvec[:, 0:256], pvec[:, 0:256],
                                               AF.Exp))
        cx.on("vector", lambda v: v.tensor_scalar_mul(pvec[:, 0:256],
                                                      pvec[:, 0:256], -1.0))
        cx.on("scalar", lambda s: s.activation(srow[0:1, 9:10], srow[0:1, 8:9],
                                               AF.Tanh, scale=4.0))
        cx.on("scalar", lambda s: s.activation(srow[0:1, 9:10], srow[0:1, 9:10],
                                               AF.Exp, scale=0.5 * LN2))
        cx.on("tensor", lambda t: t.matmul(pb[0][:, 0:1], ones1r[:],
                                           srow[0:1, 9:10], start=True, stop=True,
                                           **MM))
        cx.on("scalar", lambda s: s.copy(sscal[:, 10:11], pb[0][:, 0:1]))
        cx.on("gpsimd", lambda g: g.dma_start(out=cc_in[:], in_=smax[0:1, 0:8]),
              dma=True)
        cx.on("gpsimd", lambda g: g.collective_compute(
            "AllReduce", ALU.max, replica_groups=[list(range(NCORES))],
            ins=[cc_in], outs=[cc_outs[5]]), force_wait=True)

        def wt_prep_sbuf(src, n_rows, n_cols, wt, wt_stride, stage):
            nmt, nkt = n_rows // 128, n_cols // 128
            for m in range(nmt):
                cx.on("sync", dma(stage[:, 0:n_cols], src[m * 128:(m + 1) * 128, :]),
                      dma=True)
                for kt0 in range(0, nkt, 8):
                    kts = list(range(kt0, min(kt0 + 8, nkt)))

                    def tr(t, kts=kts):
                        for kt in kts:
                            last = t.transpose(pb[kt % 8][:, 0:128],
                                               stage[:, kt * 128:kt * 128 + 128],
                                               ident[:])
                        return last

                    def cp(s, kts=kts, m=m):
                        for kt in kts:
                            last = s.copy(wt[:, kt * wt_stride + m * 128:
                                             kt * wt_stride + (m + 1) * 128],
                                          pb[kt % 8][:, 0:128])
                        return last

                    cx.on("tensor", tr)
                    cx.on("scalar", cp)

        def emit_ln(rsb_t, store_dst, jj, max_slot=None):
            def rows(v):
                v.tensor_scalar_mul(mrow[:], pb[6][0:1, 0:TB], 1.0 / DM)
                v.tensor_scalar_mul(vrow[:], pb[7][0:1, 0:TB], 1.0 / DM)
                v.tensor_tensor(m2row[:], mrow[:], mrow[:], ALU.mult)
                v.tensor_tensor(vrow[:], vrow[:], m2row[:], ALU.subtract)
                return v.tensor_copy(m2row[:], vrow[:])

            cx.on("vector", rows)
            cx.on("scalar", lambda s: s.activation(vrow[:], vrow[:], AF.Sqrt,
                                                   bias=constv[0:1, 0:1]))

            def rsqrt_ref(v):
                v.tensor_scalar_add(wrow[:], m2row[:], LN_EPS)
                v.reciprocal(vrow[:], vrow[:])
                for _ in range(2):
                    v.tensor_tensor(trow[:], vrow[:], vrow[:], ALU.mult)
                    v.tensor_tensor(trow[:], trow[:], wrow[:], ALU.mult)
                    v.tensor_scalar(trow[:], trow[:], -0.5, 1.5, ALU.mult,
                                    ALU.add)
                    last = v.tensor_tensor(vrow[:], vrow[:], trow[:], ALU.mult)
                return last

            cx.on("vector", rsqrt_ref)

            def bc(t):
                t.matmul(pb[6][:, 0:TB], ones1r[:], mrow[:], start=True, stop=True,
                         **MM)
                return t.matmul(pb[7][:, 0:TB], ones1r[:], vrow[:], start=True,
                                stop=True, **MM)

            cx.on("tensor", bc)

            def apply(v):
                for m in range(8):
                    sl = rsb_t[:, m * TB:(m + 1) * TB]
                    v.tensor_tensor(sl, sl, pb[6][:, 0:TB], ALU.subtract)
                    last = v.tensor_tensor(sl, sl, pb[7][:, 0:TB], ALU.mult)
                    if max_slot is not None:
                        v.tensor_reduce(tmp128[:], sl, AX.X, ALU.max,
                                        apply_absolute_value=True)
                        last = v.tensor_tensor(smax[:, max_slot:max_slot + 1],
                                               smax[:, max_slot:max_slot + 1],
                                               tmp128[:], ALU.max)
                return last

            cx.on("vector", apply)
            if store_dst is not None:
                cx.on("sync", lambda e: [
                    e.dma_start(out=store_dst[m * 128:(m + 1) * 128,
                                              jj * TB:(jj + 1) * TB],
                                in_=rsb_t[:, m * TB:(m + 1) * TB])
                    for m in range(8)][-1], dma=True)

        def exchange(slot, cc_idx, out_col):
            cx.on("tensor", lambda t: t.transpose(
                pb[0][0:1, 0:128], smax[:, slot:slot + 1], ident[:]))
            cx.on("vector", lambda v: v.tensor_reduce(
                srow[0:1, 0:1], pb[0][0:1, 0:128], AX.X, ALU.max))
            cx.on("gpsimd", lambda g: g.dma_start(out=cc_in[0:1, 0:1],
                                                  in_=srow[0:1, 0:1]), dma=True)
            cx.on("gpsimd", lambda g: g.collective_compute(
                "AllReduce", ALU.max, replica_groups=[list(range(NCORES))],
                ins=[cc_in], outs=[cc_outs[cc_idx]]), force_wait=True)
            cx.on("gpsimd", lambda g: g.dma_start(out=srow[0:1, 0:1],
                                                  in_=cc_outs[cc_idx][0:1, 0:1]),
                  dma=True, force_wait=True)

            cx.on("tensor", lambda t: t.matmul(pb[0][:, 0:1], ones1r[:],
                                               srow[0:1, 0:1], start=True,
                                               stop=True, **MM))
            cx.on("scalar", lambda s: s.copy(sscal[:, 14:15], pb[0][:, 0:1]))

            def sc(v):
                v.tensor_scalar(sscal8[:, 0:8], ones8[:], sscal[:, 14:15], None,
                                ALU.mult)
                v.tensor_scalar(sscal8[:, 0:8], sscal8[:, 0:8], 1.0 / QMAX0,
                                1e-12, ALU.mult, ALU.max)
                v.reciprocal(sscal8[:, 8:16], sscal8[:, 0:8])
                v.tensor_copy(sscal[:, out_col:out_col + 1], sscal8[:, 0:1])
                return v.tensor_copy(sscal[:, out_col + 1:out_col + 2],
                                     sscal8[:, 8:9])

            cx.on("vector", sc)

        def mxfp4_emit(wb, ib, amb, scb, eb, dwf_ap, ab, g2b, sgb, ncols):
            nb = ncols // 32
            wv = wb.rearrange("p (b q) -> p b q", q=32)
            amv = amb.rearrange("p (b o) -> p b o", o=1)
            scv = scb.rearrange("p (b o) -> p b o", o=1)

            def step1(v):
                v.tensor_reduce(amv, wv, AX.X, ALU.max, apply_absolute_value=True)
                return v.tensor_single_scalar(amb, amb, 1e-30, ALU.max)

            cx.on("vector", step1)
            cx.on("scalar", lambda s: s.activation(scb, amb, AF.Ln))

            def step2(v):
                v.tensor_scalar(eb, scb, 1.0 / LN2, -0.5, ALU.mult, ALU.add)
                return v.tensor_copy(scb, eb)

            cx.on("vector", step2)
            cx.on("scalar", lambda s: s.activation(scb, scb, AF.Exp, scale=LN2,
                                                   bias=constv[:, 1:2]))
            if dwf_ap is not None:
                cx.on("vector", lambda v: v.tensor_tensor(scb, scb, dwf_ap,
                                                          ALU.mult))
            else:
                cx.on("vector", lambda v: v.tensor_scalar(scb, scb,
                                                          sscal[:, 10:11], None,
                                                          ALU.mult))

            def step3a(v):
                v.reciprocal(amb, scb)
                return v.tensor_tensor(wv, wv, amv.broadcast_to([128, nb, 32]),
                                       ALU.mult)

            cx.on("vector", step3a)
            cx.on("scalar", lambda s: s.activation(ab, wb, AF.Abs))

            def step3(v):
                v.tensor_single_scalar(ab, ab, 6.0, ALU.min)
                v.tensor_single_scalar(g2b, ab, 2.0, ALU.is_ge)
                v.tensor_single_scalar(sgb, ab, 4.0, ALU.is_ge)
                v.scalar_tensor_tensor(g2b, sgb, 0.5, g2b, ALU.mult, ALU.add)
                v.tensor_scalar(g2b, g2b, -1.0, 2.0, ALU.mult, ALU.add)
                v.tensor_tensor(ib, ab, g2b, ALU.mult)
                v.reciprocal(g2b, g2b)
                v.tensor_tensor(ab, ib, g2b, ALU.mult)
                return v.tensor_single_scalar(sgb, wb, 0.0, ALU.is_ge)

            cx.on("vector", step3)

            def step4(v):
                v.tensor_scalar(sgb, sgb, 2.0, -1.0, ALU.mult, ALU.add)
                av = ab.rearrange("p (b q) -> p b q", q=32)
                v.tensor_tensor(ab, ab, sgb, ALU.mult)
                return v.tensor_tensor(wv, av, scv.broadcast_to([128, nb, 32]),
                                       ALU.mult)

            cx.on("vector", step4)

        ar = Arena()
        WQKVT = ar.al("wqkvt", [128, 8 * 3 * DM])
        pa_stage = ar.al("pa_stage", [128, DM])
        pa_xtc = ar.al("pa_xtc", [128, 8 * TB])
        pa_cos = ar.al("pa_cos", [128, S])
        pa_sin = ar.al("pa_sin", [128, S])
        pa_ev = ar.al("pa_ev", [128, TB])
        pa_t1 = ar.al("pa_t1", [128, TB])
        pa_t2 = ar.al("pa_t2", [128, TB])
        pa_vt = ar.al("pa_vt", [128, 512])

        cx.on("sync", lambda e: (e.dma_start(out=pa_cos[:], in_=c_cos[:]),
                                 e.dma_start(out=pa_sin[:], in_=c_sin[:]))[-1],
              dma=True)
        wt_prep_sbuf(w_qkv, 3 * DM, DM, WQKVT, 3 * DM, pa_stage)

        pair_mask = [(i // 2) * 2 + (1 - i % 2) for i in range(32)]
        for c in range(NT):
            for i in range(4):
                cx.on("sync", dma(pa_stage[:, 0:DM],
                                  x_in[c * TB + i * 128:c * TB + (i + 1) * 128, :]),
                      dma=True)

                def trx(t):
                    for kt in range(8):
                        last = t.transpose(pb[kt][:, 0:128],
                                           pa_stage[:, kt * 128:kt * 128 + 128],
                                           ident[:])
                    return last

                def cpx(s, i=i):
                    for kt in range(8):
                        last = s.copy(
                            pa_xtc[:, kt * TB + i * 128:kt * TB + (i + 1) * 128],
                            pb[kt][:, 0:128])
                    return last

                cx.on("tensor", trx)
                cx.on("scalar", cpx)
            cx.on("sync", lambda e, c=c: [
                e.dma_start(out=sXT[pt * 128:(pt + 1) * 128, c * TB:(c + 1) * TB],
                            in_=pa_xtc[:, pt * TB:(pt + 1) * TB])
                for pt in range(8)][-1], dma=True)
            for m in range(24):
                def mm(t, m=m):
                    for k in range(8):
                        last = t.matmul(
                            pb[m % 2][:, 0:TB],
                            WQKVT[:, k * 3 * DM + m * 128:k * 3 * DM + (m + 1) * 128],
                            pa_xtc[:, k * TB:(k + 1) * TB],
                            start=(k == 0), stop=(k == 7), **MM)
                    return last

                cx.on("tensor", mm)
                cx.on("scalar", lambda s, m=m: s.copy(pa_ev[:], pb[m % 2][:, 0:TB]))
                if m < 16:
                    def rope(v, c=c):
                        v.tensor_tensor(pa_t1[:], pa_ev[:],
                                        pa_cos[:, c * TB:(c + 1) * TB], ALU.mult)
                        v.stream_shuffle(pa_t2[:], pa_ev[:], pair_mask)
                        v.tensor_tensor(pa_t2[:], pa_t2[:],
                                        pa_sin[:, c * TB:(c + 1) * TB], ALU.mult)
                        return v.tensor_add(pa_ev[:], pa_t1[:], pa_t2[:])

                    cx.on("vector", rope)
                    qk = sQ if m < 8 else sK
                    hm = (m % 8) * 2
                    cx.on("sync", lambda e, qk=qk, hm=hm, c=c: (
                        e.dma_start(out=qk[hm, :, c * TB:(c + 1) * TB],
                                    in_=pa_ev[0:64, :]),
                        e.dma_start(out=qk[hm + 1, :, c * TB:(c + 1) * TB],
                                    in_=pa_ev[64:128, :]))[-1], dma=True)
                else:
                    h2 = (m - 16) * 2

                    def trv(t):
                        for i in range(4):
                            last = t.transpose(pb[4 + i][:, 0:128],
                                               pa_ev[:, i * 128:(i + 1) * 128],
                                               ident[:])
                        return last

                    def cpv(s):
                        for i in range(4):
                            last = s.copy(pa_vt[:, i * 128:(i + 1) * 128],
                                          pb[4 + i][:, 0:128])
                        return last

                    cx.on("tensor", trv)
                    cx.on("scalar", cpv)

                    def stv(e, h2=h2, c=c):
                        for i in range(4):
                            kcg = c * 4 + i
                            e.dma_start(out=sVT[h2, kcg],
                                        in_=pa_vt[:, i * 128:i * 128 + 64])
                            last = e.dma_start(out=sVT[h2 + 1, kcg],
                                               in_=pa_vt[:, i * 128 + 64:
                                                         (i + 1) * 128])
                        return last

                    cx.on("sync", stv, dma=True)

        ckpt("PA")

        ar = Arena()
        WOT = ar.al("wot", [128, 8 * DM])
        pb_stage = ar.al("pb_stage", [128, DM])
        kbuf = ar.al("kbuf", [64, S])
        qbuf = ar.al("qbuf", [64, TB])
        vtb = ar.al("vtb", [128, 32 * 65])
        expT = ar.al("expT", [128, 4 * TB])
        attn_sb = ar.al("attn_sb", [128, 8 * TB])
        rsb = ar.al("rsb", [128, 8 * TB])
        r2t8 = ar.al("r2t8", [128, 8 * TB])
        xres8 = ar.al("xres8", [128, 8 * TB])
        rbc64 = ar.al("rbc64", [64, TB])

        wt_prep_sbuf(w_o, DM, DM, WOT, DM, pb_stage)
        cx.on("vector", lambda v: v.memset(vtb[:], 1.0))

        for j in range(NT):
            nk = 4 * (j + 1)
            for h in range(NH):
                def hload(e, h=h, j=j, nk=nk):
                    e.dma_start(out=qbuf[:], in_=sQ[h, :, j * TB:(j + 1) * TB])
                    e.dma_start(out=kbuf[:, 0:nk * 128], in_=sK[h, :, 0:nk * 128])
                    for kc0 in range(0, nk, 8):
                        kn = min(8, nk - kc0)
                        last = e.dma_start(
                            out=vtb.ap().rearrange("p (kc c) -> p kc c", c=65)
                            [:, kc0:kc0 + kn, 0:64],
                            in_=sVT[h, kc0:kc0 + kn].rearrange("kc p c -> p kc c"))
                    return last

                cx.on("sync", hload, dma=True)
                for kc0 in range(0, nk, 4):
                    def mm_s(t, kc0=kc0):
                        for i in range(4):
                            last = t.matmul(
                                pb[i][:, 0:TB],
                                kbuf[:, (kc0 + i) * 128:(kc0 + i + 1) * 128],
                                qbuf[:], start=True, stop=True, **MM)
                        return last

                    cx.on("tensor", mm_s)

                    def ex(s):
                        for i in range(4):
                            last = s.activation(expT[:, i * TB:(i + 1) * TB],
                                                pb[i][:, 0:TB], AF.Exp, scale=0.125)
                        return last

                    cx.on("scalar", ex)
                    if kc0 == 4 * j:
                        cx.on("vector", lambda v: v.tensor_tensor(
                            expT[:], expT[:], maskt[:], ALU.mult))

                    def mm_av(t, kc0=kc0, nk=nk):
                        for i in range(4):
                            last = t.matmul(
                                pb[4][0:65, 0:TB],
                                vtb[:, (kc0 + i) * 65:(kc0 + i + 1) * 65],
                                expT[:, i * TB:(i + 1) * TB],
                                start=(kc0 + i == 0), stop=(kc0 + i == nk - 1),
                                **MM)
                        return last

                    cx.on("tensor", mm_av)
                cx.on("vector", lambda v: v.reciprocal(mrow[:], pb[4][64:65, 0:TB]))
                cx.on("tensor", lambda t: t.matmul(pb[5][0:64, 0:TB],
                                                   ones1r[0:1, 0:64], mrow[:],
                                                   start=True, stop=True, **MM))
                cx.on("scalar", lambda s: s.copy(rbc64[:], pb[5][0:64, 0:TB]))
                pt, sub = h // 2, (h % 2) * 64
                cx.on("vector", lambda v, pt=pt, sub=sub: v.tensor_tensor(
                    attn_sb[sub:sub + 64, pt * TB:(pt + 1) * TB],
                    pb[4][0:64, 0:TB], rbc64[:], ALU.mult))
            if "ATT" in debug:
                cx.on("sync", lambda e, j=j: [
                    e.dma_start(out=sATT[pt * 128:(pt + 1) * 128,
                                         j * TB:(j + 1) * TB],
                                in_=attn_sb[:, pt * TB:(pt + 1) * TB])
                    for pt in range(8)], dma=True)
            cx.on("sync", lambda e, j=j: [
                e.dma_start(out=xres8[:, m * TB:(m + 1) * TB],
                            in_=sXT[m * 128:(m + 1) * 128, j * TB:(j + 1) * TB])
                for m in range(8)][-1], dma=True)

            def mm_o(t):
                for m in range(8):
                    for k in range(8):
                        last = t.matmul(
                            pb[m][:, 0:TB],
                            WOT[:, k * DM + m * 128:k * DM + (m + 1) * 128],
                            attn_sb[:, k * TB:(k + 1) * TB],
                            start=(k == 0), stop=(k == 7), **MM)
                return last

            cx.on("tensor", mm_o)

            def radd(v):
                for m in range(8):
                    last = v.tensor_add(rsb[:, m * TB:(m + 1) * TB],
                                        pb[m][:, 0:TB],
                                        xres8[:, m * TB:(m + 1) * TB])
                return last

            cx.on("vector", radd)

            def rsq(s):
                for m in range(8):
                    last = s.activation(r2t8[:, m * TB:(m + 1) * TB],
                                        rsb[:, m * TB:(m + 1) * TB], AF.Square)
                return last

            cx.on("scalar", rsq)

            def mm_st(t):
                for m in range(8):
                    t.matmul(pb[6][0:1, 0:TB], ones128[:, 0:1],
                             rsb[:, m * TB:(m + 1) * TB], start=(m == 0),
                             stop=(m == 7), **MM)
                    last = t.matmul(pb[7][0:1, 0:TB], ones128[:, 0:1],
                                    r2t8[:, m * TB:(m + 1) * TB], start=(m == 0),
                                    stop=(m == 7), **MM)
                return last

            cx.on("tensor", mm_st)
            emit_ln(rsb, sX1, j, max_slot=0)

        ckpt("PB")

        exchange(0, 0, 0)
        ar = Arena()
        c0_x = ar.al("c0_x", [128, S])
        c0_i = ar.al("c0_i", [128, S], I32)
        for pt in range(8):
            cx.on("sync", dma(c0_x[:], sX1[pt * 128:(pt + 1) * 128, :]), dma=True)

            def qdq(v):
                v.tensor_scalar(c0_i[:], c0_x[:], sscal[:, 1:2], None, ALU.mult)
                return v.tensor_scalar(c0_x[:], c0_i[:], sscal[:, 0:1], None,
                                       ALU.mult)

            cx.on("vector", qdq)
            cx.on("sync", dma(sXQ[pt * 128:(pt + 1) * 128, :], c0_x[:]), dma=True)

        ckpt("PC0")

        ar = Arena()
        p1_w = ar.al("p1_w", [128, DM])
        p1_i = ar.al("p1_i", [128, DM], I32)
        p1_am = ar.al("p1_am", [128, 32])
        p1_rs = ar.al("p1_rs", [128, 32])
        p1_wt = ar.al("p1_wt", [128, 8 * 128])
        xq_sb = ar.al("xq_sb", [128, 8 * S])
        p1_st = ar.al("p1_st", [128, 8 * 128])
        p1_ev = ar.al("p1_ev", [128, S])

        def lpbq_quant(v, wt_, it_, amt, rst, nb):
            wv = wt_.rearrange("p (b q) -> p b q", q=32)
            iv = it_.rearrange("p (b q) -> p b q", q=32)
            amv = amt.rearrange("p (b o) -> p b o", o=1)
            rsv = rst.rearrange("p (b o) -> p b o", o=1)
            v.tensor_reduce(amv, wv, AX.X, ALU.max, apply_absolute_value=True)
            v.tensor_scalar(amt, amt, 1.0 / 7.0, 1e-12, ALU.mult, ALU.max)
            v.reciprocal(rst, amt)
            v.tensor_tensor(iv, wv, rsv.broadcast_to([128, nb, 32]), ALU.mult)
            return v.tensor_tensor(wv, iv, amv.broadcast_to([128, nb, 32]),
                                   ALU.mult)

        def lpbq_prep(src, dstT, n_rows, n_cols, wbuf, ibuf, ambuf, rsbuf, wtbuf):
            nmt, nkt, nb = n_rows // 128, n_cols // 128, n_cols // 32
            for m in range(nmt):
                cx.on("sync", dma(wbuf[:, 0:n_cols], src[m * 128:(m + 1) * 128, :]),
                      dma=True)
                cx.on("vector", lambda v, nb=nb, ncc=n_cols: lpbq_quant(
                    v, wbuf[:, 0:ncc], ibuf[:, 0:ncc], ambuf[:, 0:nb],
                    rsbuf[:, 0:nb], nb))
                for kt0 in range(0, nkt, 8):
                    kts = list(range(kt0, min(kt0 + 8, nkt)))

                    def tr(t, kts=kts):
                        for kt in kts:
                            last = t.transpose(pb[kt % 8][:, 0:128],
                                               wbuf[:, kt * 128:kt * 128 + 128],
                                               ident[:])
                        return last

                    def cp(s, kts=kts):
                        for kt in kts:
                            last = s.copy(wtbuf[:, kt * 128:(kt + 1) * 128],
                                          pb[kt % 8][:, 0:128])
                        return last

                    cx.on("tensor", tr)
                    cx.on("scalar", cp)
                cx.on("sync", lambda e, m=m, nkt=nkt, dstT=dstT, wtbuf=wtbuf: [
                    e.dma_start(out=dstT[kt, :, m * 128:(m + 1) * 128],
                                in_=wtbuf[:, kt * 128:(kt + 1) * 128])
                    for kt in range(nkt)][-1], dma=True)

        lpbq_prep(w_up, sWUT, FF, DM, p1_w, p1_i, p1_am, p1_rs, p1_wt)
        lpbq_prep(w_gate, sWGT, FF, DM, p1_w, p1_i, p1_am, p1_rs, p1_wt)

        cx.on("sync", lambda e: [
            e.dma_start(out=xq_sb[:, pt * S:(pt + 1) * S],
                        in_=sXQ[pt * 128:(pt + 1) * 128, :]) for pt in range(8)][-1],
            dma=True)

        def big_mm(dstD, srcWT, max_slot):
            for m in range(FF // 128):
                cx.on("sync", lambda e, m=m, srcWT=srcWT: [
                    e.dma_start(out=p1_st[:, k * 128:(k + 1) * 128],
                                in_=srcWT[k, :, m * 128:(m + 1) * 128])
                    for k in range(8)][-1], dma=True)

                def mm(t):
                    for k in range(8):
                        for tt in range(8):
                            last = t.matmul(pb[tt][:, 0:TB],
                                            p1_st[:, k * 128:(k + 1) * 128],
                                            xq_sb[:, k * S + tt * TB:
                                                  k * S + (tt + 1) * TB],
                                            start=(k == 0), stop=(k == 7), **MM)
                    return last

                cx.on("tensor", mm)

                def evac(s):
                    for tt in range(8):
                        last = s.copy(p1_ev[:, tt * TB:(tt + 1) * TB],
                                      pb[tt][:, 0:TB])
                    return last

                cx.on("scalar", evac)

                def mx(v):
                    v.tensor_reduce(tmp128[:], p1_ev[:], AX.X, ALU.max,
                                    apply_absolute_value=True)
                    return v.tensor_tensor(smax[:, max_slot:max_slot + 1],
                                           smax[:, max_slot:max_slot + 1],
                                           tmp128[:], ALU.max)

                cx.on("vector", mx)
                cx.on("sync", lambda e, m=m, dstD=dstD: e.dma_start(
                    out=dstD[m * 128:(m + 1) * 128, :], in_=p1_ev[:]), dma=True)

        big_mm(sUP, sWUT, 1)
        big_mm(sGA, sWGT, 2)

        ckpt("PC1")

        exchange(1, 1, 2)
        exchange(2, 2, 4)
        ar = Arena()
        c2_u = ar.al("c2_u", [128, S])
        c2_g = ar.al("c2_g", [128, S])
        c2_i = ar.al("c2_i", [128, S], I32)
        c2_s = ar.al("c2_s", [128, S])
        for mt in range(FF // 128):
            r0, r1 = mt * 128, (mt + 1) * 128
            cx.on("sync", lambda e, r0=r0, r1=r1: (
                e.dma_start(out=c2_u[:], in_=sUP[r0:r1, :]),
                e.dma_start(out=c2_g[:], in_=sGA[r0:r1, :]))[-1], dma=True)

            def q2(v):
                v.tensor_scalar(c2_i[:], c2_u[:], sscal[:, 3:4], None, ALU.mult)
                v.tensor_scalar(c2_u[:], c2_i[:], sscal[:, 2:3], None, ALU.mult)
                v.tensor_scalar(c2_i[:], c2_g[:], sscal[:, 5:6], None, ALU.mult)
                return v.tensor_scalar(c2_g[:], c2_i[:], sscal[:, 4:5], None,
                                       ALU.mult)

            cx.on("vector", q2)
            cx.on("sync", lambda e, r0=r0, r1=r1: e.dma_start(
                out=sUPQ[r0:r1, :], in_=c2_u[:]), dma=True)
            cx.on("scalar", lambda s: s.activation(c2_s[:], c2_g[:], AF.Sigmoid))

            def q3(v):
                v.tensor_scalar(c2_i[:], c2_s[:], 65536.0, None, ALU.mult)
                v.tensor_scalar(c2_s[:], c2_i[:], 65535.0, 1.0 / 65536.0,
                                ALU.min, ALU.mult)
                v.tensor_tensor(c2_g[:], c2_g[:], c2_s[:], ALU.mult)
                v.tensor_reduce(tmp128[:], c2_g[:], AX.X, ALU.max,
                                apply_absolute_value=True)
                return v.tensor_tensor(smax[:, 3:4], smax[:, 3:4], tmp128[:],
                                       ALU.max)

            cx.on("vector", q3)
            cx.on("sync", lambda e, r0=r0, r1=r1: e.dma_start(
                out=sG1[r0:r1, :], in_=c2_g[:]), dma=True)

        ckpt("PC2")

        exchange(3, 3, 6)
        for mt in range(FF // 128):
            r0, r1 = mt * 128, (mt + 1) * 128
            cx.on("sync", lambda e, r0=r0, r1=r1: (
                e.dma_start(out=c2_g[:], in_=sG1[r0:r1, :]),
                e.dma_start(out=c2_u[:], in_=sUPQ[r0:r1, :]))[-1], dma=True)

            def q4(v):
                v.tensor_scalar(c2_i[:], c2_g[:], sscal[:, 7:8], None, ALU.mult)
                v.tensor_scalar(c2_g[:], c2_i[:], sscal[:, 6:7], None, ALU.mult)
                v.tensor_tensor(c2_g[:], c2_g[:], c2_u[:], ALU.mult)
                v.tensor_reduce(tmp128[:], c2_g[:], AX.X, ALU.max,
                                apply_absolute_value=True)
                return v.tensor_tensor(smax[:, 4:5], smax[:, 4:5], tmp128[:],
                                       ALU.max)

            cx.on("vector", q4)
            cx.on("sync", lambda e, r0=r0, r1=r1: e.dma_start(
                out=sPROD[r0:r1, :], in_=c2_g[:]), dma=True)

        ckpt("PC3")

        exchange(4, 4, 8)
        ar = Arena()
        c4_w = ar.al("c4_w", [128, FF])
        c4_i = ar.al("c4_i", [128, FF], I32)
        c4_am = ar.al("c4_am", [128, 96])
        c4_rs = ar.al("c4_rs", [128, 96])
        c4_wt = ar.al("c4_wt", [128, 24 * 128])
        c4_st = ar.al("c4_st", [128, 8 * DM])
        c4_pq = ar.al("c4_pq", [128, 8 * TB])
        c4_oi = ar.al("c4_oi", [128, TB], I32)
        c4_rsb = ar.al("c4_rsb", [128, 8 * TB])
        c4_r2 = ar.al("c4_r2", [128, 8 * TB])
        c4_xr = ar.al("c4_xr", [128, 8 * TB])

        lpbq_prep(w_down, sWDT, DM, FF, c4_w, c4_i, c4_am, c4_rs, c4_wt)

        for tt in range(NT):
            for kg in range(3):
                def ld(e, tt=tt, kg=kg):
                    for kk in range(8):
                        k = kg * 8 + kk
                        e.dma_start(out=c4_st[:, kk * DM:(kk + 1) * DM],
                                    in_=sWDT[k, :, :])
                        last = e.dma_start(
                            out=c4_pq[:, kk * TB:(kk + 1) * TB],
                            in_=sPROD[k * 128:(k + 1) * 128,
                                      tt * TB:(tt + 1) * TB])
                    return last

                cx.on("sync", ld, dma=True)

                def oq(v):
                    for kk in range(8):
                        sl = c4_pq[:, kk * TB:(kk + 1) * TB]
                        v.tensor_scalar(c4_oi[:], sl, sscal[:, 9:10], None,
                                        ALU.mult)
                        last = v.tensor_scalar(sl, c4_oi[:], sscal[:, 8:9], None,
                                               ALU.mult)
                    return last

                cx.on("vector", oq)

                def mm(t, kg=kg):
                    for kk in range(8):
                        k = kg * 8 + kk
                        for m in range(8):
                            last = t.matmul(
                                pb[m][:, 0:TB],
                                c4_st[:, kk * DM + m * 128:kk * DM + (m + 1) * 128],
                                c4_pq[:, kk * TB:(kk + 1) * TB],
                                start=(k == 0), stop=(k == 23), **MM)
                    return last

                cx.on("tensor", mm)
            cx.on("sync", lambda e, tt=tt: [
                e.dma_start(out=c4_xr[:, m * TB:(m + 1) * TB],
                            in_=sX1[m * 128:(m + 1) * 128, tt * TB:(tt + 1) * TB])
                for m in range(8)][-1], dma=True)

            def radd(v):
                for m in range(8):
                    last = v.tensor_add(c4_rsb[:, m * TB:(m + 1) * TB],
                                        pb[m][:, 0:TB],
                                        c4_xr[:, m * TB:(m + 1) * TB])
                return last

            cx.on("vector", radd)

            def rsq(s):
                for m in range(8):
                    last = s.activation(c4_r2[:, m * TB:(m + 1) * TB],
                                        c4_rsb[:, m * TB:(m + 1) * TB], AF.Square)
                return last

            cx.on("scalar", rsq)

            def mm_st(t):
                for m in range(8):
                    t.matmul(pb[6][0:1, 0:TB], ones128[:, 0:1],
                             c4_rsb[:, m * TB:(m + 1) * TB], start=(m == 0),
                             stop=(m == 7), **MM)
                    last = t.matmul(pb[7][0:1, 0:TB], ones128[:, 0:1],
                                    c4_r2[:, m * TB:(m + 1) * TB], start=(m == 0),
                                    stop=(m == 7), **MM)
                return last

            cx.on("tensor", mm_st)
            emit_ln(c4_rsb, sX2, tt)

        ckpt("PC4")

        ar = Arena()
        d_wst = ar.al("d_wst", [128, DM])
        d_wt = ar.al("d_wt", [128, 8 * 128])
        x2_sb = ar.al("x2_sb", [128, 8 * S])
        d_st = ar.al("d_st", [128, 8 * 128])
        d_cb = ar.al("d_cb", [128, S + 3])
        d_acc = ar.al("d_acc", [128, S])
        xpt = ar.al("xpt", [128, 16 * 33])
        xps = ar.al("xps", [33, TB])

        for m in range(2 * DI // 128):
            cx.on("sync", dma(d_wst[:, 0:DM], w_ip[m * 128:(m + 1) * 128, :]),
                  dma=True)
            for kt0 in range(0, 8, 4):
                kts = list(range(kt0, kt0 + 4))

                def tr(t, kts=kts):
                    for kt in kts:
                        last = t.transpose(pb[4 + kt % 4][:, 0:128],
                                           d_wst[:, kt * 128:kt * 128 + 128],
                                           ident[:])
                    return last

                def cp(s, kts=kts):
                    for kt in kts:
                        last = s.copy(d_wt[:, kt * 128:(kt + 1) * 128],
                                      pb[4 + kt % 4][:, 0:128])
                    return last

                cx.on("tensor", tr)
                cx.on("scalar", cp)
            cx.on("sync", lambda e, m=m: [
                e.dma_start(out=sWIPT[kt, :, m * 128:(m + 1) * 128],
                            in_=d_wt[:, kt * 128:(kt + 1) * 128])
                for kt in range(8)][-1], dma=True)

        cx.on("sync", lambda e: [
            e.dma_start(out=x2_sb[:, pt * S:(pt + 1) * S],
                        in_=sX2[pt * 128:(pt + 1) * 128, :]) for pt in range(8)][-1],
            dma=True)

        for m in range(2 * DI // 128):
            cx.on("sync", lambda e, m=m: [
                e.dma_start(out=d_st[:, k * 128:(k + 1) * 128],
                            in_=sWIPT[k, :, m * 128:(m + 1) * 128])
                for k in range(8)][-1], dma=True)

            def mm(t):
                for k in range(8):
                    for tt in range(8):
                        last = t.matmul(pb[tt][:, 0:TB],
                                        d_st[:, k * 128:(k + 1) * 128],
                                        x2_sb[:, k * S + tt * TB:k * S + (tt + 1) * TB],
                                        start=(k == 0), stop=(k == 7), **MM)
                return last

            cx.on("tensor", mm)
            if m < 16:
                db = m

                def evac(s):
                    for tt in range(8):
                        last = s.copy(d_cb[:, 3 + tt * TB:3 + (tt + 1) * TB],
                                      pb[tt][:, 0:TB])
                    return last

                cx.on("scalar", evac)

                def conv(v, db=db):
                    v.memset(d_cb[:, 0:3], 0.0)
                    v.tensor_scalar(d_acc[:], d_cb[:, 0:S],
                                    pvec[:, 272 + db * 4:273 + db * 4], None,
                                    ALU.mult)
                    for kk in range(1, 4):
                        last = v.scalar_tensor_tensor(
                            d_acc[:], d_cb[:, kk:kk + S],
                            pvec[:, 272 + db * 4 + kk:273 + db * 4 + kk],
                            d_acc[:], ALU.mult, ALU.add)
                    return last

                cx.on("vector", conv)
                cx.on("scalar", lambda s, db=db: s.activation(
                    d_cb[:, 0:S], d_acc[:], AF.Silu,
                    bias=pvec[:, 368 + db:369 + db]))
                cx.on("sync", lambda e, db=db: e.dma_start(
                    out=sXC[db * 128:(db + 1) * 128, :], in_=d_cb[:, 0:S]),
                    dma=True)
            else:
                db = m - 16

                def zsilu(s):
                    for tt in range(8):
                        last = s.activation(d_acc[:, tt * TB:(tt + 1) * TB],
                                            pb[tt][:, 0:TB], AF.Silu)
                    return last

                cx.on("scalar", zsilu)
                cx.on("sync", lambda e, db=db: e.dma_start(
                    out=sSZ[db * 128:(db + 1) * 128, :], in_=d_acc[:, 0:S]),
                    dma=True)

        for kt in range(16):
            cx.on("sync", dma(d_wst[0:33, 0:128], w_xp[:, kt * 128:(kt + 1) * 128]),
                  dma=True)
            cx.on("tensor", lambda t: t.transpose(pb[4][0:128, 0:33],
                                                  d_wst[0:33, 0:128],
                                                  ident[0:33, 0:33]))
            cx.on("scalar", lambda s, kt=kt: s.copy(xpt[:, kt * 33:(kt + 1) * 33],
                                                    pb[4][0:128, 0:33]))
        for tt in range(8):
            cx.on("sync", lambda e, tt=tt: [
                e.dma_start(out=d_acc[:, k * 256:(k + 1) * 256],
                            in_=sXC[k * 128:(k + 1) * 128,
                                    tt * TB:tt * TB + 256])
                for k in range(16)][-1], dma=True)
            cx.on("sync", lambda e, tt=tt: [
                e.dma_start(out=d_cb[:, k * 256:(k + 1) * 256],
                            in_=sXC[k * 128:(k + 1) * 128,
                                    tt * TB + 256:(tt + 1) * TB])
                for k in range(16)][-1], dma=True)

            def mmx(t, tt=tt):
                for k in range(16):
                    t.matmul(pb[6][0:33, 0:256], xpt[:, k * 33:(k + 1) * 33],
                             d_acc[:, k * 256:(k + 1) * 256],
                             start=(k == 0), stop=(k == 15), **MM)
                    last = t.matmul(pb[7][0:33, 0:256],
                                    xpt[:, k * 33:(k + 1) * 33],
                                    d_cb[:, k * 256:(k + 1) * 256],
                                    start=(k == 0), stop=(k == 15), **MM)
                return last

            cx.on("tensor", mmx)

            def xpev(s):
                s.copy(xps[:, 0:256], pb[6][0:33, 0:256])
                return s.copy(xps[:, 256:512], pb[7][0:33, 0:256])

            cx.on("scalar", xpev)
            cx.on("sync", lambda e, tt=tt: e.dma_start(
                out=sXP[:, tt * TB:(tt + 1) * TB], in_=xps[:]), dma=True)

        ckpt("PD")

        ar = Arena()
        e_bbc = ar.al("e_bbc", [128, 16 * TB])
        e_cbc = ar.al("e_cbc", [128, 16 * TB])
        e_da = ar.al("e_da", [128, 16 * TB])
        e_m = ar.al("e_m", [128, 16 * TB])
        e_dtb = ar.al("e_dtb", [128, TB])
        e_row = ar.al("e_row", [1, 4 * TB])
        e_dt = ar.al("e_dt", [128, TB])
        e_sp1 = ar.al("e_sp1", [128, TB])
        e_u = ar.al("e_u", [128, TB])
        e_xc = ar.al("e_xc", [128, TB])
        e_sz = ar.al("e_sz", [128, TB])
        e_y = ar.al("e_y", [128, TB])
        e_st = ar.al("e_st", [128, 16 * 16])

        cx.on("vector", lambda v: v.memset(e_st[:], 0.0))
        for c in range(NT):
            cx.on("sync", lambda e, c=c: e.dma_start(
                out=e_row[0:1, 0:TB], in_=sXP[0:1, c * TB:(c + 1) * TB]), dma=True)
            cx.on("tensor", lambda t: t.matmul(pb[4][:, 0:TB], ones1r[:],
                                               e_row[0:1, 0:TB], start=True,
                                               stop=True, **MM))
            cx.on("scalar", lambda s: s.copy(e_dtb[:], pb[4][:, 0:TB]))
            for grp, dst in ((1, e_bbc), (17, e_cbc)):
                for n0 in range(0, 16, 4):
                    cx.on("sync", lambda e, c=c, grp=grp, n0=n0: e.dma_start(
                        out=e_row.ap().rearrange("p (n t) -> p n t", n=4),
                        in_=sXP[grp + n0:grp + n0 + 4, c * TB:(c + 1) * TB]),
                        dma=True)

                    def bmm(t):
                        for i in range(4):
                            last = t.matmul(pb[i][:, 0:TB], ones1r[:],
                                            e_row[0:1, i * TB:(i + 1) * TB],
                                            start=True, stop=True, **MM)
                        return last

                    def bcp(s, n0=n0, dst=dst):
                        for i in range(4):
                            last = s.copy(dst[:, (n0 + i) * TB:(n0 + i + 1) * TB],
                                          pb[i][:, 0:TB])
                        return last

                    cx.on("tensor", bmm)
                    cx.on("scalar", bcp)
            for db in range(16):
                cx.on("sync", lambda e, db=db, c=c: (
                    e.dma_start(out=e_xc[:],
                                in_=sXC[db * 128:(db + 1) * 128,
                                        c * TB:(c + 1) * TB]),
                    e.dma_start(out=e_sz[:],
                                in_=sSZ[db * 128:(db + 1) * 128,
                                        c * TB:(c + 1) * TB]))[-1], dma=True)

                def sp(s, db=db):
                    s.activation(e_sp1[:], e_dtb[:], AF.Abs,
                                 scale=pvec[:, 336 + db:337 + db],
                                 bias=pvec[:, 352 + db:353 + db])
                    s.activation(e_sp1[:], e_sp1[:], AF.Exp, scale=-1.0)
                    s.activation(e_sp1[:], e_sp1[:], AF.Ln, bias=1.0)
                    return s.activation(e_dt[:], e_dtb[:], AF.Relu,
                                        scale=pvec[:, 336 + db:337 + db],
                                        bias=pvec[:, 352 + db:353 + db])

                cx.on("scalar", sp)

                def mkdt(v):
                    v.tensor_add(e_dt[:], e_dt[:], e_sp1[:])
                    return v.tensor_tensor(e_u[:], e_dt[:], e_xc[:], ALU.mult)

                cx.on("vector", mkdt)

                def dA(s, db=db):
                    for n in range(16):
                        last = s.activation(e_da[:, n * TB:(n + 1) * TB], e_dt[:],
                                            AF.Exp,
                                            scale=pvec[:, db * 16 + n:
                                                       db * 16 + n + 1])
                    return last

                cx.on("scalar", dA)
                cx.on("vector", lambda v: v.tensor_tensor(
                    e_m.ap().rearrange("p (n t) -> p n t", n=16),
                    e_u.ap().rearrange("p (o t) -> p o t", o=1)
                    .broadcast_to([128, 16, TB]),
                    e_bbc.ap().rearrange("p (n t) -> p n t", n=16), ALU.mult))

                def scan(v, db=db):
                    for n in range(16):
                        sl = slice(n * TB, (n + 1) * TB)
                        last = v.tensor_tensor_scan(
                            e_m[:, sl], e_da[:, sl], e_m[:, sl],
                            e_st[:, db * 16 + n:db * 16 + n + 1],
                            ALU.mult, ALU.add)
                    return last

                cx.on("vector", scan)
                cx.on("vector", lambda v, db=db: v.tensor_copy(
                    e_st[:, db * 16:(db + 1) * 16],
                    e_m.ap().rearrange("p (n t) -> p n t", n=16)[:, :, TB - 1:TB]))
                cx.on("vector", lambda v: v.tensor_tensor(e_m[:], e_m[:], e_cbc[:],
                                                          ALU.mult))
                cx.on("vector", lambda v: v.tensor_reduce(
                    e_y.ap().rearrange("p (t o) -> p t o", o=1),
                    e_m.ap().rearrange("p (n t) -> p t n", n=16), AX.X, ALU.add))

                def gate(v, db=db):
                    v.scalar_tensor_tensor(e_y[:], e_xc[:],
                                           pvec[:, 256 + db:257 + db], e_y[:],
                                           ALU.mult, ALU.add)
                    return v.tensor_tensor(e_y[:], e_y[:], e_sz[:], ALU.mult)

                cx.on("vector", gate)
                cx.on("sync", lambda e, db=db, c=c: e.dma_start(
                    out=sG[db * 128:(db + 1) * 128, c * TB:(c + 1) * TB],
                    in_=e_y[:]), dma=True)

        ckpt("PE")

        ar = Arena()
        WOPT = ar.al("wopt", [128, 16 * DM])
        f_wst = ar.al("f_wst", [128, DI])
        f_g16 = ar.al("f_g16", [128, 16 * TB])
        f_rsb = ar.al("f_rsb", [128, 8 * TB])
        f_r2 = ar.al("f_r2", [128, 8 * TB])
        f_xr = ar.al("f_xr", [128, 8 * TB])

        for m in range(8):
            cx.on("sync", dma(f_wst[:, 0:DI], w_op[m * 128:(m + 1) * 128, :]),
                  dma=True)
            for kt0 in range(0, 16, 4):
                kts = list(range(kt0, kt0 + 4))

                def tr(t, kts=kts):
                    for kt in kts:
                        last = t.transpose(pb[4 + kt % 4][:, 0:128],
                                           f_wst[:, kt * 128:kt * 128 + 128],
                                           ident[:])
                    return last

                def cp(s, kts=kts, m=m):
                    for kt in kts:
                        last = s.copy(WOPT[:, kt * DM + m * 128:kt * DM + (m + 1) * 128],
                                      pb[4 + kt % 4][:, 0:128])
                    return last

                cx.on("tensor", tr)
                cx.on("scalar", cp)

        for tt in range(NT):
            def gld(e, tt=tt):
                for k in range(16):
                    last = e.dma_start(
                        out=f_g16[:, k * TB:(k + 1) * TB],
                        in_=sG[k * 128:(k + 1) * 128, tt * TB:(tt + 1) * TB])
                for m in range(8):
                    last = e.dma_start(
                        out=f_xr[:, m * TB:(m + 1) * TB],
                        in_=sX2[m * 128:(m + 1) * 128, tt * TB:(tt + 1) * TB])
                return last

            cx.on("sync", gld, dma=True)

            def mm(t):
                for k in range(16):
                    for m in range(8):
                        last = t.matmul(pb[m][:, 0:TB],
                                        WOPT[:, k * DM + m * 128:k * DM + (m + 1) * 128],
                                        f_g16[:, k * TB:(k + 1) * TB],
                                        start=(k == 0), stop=(k == 15), **MM)
                return last

            cx.on("tensor", mm)

            def radd(v):
                for m in range(8):
                    last = v.tensor_add(f_rsb[:, m * TB:(m + 1) * TB],
                                        pb[m][:, 0:TB],
                                        f_xr[:, m * TB:(m + 1) * TB])
                return last

            cx.on("vector", radd)

            def rsq(s):
                for m in range(8):
                    last = s.activation(f_r2[:, m * TB:(m + 1) * TB],
                                        f_rsb[:, m * TB:(m + 1) * TB], AF.Square)
                return last

            cx.on("scalar", rsq)

            def mm_st(t):
                for m in range(8):
                    t.matmul(pb[6][0:1, 0:TB], ones128[:, 0:1],
                             f_rsb[:, m * TB:(m + 1) * TB], start=(m == 0),
                             stop=(m == 7), **MM)
                    last = t.matmul(pb[7][0:1, 0:TB], ones128[:, 0:1],
                                    f_r2[:, m * TB:(m + 1) * TB], start=(m == 0),
                                    stop=(m == 7), **MM)
                return last

            cx.on("tensor", mm_st)
            emit_ln(f_rsb, sX3, tt)

        ckpt("PF")

        ar = Arena()
        WLT = ar.al("wlt", [128, 8 * DM])
        g_w = ar.al("g_w", [128, DM])
        g_i = ar.al("g_i", [128, S], I32)
        g_am = ar.al("g_am", [128, 128])
        g_sc = ar.al("g_sc", [128, 128])
        g_e = ar.al("g_e", [128, 128], I32)
        g_dw = ar.al("g_dw", [128, 32])
        g_a = ar.al("g_a", [128, S])
        g_g2 = ar.al("g_g2", [128, S])
        g_sg = ar.al("g_sg", [128, S])
        g_x3 = ar.al("g_x3", [128, 8 * TB])
        g_xt = ar.al("g_xt", [128, 4 * DM])
        g_rsb = ar.al("g_rsb", [128, 8 * TB])
        g_r2 = ar.al("g_r2", [128, 8 * TB])

        for m in range(8):
            cx.on("sync", dma(g_dw[:, 0:32], dw_l1[m * 128:(m + 1) * 128, :]),
                  dma=True)
            cx.on("scalar", lambda s: s.activation(g_dw[:, 0:32], g_dw[:, 0:32],
                                                   AF.Tanh, scale=4.0))
            cx.on("scalar", lambda s: s.activation(g_dw[:, 0:32], g_dw[:, 0:32],
                                                   AF.Exp, scale=0.5 * LN2))
            cx.on("sync", dma(g_w[:, 0:DM], w_l1[m * 128:(m + 1) * 128, :]),
                  dma=True)
            mxfp4_emit(g_w[:, 0:DM], g_i[:, 0:DM], g_am[:, 0:32], g_sc[:, 0:32],
                       g_e[:, 0:32], g_dw[:, 0:32], g_a[:, 0:DM], g_g2[:, 0:DM],
                       g_sg[:, 0:DM], DM)

            def trw(t):
                for kt in range(8):
                    last = t.transpose(pb[kt][:, 0:128],
                                       g_w[:, kt * 128:kt * 128 + 128], ident[:])
                return last

            def cpw(s, m=m):
                for kt in range(8):
                    last = s.copy(WLT[:, kt * DM + m * 128:kt * DM + (m + 1) * 128],
                                  pb[kt][:, 0:128])
                return last

            cx.on("tensor", trw)
            cx.on("scalar", cpw)

        def tr8(src_t, src_base, dst_t, dst_base, src_stride, dst_stride):
            def tr(t):
                for kt in range(8):
                    last = t.transpose(
                        pb[kt][:, 0:128],
                        src_t[:, src_base + kt * src_stride:
                              src_base + kt * src_stride + 128], ident[:])
                return last

            def cp(s):
                for kt in range(8):
                    last = s.copy(dst_t[:, dst_base + kt * dst_stride:
                                        dst_base + kt * dst_stride + 128],
                                  pb[kt][:, 0:128])
                return last

            cx.on("tensor", tr)
            cx.on("scalar", cp)

        for c in range(NT):
            cx.on("sync", lambda e, c=c: [
                e.dma_start(out=g_x3[:, pt * TB:(pt + 1) * TB],
                            in_=sX3[pt * 128:(pt + 1) * 128, c * TB:(c + 1) * TB])
                for pt in range(8)][-1], dma=True)
            for i in range(4):
                tr8(g_x3, i * 128, g_xt, i * DM, TB, 128)
            mxfp4_emit(g_xt[:, 0:4 * DM], g_i[:, 0:4 * DM], g_am[:, 0:128],
                       g_sc[:, 0:128], g_e[:, 0:128], None, g_a[:, 0:4 * DM],
                       g_g2[:, 0:4 * DM], g_sg[:, 0:4 * DM], 4 * DM)
            for i in range(4):
                tr8(g_xt, i * DM, g_x3, i * 128, 128, TB)

            def mm(t):
                for k in range(8):
                    for m in range(8):
                        last = t.matmul(pb[m][:, 0:TB],
                                        WLT[:, k * DM + m * 128:k * DM + (m + 1) * 128],
                                        g_x3[:, k * TB:(k + 1) * TB],
                                        start=(k == 0), stop=(k == 7), **MM)
                return last

            cx.on("tensor", mm)
            cx.on("sync", lambda e, c=c: [
                e.dma_start(out=g_r2[:, m * TB:(m + 1) * TB],
                            in_=sX3[m * 128:(m + 1) * 128, c * TB:(c + 1) * TB])
                for m in range(8)][-1], dma=True)

            def radd(v):
                for m in range(8):
                    last = v.scalar_tensor_tensor(
                        g_rsb[:, m * TB:(m + 1) * TB], pb[m][:, 0:TB],
                        pvec[:, 384 + m:385 + m], g_r2[:, m * TB:(m + 1) * TB],
                        ALU.add, ALU.add)
                return last

            cx.on("vector", radd)

            def rsq(s):
                for m in range(8):
                    last = s.activation(g_r2[:, m * TB:(m + 1) * TB],
                                        g_rsb[:, m * TB:(m + 1) * TB], AF.Square)
                return last

            cx.on("scalar", rsq)

            def mm_st(t):
                for m in range(8):
                    t.matmul(pb[6][0:1, 0:TB], ones128[:, 0:1],
                             g_rsb[:, m * TB:(m + 1) * TB], start=(m == 0),
                             stop=(m == 7), **MM)
                    last = t.matmul(pb[7][0:1, 0:TB], ones128[:, 0:1],
                                    g_r2[:, m * TB:(m + 1) * TB], start=(m == 0),
                                    stop=(m == 7), **MM)
                return last

            cx.on("tensor", mm_st)
            emit_ln(g_rsb, None, c)
            for i in range(4):
                tr8(g_rsb, i * 128, g_xt, i * DM, TB, 128)
            cx.on("sync", lambda e, c=c: [
                e.dma_start(out=out_d[c * TB + i * 128:c * TB + (i + 1) * 128, :],
                            in_=g_xt[:, i * DM:(i + 1) * DM]) for i in range(4)][-1],
                dma=True)

        ckpt("PG")

        cx.stopped = False
        for name in debug:
            cx.on("sync", dma(dbg[name], dbg_src[name]), dma=True)

        cx.final_wait("sync")

    auto_end = 16512 + (212863 - nc.sbuf_bytes_remaining)
    assert auto_end <= ARENA0, f"auto sbuf end {auto_end} exceeds ARENA0 {ARENA0}"
    return nc, dbg


_BUILT = {}


def _get_built(debug=(), stop_at=None):
    key = (tuple(debug), stop_at)
    if key not in _BUILT:
        _BUILT[key] = build(debug, stop_at)
    return _BUILT[key]


def kernel(**inputs):
    nc, _ = _get_built(DEBUG, STOP_AT)
    consts = _host_constants()
    x = np.ascontiguousarray(np.asarray(inputs["L0_attn_x"], np.float32))
    base = {
        "w_qkv": np.ascontiguousarray(inputs["L0_attn_W_qkv"], np.float32),
        "w_o": np.ascontiguousarray(inputs["L0_attn_W_o"], np.float32),
        "w_gate": np.ascontiguousarray(inputs["L0_ffn_w_gate"], np.float32),
        "w_up": np.ascontiguousarray(inputs["L0_ffn_w_up"], np.float32),
        "w_down": np.ascontiguousarray(inputs["L0_ffn_w_down"], np.float32),
        "w_ip": np.ascontiguousarray(inputs["L1_attn_in_proj_w"], np.float32),
        "conv_w": np.ascontiguousarray(
            np.asarray(inputs["L1_attn_conv_w"], np.float32).reshape(DI, 4)),
        "conv_b": np.ascontiguousarray(
            np.asarray(inputs["L1_attn_conv_b"], np.float32).reshape(DI, 1)),
        "w_xp": np.ascontiguousarray(inputs["L1_attn_x_proj_w"], np.float32),
        "dt_w": np.ascontiguousarray(
            np.asarray(inputs["L1_attn_dt_proj_w"], np.float32).reshape(DI, 1)),
        "dt_b": np.ascontiguousarray(
            np.asarray(inputs["L1_attn_dt_proj_b"], np.float32).reshape(DI, 1)),
        "a_log": np.ascontiguousarray(inputs["L1_attn_A_log"], np.float32),
        "d_vec": np.ascontiguousarray(
            np.asarray(inputs["L1_attn_D"], np.float32).reshape(DI, 1)),
        "w_op": np.ascontiguousarray(inputs["L1_attn_out_proj_w"], np.float32),
        "w_l1": np.ascontiguousarray(inputs["L1_ffn_weight"], np.float32),
        "b_l1": np.ascontiguousarray(
            np.asarray(inputs["L1_ffn_bias"], np.float32).reshape(DM, 1)),
        "dw_l1": np.ascontiguousarray(inputs["L1_ffn_delta_w"], np.float32),
        "da_l1": np.ascontiguousarray(
            np.asarray(inputs["L1_ffn_delta_act"], np.float32).reshape(1, 1)),
    }
    base.update(consts)
    in_maps = []
    for b in range(NCORES):
        m = dict(base)
        m["x"] = np.ascontiguousarray(x[b])
        in_maps.append(m)
    res = run_bass_kernel_spmd(nc, in_maps, core_ids=list(range(NCORES)))
    out = np.stack([np.asarray(res.results[b]["out"]) for b in range(NCORES)],
                   axis=0)
    if DEBUG:
        kernel._dbg = [{k: np.asarray(res.results[b]["dbg_" + k]) for k in DEBUG}
                       for b in range(NCORES)]
    return out


# version 04
# speedup vs baseline: 1.0618x; 1.0618x over previous
import numpy as np
import concourse.bass as bass
import concourse.mybir as mybir
from concourse.bass_utils import run_bass_kernel_spmd
from contextlib import ExitStack

F32 = mybir.dt.float32
I32 = mybir.dt.int32
AF = mybir.ActivationFunctionType
ALU = mybir.AluOpType
AX = mybir.AxisListType

S = 4096
DM = 1024
FF = 3072
DI = 2048
DSN = 16
NH = 16
DK = 64
TB = 512
NT = S // TB
LN_EPS = 1e-5
QMAX0 = 32767.0
LN2 = float(np.log(2.0))
NCORES = 8

DEBUG = ()
STOP_AT = None


class Ctx:

    def __init__(self, nc, es):
        self.nc = nc
        self.ch = es.enter_context(nc.semaphore("chain"))
        self.d2 = es.enter_context(nc.semaphore("dma2"))
        self.t = 0
        self.dn = 0
        self.block = None
        self.stopped = False
        self._eng = None
        self._dma = False
        self._fns = []
        self._pre = None

    class _SmallGuard:

        SMALL = 512
        _PASS = ("drain", "wait_ge", "nop", "engine_nop")

        def __init__(self, eng):
            self._e = eng

        def __getattr__(self, name):
            f = getattr(self._e, name)
            if name in Ctx._SmallGuard._PASS or not callable(f):
                return f

            def wrap(*a, **kw):
                r = f(*a, **kw)
                out = a[0] if a else kw.get("out")
                fs = None
                try:
                    fs = out.free_size()
                except Exception:
                    pass
                if fs is not None and fs < Ctx._SmallGuard.SMALL:
                    self._e.drain()
                return r

            return wrap

    class _DmaRec:

        def __init__(self, eng):
            self._eng = eng
            self.insts = []

        def dma_start(self, **kw):
            i = self._eng.dma_start(**kw)
            self.insts.append(i)
            return i

        def wait_ge(self, *a):
            return self._eng.wait_ge(*a)

    def _flush(self):
        if not self._fns:
            return
        dec = getattr(self.block, self._eng)
        t0 = self.t
        fns = self._fns
        self._fns = []
        is_dma = self._dma
        eng_name = self._eng
        n = [0]

        pre = self._pre
        self._pre = None

        def body(eng):
            if pre is not None and pre > 0:
                eng.wait_ge(self.d2, pre)
            if t0 > 0:
                eng.wait_ge(self.ch, t0)
            if is_dma:
                rec = Ctx._DmaRec(eng)
                for fn in fns:
                    fn(rec)
                assert rec.insts
                for i in rec.insts:
                    i.then_inc(self.ch, 16)
                n[0] = 16 * len(rec.insts)
            else:
                geng = eng if eng_name in ("tensor", "sync") \
                    else Ctx._SmallGuard(eng)
                for fn in fns:
                    last = fn(geng)
                assert last is not None, "step fn must return its last instruction"
                last.then_inc(self.ch, 1)
                n[0] = 1

        dec(body)
        assert n[0] > 0
        self.t = t0 + n[0]

    def on(self, eng_name, fn, dma=False, force_wait=False, pre_dma=None):
        if self.stopped:
            return
        if pre_dma is None and self._fns and self._eng == eng_name \
                and self._dma == dma and not force_wait:
            self._fns.append(fn)
            return
        self._flush()
        self._eng = eng_name
        self._dma = dma
        self._fns = [fn]
        self._pre = pre_dma

    def dma_async(self, fn, after=None, after_d2=None):
        if self.stopped:
            return self.dn
        self._flush()
        dec = self.block.sync
        n = [0]
        d2 = self.d2

        def body(eng):
            if after_d2 is not None and after_d2 > 0:
                eng.wait_ge(self.d2, after_d2)
            if after is not None and after > 0:
                eng.wait_ge(self.ch, after)
            rec = Ctx._DmaRec(eng)
            fn(rec)
            assert rec.insts
            for i in rec.insts:
                i.then_inc(d2, 16)
            n[0] = 16 * len(rec.insts)

        dec(body)
        assert n[0] > 0
        self.dn += n[0]
        self._eng = None
        return self.dn

    def mark(self):
        self._flush()
        return self.t

    def final_wait(self, eng_name):
        self._flush()
        dec = getattr(self.block, eng_name)
        t0 = self.t

        def body(eng):
            eng.wait_ge(self.ch, t0)

        dec(body)


def _host_constants():
    dk = DK
    inv_freq = (np.float32(10000.0) ** (np.arange(0, dk, 2, dtype=np.float32)
                                        / np.float32(dk))).astype(np.float32)
    ang = (np.arange(S, dtype=np.float32)[:, None] / inv_freq[None, :]).astype(np.float32)
    cos = np.cos(ang).astype(np.float32)
    sin = np.sin(ang).astype(np.float32)
    cos64 = np.repeat(cos.T, 2, axis=0)
    sin64 = np.repeat(sin.T, 2, axis=0)
    cos128 = np.concatenate([cos64, cos64], axis=0)
    sgn = np.where(np.arange(64) % 2 == 0, -1.0, 1.0).astype(np.float32)[:, None]
    sin64s = (sin64 * sgn).astype(np.float32)
    sin128 = np.concatenate([sin64s, sin64s], axis=0)
    masks = []
    for r in range(4):
        kk = np.arange(128)[:, None] + 128 * r
        qq = np.arange(TB)[None, :]
        masks.append((kk <= qq).astype(np.float32))
    return {
        "c_cos": np.ascontiguousarray(cos128),
        "c_sin": np.ascontiguousarray(sin128),
        "c_mask": np.ascontiguousarray(np.concatenate(masks, axis=1)),
        "c_ident": np.eye(128, dtype=np.float32),
    }


def build(debug=(), stop_at=None):
    nc = bass.Bass("TRN2", target_bir_lowering=False, detect_race_conditions=False)
    es = ExitStack()

    def din(name, shape):
        return nc.dram_tensor(name, list(shape), F32, kind="ExternalInput").ap()

    x_in = din("x", (S, DM))
    w_qkv = din("w_qkv", (3 * DM, DM))
    w_o = din("w_o", (DM, DM))
    w_gate = din("w_gate", (FF, DM))
    w_up = din("w_up", (FF, DM))
    w_down = din("w_down", (DM, FF))
    w_ip = din("w_ip", (2 * DI, DM))
    conv_w = din("conv_w", (DI, 4))
    conv_b = din("conv_b", (DI, 1))
    w_xp = din("w_xp", (2 * DSN + 1, DI))
    dt_w = din("dt_w", (DI, 1))
    dt_b = din("dt_b", (DI, 1))
    a_log = din("a_log", (DI, DSN))
    d_vec = din("d_vec", (DI, 1))
    w_op = din("w_op", (DM, DI))
    w_l1 = din("w_l1", (DM, DM))
    b_l1 = din("b_l1", (DM, 1))
    dw_l1 = din("dw_l1", (DM, 32))
    da_l1 = din("da_l1", (1, 1))
    c_cos = din("c_cos", (128, S))
    c_sin = din("c_sin", (128, S))
    c_mask = din("c_mask", (128, 4 * TB))
    c_ident = din("c_ident", (128, 128))

    out_d = nc.dram_tensor("out", [S, DM], F32, kind="ExternalOutput").ap()

    def dscr(name, shape):
        return nc.dram_tensor(name, list(shape), F32).ap()

    sXT = dscr("sXT", (DM, S))
    sQ = dscr("sQ", (NH, DK, S))
    sK = dscr("sK", (NH, DK, S))
    sVT = dscr("sVT", (NH, S // 128, 128, DK))
    sX1 = dscr("sX1", (DM, S))
    sXQ = dscr("sXQ", (DM, S))
    sWUT = dscr("sWUT", (DM // 128, 128, FF))
    sWGT = dscr("sWGT", (DM // 128, 128, FF))
    sWDT = dscr("sWDT", (FF // 128, 128, DM))
    sUP = dscr("sUP", (FF, S))
    sGA = dscr("sGA", (FF, S))
    sUPQ = dscr("sUPQ", (FF, S))
    sG1 = dscr("sG1", (FF, S))
    sPROD = dscr("sPROD", (FF, S))
    sX2 = dscr("sX2", (DM, S))
    sWIPT = dscr("sWIPT", (DM // 128, 128, 2 * DI))
    sXC = dscr("sXC", (DI, S))
    sSZ = dscr("sSZ", (DI, S))
    sG = dscr("sG", (DI, S))
    sX3 = dscr("sX3", (DM, S))
    sXP = dscr("sXP", (33, S))
    sATT = dscr("sATT", (DM, S))
    sSC = dscr("sSC", (128, 4 * TB))
    sEX = dscr("sEX", (128, 4 * TB))
    cc_in = nc.dram_tensor("cc_in", [1, 8], F32).ap()
    cc_outs = [nc.dram_tensor(f"cc_out{i}", [1, 8], F32, addr_space="Shared").ap()
               for i in range(6)]

    dbg_shapes = {"XT": (DM, S), "Q": (NH, DK, S), "K": (NH, DK, S),
                  "VT": (NH, S // 128, 128, DK), "X1": (DM, S), "XQ": (DM, S), "ATT": (DM, S), "SC": (128, 4 * TB), "EX": (128, 4 * TB),
                  "UP": (FF, S), "GA": (FF, S), "UPQ": (FF, S), "G1": (FF, S),
                  "PROD": (FF, S), "X2": (DM, S), "XC": (DI, S), "SZ": (DI, S),
                  "G": (DI, S), "X3": (DM, S), "XP": (33, S)}
    dbg_src = {"XT": sXT, "Q": sQ, "K": sK, "VT": sVT, "X1": sX1, "XQ": sXQ, "ATT": sATT, "SC": sSC, "EX": sEX,
               "UP": sUP, "GA": sGA, "UPQ": sUPQ, "G1": sG1, "PROD": sPROD,
               "X2": sX2, "XC": sXC, "SZ": sSZ, "G": sG, "X3": sX3, "XP": sXP}
    dbg = {}
    for name in debug:
        dbg[name] = nc.dram_tensor("dbg_" + name, list(dbg_shapes[name]), F32,
                                   kind="ExternalOutput").ap()

    ident = nc.alloc_sbuf_tensor("ident", [128, 128], F32)
    pvec = nc.alloc_sbuf_tensor("pvec", [128, 512], F32)
    ones128 = nc.alloc_sbuf_tensor("ones128", [128, 1], F32)
    ones1r = nc.alloc_sbuf_tensor("ones1r", [1, 128], F32)
    smax = nc.alloc_sbuf_tensor("smax", [128, 8], F32)
    srow = nc.alloc_sbuf_tensor("srow", [1, 64], F32)
    sscal = nc.alloc_sbuf_tensor("sscal", [128, 16], F32)
    maskt = nc.alloc_sbuf_tensor("maskt", [128, 4 * TB], F32)
    mrow = nc.alloc_sbuf_tensor("mrow", [1, TB], F32)
    vrow = nc.alloc_sbuf_tensor("vrow", [1, TB], F32)
    m2row = nc.alloc_sbuf_tensor("m2row", [1, TB], F32)
    wrow = nc.alloc_sbuf_tensor("wrow", [1, TB], F32)
    trow = nc.alloc_sbuf_tensor("trow", [1, TB], F32)
    tmp128 = nc.alloc_sbuf_tensor("tmp128", [128, 1], F32)
    constv = nc.alloc_sbuf_tensor("constv", [128, 4], F32)
    ones8 = nc.alloc_sbuf_tensor("ones8", [128, 8], F32)
    sscal8 = nc.alloc_sbuf_tensor("sscal8", [128, 16], F32)

    ARENA0 = 42 * 1024
    ARENA_SZ = 183 * 1024

    class Arena:
        def __init__(self):
            self.off = 0

        def al(self, name, shape, dtype=F32):
            nbytes = int(np.prod(shape[1:])) * mybir.dt.size(dtype)
            nbytes = (nbytes + 31) & ~31
            t = nc.alloc_sbuf_tensor_at(name, list(shape), dtype,
                                        offset=ARENA0 + self.off)
            self.off += nbytes
            assert self.off <= ARENA_SZ, f"arena overflow at {name}: {self.off}"
            return t

    pb = [nc.alloc_psum_tensor(f"pb{i}", [128, TB], F32) for i in range(8)]

    cx = Ctx(nc, es)

    def ckpt(name):
        if stop_at == name:
            cx.stopped = True

    def dma(out, in_):
        return lambda e: e.dma_start(out=out, in_=in_)

    MM = dict(skip_group_check=True)

    with nc.Block("main") as blk:
        cx.block = blk

        def p0_loads(sync):
            sync.dma_start(out=ident[:], in_=c_ident[:])
            sync.dma_start(out=maskt[:], in_=c_mask[:])
            for db in range(16):
                sync.dma_start(out=pvec[:, db * 16:(db + 1) * 16],
                               in_=a_log[db * 128:(db + 1) * 128, :])
                sync.dma_start(out=pvec[:, 256 + db:257 + db],
                               in_=d_vec[db * 128:(db + 1) * 128, :])
                sync.dma_start(out=pvec[:, 272 + db * 4:276 + db * 4],
                               in_=conv_w[db * 128:(db + 1) * 128, :])
                sync.dma_start(out=pvec[:, 336 + db:337 + db],
                               in_=dt_w[db * 128:(db + 1) * 128, :])
                sync.dma_start(out=pvec[:, 352 + db:353 + db],
                               in_=dt_b[db * 128:(db + 1) * 128, :])
                sync.dma_start(out=pvec[:, 368 + db:369 + db],
                               in_=conv_b[db * 128:(db + 1) * 128, :])
            for mb in range(8):
                sync.dma_start(out=pvec[:, 384 + mb:385 + mb],
                               in_=b_l1[mb * 128:(mb + 1) * 128, :])
            return sync.dma_start(out=srow[0:1, 8:9], in_=da_l1[:])

        cx.on("sync", p0_loads, dma=True)

        def p0_memset(v):
            v.memset(ones128[:], 1.0)
            v.memset(ones1r[:], 1.0)
            v.memset(smax[:], 0.0)
            v.memset(constv[:, 0:1], LN_EPS)
            v.memset(ones8[:], 1.0)
            return v.memset(constv[:, 1:2], -2.0 * LN2)

        cx.on("vector", p0_memset)
        cx.on("scalar", lambda s: s.activation(pvec[:, 0:256], pvec[:, 0:256],
                                               AF.Exp))
        cx.on("vector", lambda v: v.tensor_scalar_mul(pvec[:, 0:256],
                                                      pvec[:, 0:256], -1.0))
        cx.on("scalar", lambda s: s.activation(srow[0:1, 9:10], srow[0:1, 8:9],
                                               AF.Tanh, scale=4.0))
        cx.on("scalar", lambda s: s.activation(srow[0:1, 9:10], srow[0:1, 9:10],
                                               AF.Exp, scale=0.5 * LN2))
        cx.on("tensor", lambda t: t.matmul(pb[0][:, 0:1], ones1r[:],
                                           srow[0:1, 9:10], start=True, stop=True,
                                           **MM))
        cx.on("scalar", lambda s: s.copy(sscal[:, 10:11], pb[0][:, 0:1]))
        cx.on("gpsimd", lambda g: g.dma_start(out=cc_in[:], in_=smax[0:1, 0:8]),
              dma=True)
        cx.on("gpsimd", lambda g: g.collective_compute(
            "AllReduce", ALU.max, replica_groups=[list(range(NCORES))],
            ins=[cc_in], outs=[cc_outs[5]]), force_wait=True)

        def wt_prep_sbuf(src, n_rows, n_cols, wt, wt_stride, stage):
            nmt, nkt = n_rows // 128, n_cols // 128
            for m in range(nmt):
                cx.on("sync", dma(stage[:, 0:n_cols], src[m * 128:(m + 1) * 128, :]),
                      dma=True)
                for kt0 in range(0, nkt, 8):
                    kts = list(range(kt0, min(kt0 + 8, nkt)))

                    def tr(t, kts=kts):
                        for kt in kts:
                            last = t.transpose(pb[kt % 8][:, 0:128],
                                               stage[:, kt * 128:kt * 128 + 128],
                                               ident[:])
                        return last

                    def cp(s, kts=kts, m=m):
                        for kt in kts:
                            last = s.copy(wt[:, kt * wt_stride + m * 128:
                                             kt * wt_stride + (m + 1) * 128],
                                          pb[kt % 8][:, 0:128])
                        return last

                    cx.on("tensor", tr)
                    cx.on("scalar", cp)

        def emit_ln(rsb_t, store_dst, jj, max_slot=None):
            def rows(v):
                v.tensor_scalar_mul(mrow[:], pb[6][0:1, 0:TB], 1.0 / DM)
                v.tensor_scalar_mul(vrow[:], pb[7][0:1, 0:TB], 1.0 / DM)
                v.tensor_tensor(m2row[:], mrow[:], mrow[:], ALU.mult)
                v.tensor_tensor(vrow[:], vrow[:], m2row[:], ALU.subtract)
                return v.tensor_copy(m2row[:], vrow[:])

            cx.on("vector", rows)
            cx.on("scalar", lambda s: s.activation(vrow[:], vrow[:], AF.Sqrt,
                                                   bias=constv[0:1, 0:1]))

            def rsqrt_ref(v):
                v.tensor_scalar_add(wrow[:], m2row[:], LN_EPS)
                v.reciprocal(vrow[:], vrow[:])
                for _ in range(2):
                    v.tensor_tensor(trow[:], vrow[:], vrow[:], ALU.mult)
                    v.tensor_tensor(trow[:], trow[:], wrow[:], ALU.mult)
                    v.tensor_scalar(trow[:], trow[:], -0.5, 1.5, ALU.mult,
                                    ALU.add)
                    last = v.tensor_tensor(vrow[:], vrow[:], trow[:], ALU.mult)
                return last

            cx.on("vector", rsqrt_ref)

            def bc(t):
                t.matmul(pb[6][:, 0:TB], ones1r[:], mrow[:], start=True, stop=True,
                         **MM)
                return t.matmul(pb[7][:, 0:TB], ones1r[:], vrow[:], start=True,
                                stop=True, **MM)

            cx.on("tensor", bc)

            def apply(v):
                for m in range(8):
                    sl = rsb_t[:, m * TB:(m + 1) * TB]
                    v.tensor_tensor(sl, sl, pb[6][:, 0:TB], ALU.subtract)
                    last = v.tensor_tensor(sl, sl, pb[7][:, 0:TB], ALU.mult)
                    if max_slot is not None:
                        v.tensor_reduce(tmp128[:], sl, AX.X, ALU.max,
                                        apply_absolute_value=True)
                        last = v.tensor_tensor(smax[:, max_slot:max_slot + 1],
                                               smax[:, max_slot:max_slot + 1],
                                               tmp128[:], ALU.max)
                return last

            cx.on("vector", apply)
            if store_dst is not None:
                cx.on("sync", lambda e: [
                    e.dma_start(out=store_dst[m * 128:(m + 1) * 128,
                                              jj * TB:(jj + 1) * TB],
                                in_=rsb_t[:, m * TB:(m + 1) * TB])
                    for m in range(8)][-1], dma=True)

        def exchange(slot, cc_idx, out_col):
            cx.on("tensor", lambda t: t.transpose(
                pb[0][0:1, 0:128], smax[:, slot:slot + 1], ident[:]))
            cx.on("vector", lambda v: v.tensor_reduce(
                srow[0:1, 0:1], pb[0][0:1, 0:128], AX.X, ALU.max))
            cx.on("gpsimd", lambda g: g.dma_start(out=cc_in[0:1, 0:1],
                                                  in_=srow[0:1, 0:1]), dma=True)
            cx.on("gpsimd", lambda g: g.collective_compute(
                "AllReduce", ALU.max, replica_groups=[list(range(NCORES))],
                ins=[cc_in], outs=[cc_outs[cc_idx]]), force_wait=True)
            cx.on("gpsimd", lambda g: g.dma_start(out=srow[0:1, 0:1],
                                                  in_=cc_outs[cc_idx][0:1, 0:1]),
                  dma=True, force_wait=True)

            cx.on("tensor", lambda t: t.matmul(pb[0][:, 0:1], ones1r[:],
                                               srow[0:1, 0:1], start=True,
                                               stop=True, **MM))
            cx.on("scalar", lambda s: s.copy(sscal[:, 14:15], pb[0][:, 0:1]))

            def sc(v):
                v.tensor_scalar(sscal8[:, 0:8], ones8[:], sscal[:, 14:15], None,
                                ALU.mult)
                v.tensor_scalar(sscal8[:, 0:8], sscal8[:, 0:8], 1.0 / QMAX0,
                                1e-12, ALU.mult, ALU.max)
                v.reciprocal(sscal8[:, 8:16], sscal8[:, 0:8])
                v.tensor_copy(sscal[:, out_col:out_col + 1], sscal8[:, 0:1])
                return v.tensor_copy(sscal[:, out_col + 1:out_col + 2],
                                     sscal8[:, 8:9])

            cx.on("vector", sc)

        def mxfp4_emit(wb, ib, amb, scb, eb, dwf_ap, ab, g2b, sgb, ncols):
            nb = ncols // 32
            wv = wb.rearrange("p (b q) -> p b q", q=32)
            amv = amb.rearrange("p (b o) -> p b o", o=1)
            scv = scb.rearrange("p (b o) -> p b o", o=1)

            def step1(v):
                v.tensor_reduce(amv, wv, AX.X, ALU.max, apply_absolute_value=True)
                return v.tensor_single_scalar(amb, amb, 1e-30, ALU.max)

            cx.on("vector", step1)
            cx.on("scalar", lambda s: s.activation(scb, amb, AF.Ln))

            def step2(v):
                v.tensor_scalar(eb, scb, 1.0 / LN2, -0.5, ALU.mult, ALU.add)
                return v.tensor_copy(scb, eb)

            cx.on("vector", step2)
            cx.on("scalar", lambda s: s.activation(scb, scb, AF.Exp, scale=LN2,
                                                   bias=constv[:, 1:2]))
            if dwf_ap is not None:
                cx.on("vector", lambda v: v.tensor_tensor(scb, scb, dwf_ap,
                                                          ALU.mult))
            else:
                cx.on("vector", lambda v: v.tensor_scalar(scb, scb,
                                                          sscal[:, 10:11], None,
                                                          ALU.mult))

            def step3a(v):
                v.reciprocal(amb, scb)
                return v.tensor_tensor(wv, wv, amv.broadcast_to([128, nb, 32]),
                                       ALU.mult)

            cx.on("vector", step3a)
            cx.on("scalar", lambda s: s.activation(ab, wb, AF.Abs))

            def step3(v):
                v.tensor_single_scalar(ab, ab, 6.0, ALU.min)
                v.tensor_single_scalar(g2b, ab, 2.0, ALU.is_ge)
                v.tensor_single_scalar(sgb, ab, 4.0, ALU.is_ge)
                v.scalar_tensor_tensor(g2b, sgb, 0.5, g2b, ALU.mult, ALU.add)
                v.tensor_scalar(g2b, g2b, -1.0, 2.0, ALU.mult, ALU.add)
                v.tensor_tensor(ib, ab, g2b, ALU.mult)
                v.reciprocal(g2b, g2b)
                v.tensor_tensor(ab, ib, g2b, ALU.mult)
                return v.tensor_single_scalar(sgb, wb, 0.0, ALU.is_ge)

            cx.on("vector", step3)

            def step4(v):
                v.tensor_scalar(sgb, sgb, 2.0, -1.0, ALU.mult, ALU.add)
                av = ab.rearrange("p (b q) -> p b q", q=32)
                v.tensor_tensor(ab, ab, sgb, ALU.mult)
                return v.tensor_tensor(wv, av, scv.broadcast_to([128, nb, 32]),
                                       ALU.mult)

            cx.on("vector", step4)

        ar = Arena()
        WQKVT = ar.al("wqkvt", [128, 8 * 3 * DM])
        pa_stage = ar.al("pa_stage", [128, DM])
        pa_xtc = ar.al("pa_xtc", [128, 8 * TB])
        pa_cos = ar.al("pa_cos", [128, S])
        pa_sin = ar.al("pa_sin", [128, S])
        pa_ev = ar.al("pa_ev", [128, TB])
        pa_t1 = ar.al("pa_t1", [128, TB])
        pa_t2 = ar.al("pa_t2", [128, TB])
        pa_vt = ar.al("pa_vt", [128, 512])

        cx.on("sync", lambda e: (e.dma_start(out=pa_cos[:], in_=c_cos[:]),
                                 e.dma_start(out=pa_sin[:], in_=c_sin[:]))[-1],
              dma=True)
        wt_prep_sbuf(w_qkv, 3 * DM, DM, WQKVT, 3 * DM, pa_stage)

        pair_mask = [(i // 2) * 2 + (1 - i % 2) for i in range(32)]
        for c in range(NT):
            for i in range(4):
                cx.on("sync", dma(pa_stage[:, 0:DM],
                                  x_in[c * TB + i * 128:c * TB + (i + 1) * 128, :]),
                      dma=True)

                def trx(t):
                    for kt in range(8):
                        last = t.transpose(pb[kt][:, 0:128],
                                           pa_stage[:, kt * 128:kt * 128 + 128],
                                           ident[:])
                    return last

                def cpx(s, i=i):
                    for kt in range(8):
                        last = s.copy(
                            pa_xtc[:, kt * TB + i * 128:kt * TB + (i + 1) * 128],
                            pb[kt][:, 0:128])
                    return last

                cx.on("tensor", trx)
                cx.on("scalar", cpx)
            cx.on("sync", lambda e, c=c: [
                e.dma_start(out=sXT[pt * 128:(pt + 1) * 128, c * TB:(c + 1) * TB],
                            in_=pa_xtc[:, pt * TB:(pt + 1) * TB])
                for pt in range(8)][-1], dma=True)
            for m in range(24):
                def mm(t, m=m):
                    for k in range(8):
                        last = t.matmul(
                            pb[m % 2][:, 0:TB],
                            WQKVT[:, k * 3 * DM + m * 128:k * 3 * DM + (m + 1) * 128],
                            pa_xtc[:, k * TB:(k + 1) * TB],
                            start=(k == 0), stop=(k == 7), **MM)
                    return last

                cx.on("tensor", mm)
                cx.on("scalar", lambda s, m=m: s.copy(pa_ev[:], pb[m % 2][:, 0:TB]))
                if m < 16:
                    def rope(v, c=c):
                        v.tensor_tensor(pa_t1[:], pa_ev[:],
                                        pa_cos[:, c * TB:(c + 1) * TB], ALU.mult)
                        v.stream_shuffle(pa_t2[:], pa_ev[:], pair_mask)
                        v.tensor_tensor(pa_t2[:], pa_t2[:],
                                        pa_sin[:, c * TB:(c + 1) * TB], ALU.mult)
                        return v.tensor_add(pa_ev[:], pa_t1[:], pa_t2[:])

                    cx.on("vector", rope)
                    qk = sQ if m < 8 else sK
                    hm = (m % 8) * 2
                    cx.on("sync", lambda e, qk=qk, hm=hm, c=c: (
                        e.dma_start(out=qk[hm, :, c * TB:(c + 1) * TB],
                                    in_=pa_ev[0:64, :]),
                        e.dma_start(out=qk[hm + 1, :, c * TB:(c + 1) * TB],
                                    in_=pa_ev[64:128, :]))[-1], dma=True)
                else:
                    h2 = (m - 16) * 2

                    def trv(t):
                        for i in range(4):
                            last = t.transpose(pb[4 + i][:, 0:128],
                                               pa_ev[:, i * 128:(i + 1) * 128],
                                               ident[:])
                        return last

                    def cpv(s):
                        for i in range(4):
                            last = s.copy(pa_vt[:, i * 128:(i + 1) * 128],
                                          pb[4 + i][:, 0:128])
                        return last

                    cx.on("tensor", trv)
                    cx.on("scalar", cpv)

                    def stv(e, h2=h2, c=c):
                        for i in range(4):
                            kcg = c * 4 + i
                            e.dma_start(out=sVT[h2, kcg],
                                        in_=pa_vt[:, i * 128:i * 128 + 64])
                            last = e.dma_start(out=sVT[h2 + 1, kcg],
                                               in_=pa_vt[:, i * 128 + 64:
                                                         (i + 1) * 128])
                        return last

                    cx.on("sync", stv, dma=True)

        ckpt("PA")

        ar = Arena()
        WOT = ar.al("wot", [128, 8 * DM])
        pb_stage = ar.al("pb_stage", [128, DM])
        kbuf = ar.al("kbuf", [64, S])
        qbuf = ar.al("qbuf", [64, TB])
        vtb = ar.al("vtb", [128, 32 * 65])
        expT = ar.al("expT", [128, 4 * TB])
        attn_sb = ar.al("attn_sb", [128, 8 * TB])
        rsb = ar.al("rsb", [128, 8 * TB])
        r2t8 = ar.al("r2t8", [128, 8 * TB])
        xres8 = ar.al("xres8", [128, 8 * TB])
        rbc64 = ar.al("rbc64", [64, TB])

        wt_prep_sbuf(w_o, DM, DM, WOT, DM, pb_stage)
        cx.on("vector", lambda v: v.memset(vtb[:], 1.0))

        for j in range(NT):
            nk = 4 * (j + 1)
            for h in range(NH):
                def hload(e, h=h, j=j, nk=nk):
                    e.dma_start(out=qbuf[:], in_=sQ[h, :, j * TB:(j + 1) * TB])
                    e.dma_start(out=kbuf[:, 0:nk * 128], in_=sK[h, :, 0:nk * 128])
                    for kc0 in range(0, nk, 8):
                        kn = min(8, nk - kc0)
                        last = e.dma_start(
                            out=vtb.ap().rearrange("p (kc c) -> p kc c", c=65)
                            [:, kc0:kc0 + kn, 0:64],
                            in_=sVT[h, kc0:kc0 + kn].rearrange("kc p c -> p kc c"))
                    return last

                cx.on("sync", hload, dma=True)
                for kc0 in range(0, nk, 4):
                    def mm_s(t, kc0=kc0):
                        for i in range(4):
                            last = t.matmul(
                                pb[i][:, 0:TB],
                                kbuf[:, (kc0 + i) * 128:(kc0 + i + 1) * 128],
                                qbuf[:], start=True, stop=True, **MM)
                        return last

                    cx.on("tensor", mm_s)

                    def ex(s):
                        for i in range(4):
                            last = s.activation(expT[:, i * TB:(i + 1) * TB],
                                                pb[i][:, 0:TB], AF.Exp, scale=0.125)
                        return last

                    cx.on("scalar", ex)
                    if kc0 == 4 * j:
                        cx.on("vector", lambda v: v.tensor_tensor(
                            expT[:], expT[:], maskt[:], ALU.mult))

                    def mm_av(t, kc0=kc0, nk=nk):
                        for i in range(4):
                            last = t.matmul(
                                pb[4][0:65, 0:TB],
                                vtb[:, (kc0 + i) * 65:(kc0 + i + 1) * 65],
                                expT[:, i * TB:(i + 1) * TB],
                                start=(kc0 + i == 0), stop=(kc0 + i == nk - 1),
                                **MM)
                        return last

                    cx.on("tensor", mm_av)
                cx.on("vector", lambda v: v.reciprocal(mrow[:], pb[4][64:65, 0:TB]))
                cx.on("tensor", lambda t: t.matmul(pb[5][0:64, 0:TB],
                                                   ones1r[0:1, 0:64], mrow[:],
                                                   start=True, stop=True, **MM))
                cx.on("scalar", lambda s: s.copy(rbc64[:], pb[5][0:64, 0:TB]))
                pt, sub = h // 2, (h % 2) * 64
                cx.on("vector", lambda v, pt=pt, sub=sub: v.tensor_tensor(
                    attn_sb[sub:sub + 64, pt * TB:(pt + 1) * TB],
                    pb[4][0:64, 0:TB], rbc64[:], ALU.mult))
            if "ATT" in debug:
                cx.on("sync", lambda e, j=j: [
                    e.dma_start(out=sATT[pt * 128:(pt + 1) * 128,
                                         j * TB:(j + 1) * TB],
                                in_=attn_sb[:, pt * TB:(pt + 1) * TB])
                    for pt in range(8)], dma=True)
            cx.on("sync", lambda e, j=j: [
                e.dma_start(out=xres8[:, m * TB:(m + 1) * TB],
                            in_=sXT[m * 128:(m + 1) * 128, j * TB:(j + 1) * TB])
                for m in range(8)][-1], dma=True)

            def mm_o(t):
                for m in range(8):
                    for k in range(8):
                        last = t.matmul(
                            pb[m][:, 0:TB],
                            WOT[:, k * DM + m * 128:k * DM + (m + 1) * 128],
                            attn_sb[:, k * TB:(k + 1) * TB],
                            start=(k == 0), stop=(k == 7), **MM)
                return last

            cx.on("tensor", mm_o)

            def radd(v):
                for m in range(8):
                    last = v.tensor_add(rsb[:, m * TB:(m + 1) * TB],
                                        pb[m][:, 0:TB],
                                        xres8[:, m * TB:(m + 1) * TB])
                return last

            cx.on("vector", radd)

            def rsq(s):
                for m in range(8):
                    last = s.activation(r2t8[:, m * TB:(m + 1) * TB],
                                        rsb[:, m * TB:(m + 1) * TB], AF.Square)
                return last

            cx.on("scalar", rsq)

            def mm_st(t):
                for m in range(8):
                    t.matmul(pb[6][0:1, 0:TB], ones128[:, 0:1],
                             rsb[:, m * TB:(m + 1) * TB], start=(m == 0),
                             stop=(m == 7), **MM)
                    last = t.matmul(pb[7][0:1, 0:TB], ones128[:, 0:1],
                                    r2t8[:, m * TB:(m + 1) * TB], start=(m == 0),
                                    stop=(m == 7), **MM)
                return last

            cx.on("tensor", mm_st)
            emit_ln(rsb, sX1, j, max_slot=0)

        ckpt("PB")

        exchange(0, 0, 0)
        ar = Arena()
        c0_x = ar.al("c0_x", [128, 2 * S])
        c0_i = ar.al("c0_i", [128, S], I32)
        ld_t = [0] * 8
        qmark = [0] * 8
        pc0_st = [0] * 8
        ld_t[0] = cx.dma_async(lambda e: e.dma_start(
            out=c0_x[:, 0:S], in_=sX1[0:128, :]), after=cx.mark())
        for pt in range(8):
            if pt + 1 < 8:
                ld_t[pt + 1] = cx.dma_async(
                    lambda e, pt=pt: e.dma_start(
                        out=c0_x[:, ((pt + 1) % 2) * S:((pt + 1) % 2) * S + S],
                        in_=sX1[(pt + 1) * 128:(pt + 2) * 128, :]),
                    after=qmark[pt - 1] if pt >= 1 else None,
                    after_d2=pc0_st[pt - 1] if pt >= 1 else None)

            def qdq(v, pt=pt):
                b = (pt % 2) * S
                v.tensor_scalar(c0_i[:], c0_x[:, b:b + S], sscal[:, 1:2], None,
                                ALU.mult)
                return v.tensor_scalar(c0_x[:, b:b + S], c0_i[:],
                                       sscal[:, 0:1], None, ALU.mult)

            cx.on("vector", qdq, pre_dma=ld_t[pt])
            qmark[pt] = cx.mark()
            pc0_st[pt] = cx.dma_async(lambda e, pt=pt: e.dma_start(
                out=sXQ[pt * 128:(pt + 1) * 128, :],
                in_=c0_x[:, (pt % 2) * S:(pt % 2) * S + S]), after=qmark[pt])

        ckpt("PC0")

        cx.on("vector", lambda v: v.memset(tmp128[:], 0.0), pre_dma=cx.dn)
        ar = Arena()
        p1_w = ar.al("p1_w", [128, DM])
        p1_i = ar.al("p1_i", [128, DM], I32)
        p1_am = ar.al("p1_am", [128, 32])
        p1_rs = ar.al("p1_rs", [128, 32])
        p1_wt = ar.al("p1_wt", [128, 8 * 128])
        xq_sb = ar.al("xq_sb", [128, 8 * S])
        p1_st = ar.al("p1_st", [128, 2 * 8 * 128])
        p1_ev = ar.al("p1_ev", [128, 2 * S])

        def lpbq_quant(v, wt_, it_, amt, rst, nb):
            wv = wt_.rearrange("p (b q) -> p b q", q=32)
            iv = it_.rearrange("p (b q) -> p b q", q=32)
            amv = amt.rearrange("p (b o) -> p b o", o=1)
            rsv = rst.rearrange("p (b o) -> p b o", o=1)
            v.tensor_reduce(amv, wv, AX.X, ALU.max, apply_absolute_value=True)
            v.tensor_scalar(amt, amt, 1.0 / 7.0, 1e-12, ALU.mult, ALU.max)
            v.reciprocal(rst, amt)
            v.tensor_tensor(iv, wv, rsv.broadcast_to([128, nb, 32]), ALU.mult)
            return v.tensor_tensor(wv, iv, amv.broadcast_to([128, nb, 32]),
                                   ALU.mult)

        def lpbq_prep(src, dstT, n_rows, n_cols, wbuf, ibuf, ambuf, rsbuf, wtbuf):
            nmt, nkt, nb = n_rows // 128, n_cols // 128, n_cols // 32
            for m in range(nmt):
                cx.on("sync", dma(wbuf[:, 0:n_cols], src[m * 128:(m + 1) * 128, :]),
                      dma=True)
                cx.on("vector", lambda v, nb=nb, ncc=n_cols: lpbq_quant(
                    v, wbuf[:, 0:ncc], ibuf[:, 0:ncc], ambuf[:, 0:nb],
                    rsbuf[:, 0:nb], nb))
                for kt0 in range(0, nkt, 8):
                    kts = list(range(kt0, min(kt0 + 8, nkt)))

                    def tr(t, kts=kts):
                        for kt in kts:
                            last = t.transpose(pb[kt % 8][:, 0:128],
                                               wbuf[:, kt * 128:kt * 128 + 128],
                                               ident[:])
                        return last

                    def cp(s, kts=kts):
                        for kt in kts:
                            last = s.copy(wtbuf[:, kt * 128:(kt + 1) * 128],
                                          pb[kt % 8][:, 0:128])
                        return last

                    cx.on("tensor", tr)
                    cx.on("scalar", cp)
                cx.on("sync", lambda e, m=m, nkt=nkt, dstT=dstT, wtbuf=wtbuf: [
                    e.dma_start(out=dstT[kt, :, m * 128:(m + 1) * 128],
                                in_=wtbuf[:, kt * 128:(kt + 1) * 128])
                    for kt in range(nkt)][-1], dma=True)

        lpbq_prep(w_up, sWUT, FF, DM, p1_w, p1_i, p1_am, p1_rs, p1_wt)
        lpbq_prep(w_gate, sWGT, FF, DM, p1_w, p1_i, p1_am, p1_rs, p1_wt)

        cx.on("sync", lambda e: [
            e.dma_start(out=xq_sb[:, pt * S:(pt + 1) * S],
                        in_=sXQ[pt * 128:(pt + 1) * 128, :]) for pt in range(8)][-1],
            dma=True)

        def big_mm(dstD, srcWT, max_slot, carry=None):
            NM = FF // 128
            ld_t = [0] * NM
            mm_mark = [0] * NM
            ev_mark = [0] * NM
            st_t = [0] * NM
            prev_mm, prev_st = carry if carry else (None, [None, None])

            def ld(e, m, srcWT=srcWT):
                b = (m % 2) * 1024
                for k in range(8):
                    e.dma_start(out=p1_st[:, b + k * 128:b + (k + 1) * 128],
                                in_=srcWT[k, :, m * 128:(m + 1) * 128])

            ld_t[0] = cx.dma_async(lambda e: ld(e, 0), after=prev_mm)
            for m in range(NM):
                if m + 1 < NM:
                    anchor = mm_mark[m - 1] if m >= 1 else prev_mm
                    ld_t[m + 1] = cx.dma_async(
                        lambda e, m=m: ld(e, m + 1), after=anchor)

                def mm(t, m=m):
                    b = (m % 2) * 1024
                    for k in range(8):
                        for tt in range(8):
                            last = t.matmul(pb[tt][:, 0:TB],
                                            p1_st[:, b + k * 128:b + (k + 1) * 128],
                                            xq_sb[:, k * S + tt * TB:
                                                  k * S + (tt + 1) * TB],
                                            start=(k == 0), stop=(k == 7), **MM)
                    return last

                cx.on("tensor", mm, pre_dma=ld_t[m])
                mm_mark[m] = cx.mark()

                def evac(s, m=m):
                    b = (m % 2) * S
                    for tt in range(8):
                        last = s.copy(p1_ev[:, b + tt * TB:b + (tt + 1) * TB],
                                      pb[tt][:, 0:TB])
                    return last

                cx.on("scalar", evac,
                      pre_dma=st_t[m - 2] if m >= 2 else prev_st[m])

                def mx(v, m=m):
                    b = (m % 2) * S
                    v.tensor_reduce(tmp128[:], p1_ev[:, b:b + S], AX.X, ALU.max,
                                    apply_absolute_value=True)
                    return v.tensor_tensor(smax[:, max_slot:max_slot + 1],
                                           smax[:, max_slot:max_slot + 1],
                                           tmp128[:], ALU.max)

                cx.on("vector", mx)
                ev_mark[m] = cx.mark()
                st_t[m] = cx.dma_async(
                    lambda e, m=m: e.dma_start(
                        out=dstD[m * 128:(m + 1) * 128, :],
                        in_=p1_ev[:, (m % 2) * S:(m % 2) * S + S]),
                    after=ev_mark[m])
            return (mm_mark[NM - 1], [st_t[NM - 2], st_t[NM - 1]])

        up_tail = big_mm(sUP, sWUT, 1)
        big_mm(sGA, sWGT, 2, carry=up_tail)

        ckpt("PC1")

        exchange(1, 1, 2)
        exchange(2, 2, 4)
        ar = Arena()
        c2_u = ar.al("c2_u", [128, 2 * S])
        c2_g = ar.al("c2_g", [128, 2 * S])
        c2_i = ar.al("c2_i", [128, S], I32)
        c2_s = ar.al("c2_s", [128, S])
        NM2 = FF // 128
        ld_t = [0] * NM2
        q3m = [0] * NM2
        stg1 = [0] * NM2

        def ld2(e, mt):
            b = (mt % 2) * S
            e.dma_start(out=c2_u[:, b:b + S], in_=sUP[mt * 128:(mt + 1) * 128, :])
            e.dma_start(out=c2_g[:, b:b + S], in_=sGA[mt * 128:(mt + 1) * 128, :])

        ld_t[0] = cx.dma_async(lambda e: ld2(e, 0), after_d2=cx.dn)
        for mt in range(NM2):
            b = (mt % 2) * S
            if mt + 1 < NM2:
                ld_t[mt + 1] = cx.dma_async(
                    lambda e, mt=mt: ld2(e, mt + 1),
                    after=q3m[mt - 1] if mt >= 1 else None,
                    after_d2=stg1[mt - 1] if mt >= 1 else None)
            cu, cg = c2_u[:, b:b + S], c2_g[:, b:b + S]

            def q2(v, cu=cu, cg=cg):
                v.tensor_scalar(c2_i[:], cu, sscal[:, 3:4], None, ALU.mult)
                v.tensor_scalar(cu, c2_i[:], sscal[:, 2:3], None, ALU.mult)
                v.tensor_scalar(c2_i[:], cg, sscal[:, 5:6], None, ALU.mult)
                return v.tensor_scalar(cg, c2_i[:], sscal[:, 4:5], None, ALU.mult)

            cx.on("vector", q2, pre_dma=ld_t[mt])
            q2m = cx.mark()
            cx.dma_async(lambda e, mt=mt, cu=cu: e.dma_start(
                out=sUPQ[mt * 128:(mt + 1) * 128, :], in_=cu), after=q2m)
            cx.on("scalar", lambda s, cg=cg: s.activation(c2_s[:], cg, AF.Sigmoid))

            def q3(v, cg=cg):
                v.tensor_scalar(c2_i[:], c2_s[:], 65536.0, None, ALU.mult)
                v.tensor_scalar(c2_s[:], c2_i[:], 65535.0, 1.0 / 65536.0,
                                ALU.min, ALU.mult)
                v.tensor_tensor(cg, cg, c2_s[:], ALU.mult)
                v.tensor_reduce(tmp128[:], cg, AX.X, ALU.max,
                                apply_absolute_value=True)
                return v.tensor_tensor(smax[:, 3:4], smax[:, 3:4], tmp128[:],
                                       ALU.max)

            cx.on("vector", q3)
            q3m[mt] = cx.mark()
            stg1[mt] = cx.dma_async(lambda e, mt=mt, cg=cg: e.dma_start(
                out=sG1[mt * 128:(mt + 1) * 128, :], in_=cg), after=q3m[mt])

        ckpt("PC2")

        exchange(3, 3, 6)
        ld_t = [0] * NM2
        q4m = [0] * NM2
        stpr = [0] * NM2

        def ld3(e, mt):
            b = (mt % 2) * S
            e.dma_start(out=c2_g[:, b:b + S], in_=sG1[mt * 128:(mt + 1) * 128, :])
            e.dma_start(out=c2_u[:, b:b + S], in_=sUPQ[mt * 128:(mt + 1) * 128, :])

        ld_t[0] = cx.dma_async(lambda e: ld3(e, 0), after_d2=cx.dn)
        for mt in range(NM2):
            b = (mt % 2) * S
            if mt + 1 < NM2:
                ld_t[mt + 1] = cx.dma_async(
                    lambda e, mt=mt: ld3(e, mt + 1),
                    after=q4m[mt - 1] if mt >= 1 else None,
                    after_d2=stpr[mt - 1] if mt >= 1 else None)
            cu, cg = c2_u[:, b:b + S], c2_g[:, b:b + S]

            def q4(v, cu=cu, cg=cg):
                v.tensor_scalar(c2_i[:], cg, sscal[:, 7:8], None, ALU.mult)
                v.tensor_scalar(cg, c2_i[:], sscal[:, 6:7], None, ALU.mult)
                v.tensor_tensor(cg, cg, cu, ALU.mult)
                v.tensor_reduce(tmp128[:], cg, AX.X, ALU.max,
                                apply_absolute_value=True)
                return v.tensor_tensor(smax[:, 4:5], smax[:, 4:5], tmp128[:],
                                       ALU.max)

            cx.on("vector", q4, pre_dma=ld_t[mt])
            q4m[mt] = cx.mark()
            stpr[mt] = cx.dma_async(lambda e, mt=mt, cg=cg: e.dma_start(
                out=sPROD[mt * 128:(mt + 1) * 128, :], in_=cg), after=q4m[mt])

        ckpt("PC3")

        exchange(4, 4, 8)
        cx.on("vector", lambda v: v.memset(tmp128[:], 0.0), pre_dma=cx.dn)
        ar = Arena()
        c4_w = ar.al("c4_w", [128, FF])
        c4_i = ar.al("c4_i", [128, FF], I32)
        c4_am = ar.al("c4_am", [128, 96])
        c4_rs = ar.al("c4_rs", [128, 96])
        c4_wt = ar.al("c4_wt", [128, 24 * 128])
        c4_st = ar.al("c4_st", [128, 8 * DM])
        c4_pq = ar.al("c4_pq", [128, 8 * TB])
        c4_oi = ar.al("c4_oi", [128, TB], I32)
        c4_rsb = ar.al("c4_rsb", [128, 8 * TB])
        c4_r2 = ar.al("c4_r2", [128, 8 * TB])
        c4_xr = ar.al("c4_xr", [128, 8 * TB])

        lpbq_prep(w_down, sWDT, DM, FF, c4_w, c4_i, c4_am, c4_rs, c4_wt)

        for tt in range(NT):
            for kg in range(3):
                def ld(e, tt=tt, kg=kg):
                    for kk in range(8):
                        k = kg * 8 + kk
                        e.dma_start(out=c4_st[:, kk * DM:(kk + 1) * DM],
                                    in_=sWDT[k, :, :])
                        last = e.dma_start(
                            out=c4_pq[:, kk * TB:(kk + 1) * TB],
                            in_=sPROD[k * 128:(k + 1) * 128,
                                      tt * TB:(tt + 1) * TB])
                    return last

                cx.on("sync", ld, dma=True)

                def oq(v):
                    for kk in range(8):
                        sl = c4_pq[:, kk * TB:(kk + 1) * TB]
                        v.tensor_scalar(c4_oi[:], sl, sscal[:, 9:10], None,
                                        ALU.mult)
                        last = v.tensor_scalar(sl, c4_oi[:], sscal[:, 8:9], None,
                                               ALU.mult)
                    return last

                cx.on("vector", oq)

                def mm(t, kg=kg):
                    for kk in range(8):
                        k = kg * 8 + kk
                        for m in range(8):
                            last = t.matmul(
                                pb[m][:, 0:TB],
                                c4_st[:, kk * DM + m * 128:kk * DM + (m + 1) * 128],
                                c4_pq[:, kk * TB:(kk + 1) * TB],
                                start=(k == 0), stop=(k == 23), **MM)
                    return last

                cx.on("tensor", mm)
            cx.on("sync", lambda e, tt=tt: [
                e.dma_start(out=c4_xr[:, m * TB:(m + 1) * TB],
                            in_=sX1[m * 128:(m + 1) * 128, tt * TB:(tt + 1) * TB])
                for m in range(8)][-1], dma=True)

            def radd(v):
                for m in range(8):
                    last = v.tensor_add(c4_rsb[:, m * TB:(m + 1) * TB],
                                        pb[m][:, 0:TB],
                                        c4_xr[:, m * TB:(m + 1) * TB])
                return last

            cx.on("vector", radd)

            def rsq(s):
                for m in range(8):
                    last = s.activation(c4_r2[:, m * TB:(m + 1) * TB],
                                        c4_rsb[:, m * TB:(m + 1) * TB], AF.Square)
                return last

            cx.on("scalar", rsq)

            def mm_st(t):
                for m in range(8):
                    t.matmul(pb[6][0:1, 0:TB], ones128[:, 0:1],
                             c4_rsb[:, m * TB:(m + 1) * TB], start=(m == 0),
                             stop=(m == 7), **MM)
                    last = t.matmul(pb[7][0:1, 0:TB], ones128[:, 0:1],
                                    c4_r2[:, m * TB:(m + 1) * TB], start=(m == 0),
                                    stop=(m == 7), **MM)
                return last

            cx.on("tensor", mm_st)
            emit_ln(c4_rsb, sX2, tt)

        ckpt("PC4")

        ar = Arena()
        d_wst = ar.al("d_wst", [128, DM])
        d_wt = ar.al("d_wt", [128, 8 * 128])
        x2_sb = ar.al("x2_sb", [128, 8 * S])
        d_st = ar.al("d_st", [128, 2 * 8 * 128])
        d_cb = ar.al("d_cb", [128, S + 3])
        d_acc = ar.al("d_acc", [128, S])
        xpt = ar.al("xpt", [128, 16 * 33])
        xps = ar.al("xps", [33, TB])

        for m in range(2 * DI // 128):
            cx.on("sync", dma(d_wst[:, 0:DM], w_ip[m * 128:(m + 1) * 128, :]),
                  dma=True)
            for kt0 in range(0, 8, 4):
                kts = list(range(kt0, kt0 + 4))

                def tr(t, kts=kts):
                    for kt in kts:
                        last = t.transpose(pb[4 + kt % 4][:, 0:128],
                                           d_wst[:, kt * 128:kt * 128 + 128],
                                           ident[:])
                    return last

                def cp(s, kts=kts):
                    for kt in kts:
                        last = s.copy(d_wt[:, kt * 128:(kt + 1) * 128],
                                      pb[4 + kt % 4][:, 0:128])
                    return last

                cx.on("tensor", tr)
                cx.on("scalar", cp)
            cx.on("sync", lambda e, m=m: [
                e.dma_start(out=sWIPT[kt, :, m * 128:(m + 1) * 128],
                            in_=d_wt[:, kt * 128:(kt + 1) * 128])
                for kt in range(8)][-1], dma=True)

        cx.on("sync", lambda e: [
            e.dma_start(out=x2_sb[:, pt * S:(pt + 1) * S],
                        in_=sX2[pt * 128:(pt + 1) * 128, :]) for pt in range(8)][-1],
            dma=True)

        NMD = 2 * DI // 128
        ld_t = [0] * NMD
        mm_mark = [0] * NMD
        st_t = [0] * NMD

        def ldst(e, m):
            b = (m % 2) * 1024
            for k in range(8):
                e.dma_start(out=d_st[:, b + k * 128:b + (k + 1) * 128],
                            in_=sWIPT[k, :, m * 128:(m + 1) * 128])

        ld_t[0] = cx.dma_async(lambda e: ldst(e, 0))
        for m in range(NMD):
            if m + 1 < NMD:
                ld_t[m + 1] = cx.dma_async(
                    lambda e, m=m: ldst(e, m + 1),
                    after=mm_mark[m - 1] if m >= 1 else None)

            def mm(t, m=m):
                b = (m % 2) * 1024
                for k in range(8):
                    for tt in range(8):
                        last = t.matmul(pb[tt][:, 0:TB],
                                        d_st[:, b + k * 128:b + (k + 1) * 128],
                                        x2_sb[:, k * S + tt * TB:k * S + (tt + 1) * TB],
                                        start=(k == 0), stop=(k == 7), **MM)
                return last

            cx.on("tensor", mm, pre_dma=ld_t[m])
            mm_mark[m] = cx.mark()
            if m < 16:
                db = m

                def evac(s):
                    for tt in range(8):
                        last = s.copy(d_cb[:, 3 + tt * TB:3 + (tt + 1) * TB],
                                      pb[tt][:, 0:TB])
                    return last

                cx.on("scalar", evac,
                      pre_dma=st_t[m - 1] if m >= 1 else None)

                def conv(v, db=db):
                    v.memset(d_cb[:, 0:3], 0.0)
                    v.tensor_scalar(d_acc[:], d_cb[:, 0:S],
                                    pvec[:, 272 + db * 4:273 + db * 4], None,
                                    ALU.mult)
                    for kk in range(1, 4):
                        last = v.scalar_tensor_tensor(
                            d_acc[:], d_cb[:, kk:kk + S],
                            pvec[:, 272 + db * 4 + kk:273 + db * 4 + kk],
                            d_acc[:], ALU.mult, ALU.add)
                    return last

                cx.on("vector", conv)
                cx.on("scalar", lambda s, db=db: s.activation(
                    d_cb[:, 0:S], d_acc[:], AF.Silu,
                    bias=pvec[:, 368 + db:369 + db]))
                st_t[m] = cx.dma_async(lambda e, db=db: e.dma_start(
                    out=sXC[db * 128:(db + 1) * 128, :], in_=d_cb[:, 0:S]),
                    after=cx.mark())
            else:
                db = m - 16

                def zsilu(s):
                    for tt in range(8):
                        last = s.activation(d_acc[:, tt * TB:(tt + 1) * TB],
                                            pb[tt][:, 0:TB], AF.Silu)
                    return last

                cx.on("scalar", zsilu,
                      pre_dma=st_t[m - 1] if m >= 1 else None)
                st_t[m] = cx.dma_async(lambda e, db=db: e.dma_start(
                    out=sSZ[db * 128:(db + 1) * 128, :], in_=d_acc[:, 0:S]),
                    after=cx.mark())

        cx.on("vector", lambda v: v.memset(tmp128[:], 0.0), pre_dma=cx.dn)
        for kt in range(16):
            cx.on("sync", dma(d_wst[0:33, 0:128], w_xp[:, kt * 128:(kt + 1) * 128]),
                  dma=True)
            cx.on("tensor", lambda t: t.transpose(pb[4][0:128, 0:33],
                                                  d_wst[0:33, 0:128],
                                                  ident[0:33, 0:33]))
            cx.on("scalar", lambda s, kt=kt: s.copy(xpt[:, kt * 33:(kt + 1) * 33],
                                                    pb[4][0:128, 0:33]))
        for tt in range(8):
            cx.on("sync", lambda e, tt=tt: [
                e.dma_start(out=d_acc[:, k * 256:(k + 1) * 256],
                            in_=sXC[k * 128:(k + 1) * 128,
                                    tt * TB:tt * TB + 256])
                for k in range(16)][-1], dma=True)
            cx.on("sync", lambda e, tt=tt: [
                e.dma_start(out=d_cb[:, k * 256:(k + 1) * 256],
                            in_=sXC[k * 128:(k + 1) * 128,
                                    tt * TB + 256:(tt + 1) * TB])
                for k in range(16)][-1], dma=True)

            def mmx(t, tt=tt):
                for k in range(16):
                    t.matmul(pb[6][0:33, 0:256], xpt[:, k * 33:(k + 1) * 33],
                             d_acc[:, k * 256:(k + 1) * 256],
                             start=(k == 0), stop=(k == 15), **MM)
                    last = t.matmul(pb[7][0:33, 0:256],
                                    xpt[:, k * 33:(k + 1) * 33],
                                    d_cb[:, k * 256:(k + 1) * 256],
                                    start=(k == 0), stop=(k == 15), **MM)
                return last

            cx.on("tensor", mmx)

            def xpev(s):
                s.copy(xps[:, 0:256], pb[6][0:33, 0:256])
                return s.copy(xps[:, 256:512], pb[7][0:33, 0:256])

            cx.on("scalar", xpev)
            cx.on("sync", lambda e, tt=tt: e.dma_start(
                out=sXP[:, tt * TB:(tt + 1) * TB], in_=xps[:]), dma=True)

        ckpt("PD")

        ar = Arena()
        e_bbc = ar.al("e_bbc", [128, 16 * TB])
        e_cbc = ar.al("e_cbc", [128, 16 * TB])
        e_da = ar.al("e_da", [128, 16 * TB])
        e_m = ar.al("e_m", [128, 16 * TB])
        e_dtb = ar.al("e_dtb", [128, TB])
        e_row = ar.al("e_row", [1, 4 * TB])
        e_dt = ar.al("e_dt", [128, TB])
        e_sp1 = ar.al("e_sp1", [128, TB])
        e_u = ar.al("e_u", [128, TB])
        e_xc = ar.al("e_xc", [128, 2 * TB])
        e_sz = ar.al("e_sz", [128, 2 * TB])
        e_y = ar.al("e_y", [128, 2 * TB])
        e_st = ar.al("e_st", [128, 16 * 16])

        cx.on("vector", lambda v: v.memset(e_st[:], 0.0))
        pe_ld = [0] * 16
        pe_gm = [0] * 16
        pe_st = [0] * 16
        for c in range(NT):
            cx.on("sync", lambda e, c=c: e.dma_start(
                out=e_row[0:1, 0:TB], in_=sXP[0:1, c * TB:(c + 1) * TB]), dma=True)
            cx.on("tensor", lambda t: t.matmul(pb[4][:, 0:TB], ones1r[:],
                                               e_row[0:1, 0:TB], start=True,
                                               stop=True, **MM))
            cx.on("scalar", lambda s: s.copy(e_dtb[:], pb[4][:, 0:TB]))
            for grp, dst in ((1, e_bbc), (17, e_cbc)):
                for n0 in range(0, 16, 4):
                    cx.on("sync", lambda e, c=c, grp=grp, n0=n0: e.dma_start(
                        out=e_row.ap().rearrange("p (n t) -> p n t", n=4),
                        in_=sXP[grp + n0:grp + n0 + 4, c * TB:(c + 1) * TB]),
                        dma=True)

                    def bmm(t):
                        for i in range(4):
                            last = t.matmul(pb[i][:, 0:TB], ones1r[:],
                                            e_row[0:1, i * TB:(i + 1) * TB],
                                            start=True, stop=True, **MM)
                        return last

                    def bcp(s, n0=n0, dst=dst):
                        for i in range(4):
                            last = s.copy(dst[:, (n0 + i) * TB:(n0 + i + 1) * TB],
                                          pb[i][:, 0:TB])
                        return last

                    cx.on("tensor", bmm)
                    cx.on("scalar", bcp)
            for db in range(16):
                def ldx(e, db=db, c=c):
                    b = (db % 2) * TB
                    e.dma_start(out=e_xc[:, b:b + TB],
                                in_=sXC[db * 128:(db + 1) * 128,
                                        c * TB:(c + 1) * TB])
                    e.dma_start(out=e_sz[:, b:b + TB],
                                in_=sSZ[db * 128:(db + 1) * 128,
                                        c * TB:(c + 1) * TB])

                if db == 0:
                    pe_ld[0] = cx.dma_async(ldx, after=pe_gm[14] or None)
                if db + 1 < 16:
                    anc = pe_gm[db - 1] if db >= 1 else (pe_gm[15] or None)
                    pe_ld[db + 1] = cx.dma_async(
                        lambda e, db=db, c=c: ldx(e, db + 1, c), after=anc)

                def sp(s, db=db):
                    s.activation(e_sp1[:], e_dtb[:], AF.Abs,
                                 scale=pvec[:, 336 + db:337 + db],
                                 bias=pvec[:, 352 + db:353 + db])
                    s.activation(e_sp1[:], e_sp1[:], AF.Exp, scale=-1.0)
                    s.activation(e_sp1[:], e_sp1[:], AF.Ln, bias=1.0)
                    return s.activation(e_dt[:], e_dtb[:], AF.Relu,
                                        scale=pvec[:, 336 + db:337 + db],
                                        bias=pvec[:, 352 + db:353 + db])

                cx.on("scalar", sp)

                def mkdt(v, db=db):
                    b = (db % 2) * TB
                    v.tensor_add(e_dt[:], e_dt[:], e_sp1[:])
                    return v.tensor_tensor(e_u[:], e_dt[:], e_xc[:, b:b + TB],
                                           ALU.mult)

                cx.on("vector", mkdt, pre_dma=pe_ld[db])

                def dA(s, db=db):
                    for n in range(16):
                        last = s.activation(e_da[:, n * TB:(n + 1) * TB], e_dt[:],
                                            AF.Exp,
                                            scale=pvec[:, db * 16 + n:
                                                       db * 16 + n + 1])
                    return last

                cx.on("scalar", dA)
                cx.on("vector", lambda v: v.tensor_tensor(
                    e_m.ap().rearrange("p (n t) -> p n t", n=16),
                    e_u.ap().rearrange("p (o t) -> p o t", o=1)
                    .broadcast_to([128, 16, TB]),
                    e_bbc.ap().rearrange("p (n t) -> p n t", n=16), ALU.mult))

                def scan(v, db=db):
                    for n in range(16):
                        sl = slice(n * TB, (n + 1) * TB)
                        last = v.tensor_tensor_scan(
                            e_m[:, sl], e_da[:, sl], e_m[:, sl],
                            e_st[:, db * 16 + n:db * 16 + n + 1],
                            ALU.mult, ALU.add)
                    return last

                cx.on("vector", scan)
                cx.on("vector", lambda v, db=db: v.tensor_copy(
                    e_st[:, db * 16:(db + 1) * 16],
                    e_m.ap().rearrange("p (n t) -> p n t", n=16)[:, :, TB - 1:TB]))
                cx.on("vector", lambda v: v.tensor_tensor(e_m[:], e_m[:], e_cbc[:],
                                                          ALU.mult))
                cx.on("vector", lambda v, db=db: v.tensor_reduce(
                    e_y.ap()[:, (db % 2) * TB:(db % 2) * TB + TB]
                    .rearrange("p (t o) -> p t o", o=1),
                    e_m.ap().rearrange("p (n t) -> p t n", n=16), AX.X, ALU.add),
                      pre_dma=pe_st[db - 2] if db >= 2
                      else (pe_st[14 + db] or None))

                def gate(v, db=db):
                    b = (db % 2) * TB
                    v.scalar_tensor_tensor(e_y[:, b:b + TB], e_xc[:, b:b + TB],
                                           pvec[:, 256 + db:257 + db],
                                           e_y[:, b:b + TB], ALU.mult, ALU.add)
                    return v.tensor_tensor(e_y[:, b:b + TB], e_y[:, b:b + TB],
                                           e_sz[:, b:b + TB], ALU.mult)

                cx.on("vector", gate)
                pe_gm[db] = cx.mark()
                pe_st[db] = cx.dma_async(lambda e, db=db, c=c: e.dma_start(
                    out=sG[db * 128:(db + 1) * 128, c * TB:(c + 1) * TB],
                    in_=e_y[:, (db % 2) * TB:(db % 2) * TB + TB]),
                    after=pe_gm[db])

        ckpt("PE")

        cx.on("vector", lambda v: v.memset(tmp128[:], 0.0), pre_dma=cx.dn)
        ar = Arena()
        WOPT = ar.al("wopt", [128, 16 * DM])
        f_wst = ar.al("f_wst", [128, DI])
        f_g16 = ar.al("f_g16", [128, 16 * TB])
        f_rsb = ar.al("f_rsb", [128, 8 * TB])
        f_r2 = ar.al("f_r2", [128, 8 * TB])
        f_xr = ar.al("f_xr", [128, 8 * TB])

        for m in range(8):
            cx.on("sync", dma(f_wst[:, 0:DI], w_op[m * 128:(m + 1) * 128, :]),
                  dma=True)
            for kt0 in range(0, 16, 4):
                kts = list(range(kt0, kt0 + 4))

                def tr(t, kts=kts):
                    for kt in kts:
                        last = t.transpose(pb[4 + kt % 4][:, 0:128],
                                           f_wst[:, kt * 128:kt * 128 + 128],
                                           ident[:])
                    return last

                def cp(s, kts=kts, m=m):
                    for kt in kts:
                        last = s.copy(WOPT[:, kt * DM + m * 128:kt * DM + (m + 1) * 128],
                                      pb[4 + kt % 4][:, 0:128])
                    return last

                cx.on("tensor", tr)
                cx.on("scalar", cp)

        pf_ld = [0] * NT
        pf_rm = [0] * NT
        pf_st = [0] * NT

        def gld(e, tt):
            for k in range(16):
                e.dma_start(
                    out=f_g16[:, k * TB:(k + 1) * TB],
                    in_=sG[k * 128:(k + 1) * 128, tt * TB:(tt + 1) * TB])
            for m in range(8):
                e.dma_start(
                    out=f_xr[:, m * TB:(m + 1) * TB],
                    in_=sX2[m * 128:(m + 1) * 128, tt * TB:(tt + 1) * TB])

        pf_ld[0] = cx.dma_async(lambda e: gld(e, 0))
        for tt in range(NT):
            def mm(t):
                for k in range(16):
                    for m in range(8):
                        last = t.matmul(pb[m][:, 0:TB],
                                        WOPT[:, k * DM + m * 128:k * DM + (m + 1) * 128],
                                        f_g16[:, k * TB:(k + 1) * TB],
                                        start=(k == 0), stop=(k == 15), **MM)
                return last

            cx.on("tensor", mm, pre_dma=pf_ld[tt])
            mmk = cx.mark()

            def radd(v):
                for m in range(8):
                    last = v.tensor_add(f_rsb[:, m * TB:(m + 1) * TB],
                                        pb[m][:, 0:TB],
                                        f_xr[:, m * TB:(m + 1) * TB])
                return last

            cx.on("vector", radd,
                  pre_dma=pf_st[tt - 1] if tt >= 1 else None)
            pf_rm[tt] = cx.mark()
            if tt + 1 < NT:
                pf_ld[tt + 1] = cx.dma_async(
                    lambda e, tt=tt: gld(e, tt + 1), after=pf_rm[tt])

            def rsq(s):
                for m in range(8):
                    last = s.activation(f_r2[:, m * TB:(m + 1) * TB],
                                        f_rsb[:, m * TB:(m + 1) * TB], AF.Square)
                return last

            cx.on("scalar", rsq)

            def mm_st(t):
                for m in range(8):
                    t.matmul(pb[6][0:1, 0:TB], ones128[:, 0:1],
                             f_rsb[:, m * TB:(m + 1) * TB], start=(m == 0),
                             stop=(m == 7), **MM)
                    last = t.matmul(pb[7][0:1, 0:TB], ones128[:, 0:1],
                                    f_r2[:, m * TB:(m + 1) * TB], start=(m == 0),
                                    stop=(m == 7), **MM)
                return last

            cx.on("tensor", mm_st)
            emit_ln(f_rsb, None, tt)
            pf_st[tt] = cx.dma_async(lambda e, tt=tt: [
                e.dma_start(out=sX3[m * 128:(m + 1) * 128,
                                    tt * TB:(tt + 1) * TB],
                            in_=f_rsb[:, m * TB:(m + 1) * TB])
                for m in range(8)], after=cx.mark())

        ckpt("PF")

        cx.on("vector", lambda v: v.memset(tmp128[:], 0.0), pre_dma=cx.dn)
        ar = Arena()
        WLT = ar.al("wlt", [128, 8 * DM])
        g_w = ar.al("g_w", [128, DM])
        g_i = ar.al("g_i", [128, S], I32)
        g_am = ar.al("g_am", [128, 128])
        g_sc = ar.al("g_sc", [128, 128])
        g_e = ar.al("g_e", [128, 128], I32)
        g_dw = ar.al("g_dw", [128, 32])
        g_a = ar.al("g_a", [128, S])
        g_g2 = ar.al("g_g2", [128, S])
        g_sg = ar.al("g_sg", [128, S])
        g_x3 = ar.al("g_x3", [128, 8 * TB])
        g_xt = ar.al("g_xt", [128, 4 * DM])
        g_rsb = ar.al("g_rsb", [128, 8 * TB])
        g_r2 = ar.al("g_r2", [128, 8 * TB])

        for m in range(8):
            cx.on("sync", dma(g_dw[:, 0:32], dw_l1[m * 128:(m + 1) * 128, :]),
                  dma=True)
            cx.on("scalar", lambda s: s.activation(g_dw[:, 0:32], g_dw[:, 0:32],
                                                   AF.Tanh, scale=4.0))
            cx.on("scalar", lambda s: s.activation(g_dw[:, 0:32], g_dw[:, 0:32],
                                                   AF.Exp, scale=0.5 * LN2))
            cx.on("sync", dma(g_w[:, 0:DM], w_l1[m * 128:(m + 1) * 128, :]),
                  dma=True)
            mxfp4_emit(g_w[:, 0:DM], g_i[:, 0:DM], g_am[:, 0:32], g_sc[:, 0:32],
                       g_e[:, 0:32], g_dw[:, 0:32], g_a[:, 0:DM], g_g2[:, 0:DM],
                       g_sg[:, 0:DM], DM)

            def trw(t):
                for kt in range(8):
                    last = t.transpose(pb[kt][:, 0:128],
                                       g_w[:, kt * 128:kt * 128 + 128], ident[:])
                return last

            def cpw(s, m=m):
                for kt in range(8):
                    last = s.copy(WLT[:, kt * DM + m * 128:kt * DM + (m + 1) * 128],
                                  pb[kt][:, 0:128])
                return last

            cx.on("tensor", trw)
            cx.on("scalar", cpw)

        def tr8(src_t, src_base, dst_t, dst_base, src_stride, dst_stride):
            def tr(t):
                for kt in range(8):
                    last = t.transpose(
                        pb[kt][:, 0:128],
                        src_t[:, src_base + kt * src_stride:
                              src_base + kt * src_stride + 128], ident[:])
                return last

            def cp(s):
                for kt in range(8):
                    last = s.copy(dst_t[:, dst_base + kt * dst_stride:
                                        dst_base + kt * dst_stride + 128],
                                  pb[kt][:, 0:128])
                return last

            cx.on("tensor", tr)
            cx.on("scalar", cp)

        for c in range(NT):
            cx.on("sync", lambda e, c=c: [
                e.dma_start(out=g_x3[:, pt * TB:(pt + 1) * TB],
                            in_=sX3[pt * 128:(pt + 1) * 128, c * TB:(c + 1) * TB])
                for pt in range(8)][-1], dma=True)
            for i in range(4):
                tr8(g_x3, i * 128, g_xt, i * DM, TB, 128)
            mxfp4_emit(g_xt[:, 0:4 * DM], g_i[:, 0:4 * DM], g_am[:, 0:128],
                       g_sc[:, 0:128], g_e[:, 0:128], None, g_a[:, 0:4 * DM],
                       g_g2[:, 0:4 * DM], g_sg[:, 0:4 * DM], 4 * DM)
            for i in range(4):
                tr8(g_xt, i * DM, g_x3, i * 128, 128, TB)

            def mm(t):
                for k in range(8):
                    for m in range(8):
                        last = t.matmul(pb[m][:, 0:TB],
                                        WLT[:, k * DM + m * 128:k * DM + (m + 1) * 128],
                                        g_x3[:, k * TB:(k + 1) * TB],
                                        start=(k == 0), stop=(k == 7), **MM)
                return last

            cx.on("tensor", mm)
            cx.on("sync", lambda e, c=c: [
                e.dma_start(out=g_r2[:, m * TB:(m + 1) * TB],
                            in_=sX3[m * 128:(m + 1) * 128, c * TB:(c + 1) * TB])
                for m in range(8)][-1], dma=True)

            def radd(v):
                for m in range(8):
                    last = v.scalar_tensor_tensor(
                        g_rsb[:, m * TB:(m + 1) * TB], pb[m][:, 0:TB],
                        pvec[:, 384 + m:385 + m], g_r2[:, m * TB:(m + 1) * TB],
                        ALU.add, ALU.add)
                return last

            cx.on("vector", radd)

            def rsq(s):
                for m in range(8):
                    last = s.activation(g_r2[:, m * TB:(m + 1) * TB],
                                        g_rsb[:, m * TB:(m + 1) * TB], AF.Square)
                return last

            cx.on("scalar", rsq)

            def mm_st(t):
                for m in range(8):
                    t.matmul(pb[6][0:1, 0:TB], ones128[:, 0:1],
                             g_rsb[:, m * TB:(m + 1) * TB], start=(m == 0),
                             stop=(m == 7), **MM)
                    last = t.matmul(pb[7][0:1, 0:TB], ones128[:, 0:1],
                                    g_r2[:, m * TB:(m + 1) * TB], start=(m == 0),
                                    stop=(m == 7), **MM)
                return last

            cx.on("tensor", mm_st)
            emit_ln(g_rsb, None, c)
            for i in range(4):
                tr8(g_rsb, i * 128, g_xt, i * DM, TB, 128)
            cx.on("sync", lambda e, c=c: [
                e.dma_start(out=out_d[c * TB + i * 128:c * TB + (i + 1) * 128, :],
                            in_=g_xt[:, i * DM:(i + 1) * DM]) for i in range(4)][-1],
                dma=True)

        ckpt("PG")

        cx.stopped = False
        for name in debug:
            cx.on("sync", dma(dbg[name], dbg_src[name]), dma=True)

        cx.final_wait("sync")

    auto_end = 16512 + (212863 - nc.sbuf_bytes_remaining)
    assert auto_end <= ARENA0, f"auto sbuf end {auto_end} exceeds ARENA0 {ARENA0}"
    return nc, dbg


_BUILT = {}


def _get_built(debug=(), stop_at=None):
    key = (tuple(debug), stop_at)
    if key not in _BUILT:
        _BUILT[key] = build(debug, stop_at)
    return _BUILT[key]


def kernel(**inputs):
    nc, _ = _get_built(DEBUG, STOP_AT)
    consts = _host_constants()
    x = np.ascontiguousarray(np.asarray(inputs["L0_attn_x"], np.float32))
    base = {
        "w_qkv": np.ascontiguousarray(inputs["L0_attn_W_qkv"], np.float32),
        "w_o": np.ascontiguousarray(inputs["L0_attn_W_o"], np.float32),
        "w_gate": np.ascontiguousarray(inputs["L0_ffn_w_gate"], np.float32),
        "w_up": np.ascontiguousarray(inputs["L0_ffn_w_up"], np.float32),
        "w_down": np.ascontiguousarray(inputs["L0_ffn_w_down"], np.float32),
        "w_ip": np.ascontiguousarray(inputs["L1_attn_in_proj_w"], np.float32),
        "conv_w": np.ascontiguousarray(
            np.asarray(inputs["L1_attn_conv_w"], np.float32).reshape(DI, 4)),
        "conv_b": np.ascontiguousarray(
            np.asarray(inputs["L1_attn_conv_b"], np.float32).reshape(DI, 1)),
        "w_xp": np.ascontiguousarray(inputs["L1_attn_x_proj_w"], np.float32),
        "dt_w": np.ascontiguousarray(
            np.asarray(inputs["L1_attn_dt_proj_w"], np.float32).reshape(DI, 1)),
        "dt_b": np.ascontiguousarray(
            np.asarray(inputs["L1_attn_dt_proj_b"], np.float32).reshape(DI, 1)),
        "a_log": np.ascontiguousarray(inputs["L1_attn_A_log"], np.float32),
        "d_vec": np.ascontiguousarray(
            np.asarray(inputs["L1_attn_D"], np.float32).reshape(DI, 1)),
        "w_op": np.ascontiguousarray(inputs["L1_attn_out_proj_w"], np.float32),
        "w_l1": np.ascontiguousarray(inputs["L1_ffn_weight"], np.float32),
        "b_l1": np.ascontiguousarray(
            np.asarray(inputs["L1_ffn_bias"], np.float32).reshape(DM, 1)),
        "dw_l1": np.ascontiguousarray(inputs["L1_ffn_delta_w"], np.float32),
        "da_l1": np.ascontiguousarray(
            np.asarray(inputs["L1_ffn_delta_act"], np.float32).reshape(1, 1)),
    }
    base.update(consts)
    in_maps = []
    for b in range(NCORES):
        m = dict(base)
        m["x"] = np.ascontiguousarray(x[b])
        in_maps.append(m)
    res = run_bass_kernel_spmd(nc, in_maps, core_ids=list(range(NCORES)))
    out = np.stack([np.asarray(res.results[b]["out"]) for b in range(NCORES)],
                   axis=0)
    if DEBUG:
        kernel._dbg = [{k: np.asarray(res.results[b]["dbg_" + k]) for k in DEBUG}
                       for b in range(NCORES)]
    return out


# version 05
# speedup vs baseline: 1.1109x; 1.0463x over previous
import numpy as np
import concourse.bass as bass
import concourse.mybir as mybir
from concourse.bass_utils import run_bass_kernel_spmd
from contextlib import ExitStack

F32 = mybir.dt.float32
I32 = mybir.dt.int32
AF = mybir.ActivationFunctionType
ALU = mybir.AluOpType
AX = mybir.AxisListType

S = 4096
DM = 1024
FF = 3072
DI = 2048
DSN = 16
NH = 16
DK = 64
TB = 512
NT = S // TB
LN_EPS = 1e-5
QMAX0 = 32767.0
LN2 = float(np.log(2.0))
NCORES = 8

DEBUG = ()
STOP_AT = None


class Ctx:

    def __init__(self, nc, es):
        self.nc = nc
        self.ch = es.enter_context(nc.semaphore("chain"))
        self.d2 = es.enter_context(nc.semaphore("dma2"))
        self.t = 0
        self.dn = 0
        self.block = None
        self.stopped = False
        self._eng = None
        self._dma = False
        self._fns = []
        self._pre = None
        self.scope_name = "P0"

    def scope(self, name):
        self._flush()
        self.scope_name = name

    class _SmallGuard:

        SMALL = 512
        _PASS = ("drain", "wait_ge", "nop", "engine_nop")

        def __init__(self, eng):
            self._e = eng

        def __getattr__(self, name):
            f = getattr(self._e, name)
            if name in Ctx._SmallGuard._PASS or not callable(f):
                return f

            def wrap(*a, **kw):
                r = f(*a, **kw)
                out = a[0] if a else kw.get("out")
                fs = None
                try:
                    fs = out.free_size()
                except Exception:
                    pass
                if fs is not None and fs < Ctx._SmallGuard.SMALL:
                    self._e.drain()
                return r

            return wrap

    class _DmaRec:

        def __init__(self, eng):
            self._eng = eng
            self.insts = []

        def dma_start(self, **kw):
            i = self._eng.dma_start(**kw)
            self.insts.append(i)
            return i

        def wait_ge(self, *a):
            return self._eng.wait_ge(*a)

    def _flush(self):
        if not self._fns:
            return
        dec = getattr(self.block, self._eng)
        t0 = self.t
        fns = self._fns
        self._fns = []
        is_dma = self._dma
        eng_name = self._eng
        n = [0]

        pre = self._pre
        self._pre = None
        scname = self.scope_name

        def body(eng):
          with self.nc.named_scope(scname):
            if pre is not None and pre > 0:
                eng.wait_ge(self.d2, pre)
            if t0 > 0:
                eng.wait_ge(self.ch, t0)
            if is_dma:
                rec = Ctx._DmaRec(eng)
                for fn in fns:
                    fn(rec)
                assert rec.insts
                for i in rec.insts:
                    i.then_inc(self.ch, 16)
                n[0] = 16 * len(rec.insts)
            else:
                geng = eng if eng_name in ("tensor", "sync") \
                    else Ctx._SmallGuard(eng)
                for fn in fns:
                    last = fn(geng)
                assert last is not None, "step fn must return its last instruction"
                last.then_inc(self.ch, 1)
                n[0] = 1

        dec(body)
        assert n[0] > 0
        self.t = t0 + n[0]

    def on(self, eng_name, fn, dma=False, force_wait=False, pre_dma=None):
        if self.stopped:
            return
        if pre_dma is None and self._fns and self._eng == eng_name \
                and self._dma == dma and not force_wait:
            self._fns.append(fn)
            return
        self._flush()
        self._eng = eng_name
        self._dma = dma
        self._fns = [fn]
        self._pre = pre_dma

    def dma_async(self, fn, after=None, after_d2=None):
        if self.stopped:
            return self.dn
        self._flush()
        dec = self.block.sync
        n = [0]
        d2 = self.d2

        scname = self.scope_name

        def body(eng):
          with self.nc.named_scope(scname):
            if after_d2 is not None and after_d2 > 0:
                eng.wait_ge(self.d2, after_d2)
            if after is not None and after > 0:
                eng.wait_ge(self.ch, after)
            rec = Ctx._DmaRec(eng)
            fn(rec)
            assert rec.insts
            for i in rec.insts:
                i.then_inc(d2, 16)
            n[0] = 16 * len(rec.insts)

        dec(body)
        assert n[0] > 0
        self.dn += n[0]
        self._eng = None
        return self.dn

    def par(self, specs):
        if self.stopped:
            return
        self._flush()
        t0 = self.t
        scname = self.scope_name
        for spec in specs:
            eng_name, fn = spec[0], spec[1]
            pre = spec[2] if len(spec) > 2 else None
            dec = getattr(self.block, eng_name)

            def body(eng, fn=fn, pre=pre, eng_name=eng_name):
                with self.nc.named_scope(scname):
                    if pre is not None and pre > 0:
                        eng.wait_ge(self.d2, pre)
                    if t0 > 0:
                        eng.wait_ge(self.ch, t0)
                    geng = eng if eng_name in ("tensor", "sync") \
                        else Ctx._SmallGuard(eng)
                    last = fn(geng)
                    assert last is not None
                    last.then_inc(self.ch, 1)

            dec(body)
        self.t = t0 + len(specs)

    def mark(self):
        self._flush()
        return self.t

    def final_wait(self, eng_name):
        self._flush()
        dec = getattr(self.block, eng_name)
        t0 = self.t

        def body(eng):
            eng.wait_ge(self.ch, t0)

        dec(body)


def _host_constants():
    dk = DK
    inv_freq = (np.float32(10000.0) ** (np.arange(0, dk, 2, dtype=np.float32)
                                        / np.float32(dk))).astype(np.float32)
    ang = (np.arange(S, dtype=np.float32)[:, None] / inv_freq[None, :]).astype(np.float32)
    cos = np.cos(ang).astype(np.float32)
    sin = np.sin(ang).astype(np.float32)
    cos64 = np.repeat(cos.T, 2, axis=0)
    sin64 = np.repeat(sin.T, 2, axis=0)
    cos128 = np.concatenate([cos64, cos64], axis=0)
    sgn = np.where(np.arange(64) % 2 == 0, -1.0, 1.0).astype(np.float32)[:, None]
    sin64s = (sin64 * sgn).astype(np.float32)
    sin128 = np.concatenate([sin64s, sin64s], axis=0)
    masks = []
    for r in range(4):
        kk = np.arange(128)[:, None] + 128 * r
        qq = np.arange(TB)[None, :]
        masks.append((kk <= qq).astype(np.float32))
    return {
        "c_cos": np.ascontiguousarray(cos128),
        "c_sin": np.ascontiguousarray(sin128),
        "c_mask": np.ascontiguousarray(np.concatenate(masks, axis=1)),
        "c_ident": np.eye(128, dtype=np.float32),
    }


def build(debug=(), stop_at=None):
    nc = bass.Bass("TRN2", target_bir_lowering=False, detect_race_conditions=False)
    es = ExitStack()

    def din(name, shape):
        return nc.dram_tensor(name, list(shape), F32, kind="ExternalInput").ap()

    x_in = din("x", (S, DM))
    w_qkv = din("w_qkv", (3 * DM, DM))
    w_o = din("w_o", (DM, DM))
    w_gate = din("w_gate", (FF, DM))
    w_up = din("w_up", (FF, DM))
    w_down = din("w_down", (DM, FF))
    w_ip = din("w_ip", (2 * DI, DM))
    conv_w = din("conv_w", (DI, 4))
    conv_b = din("conv_b", (DI, 1))
    w_xp = din("w_xp", (2 * DSN + 1, DI))
    dt_w = din("dt_w", (DI, 1))
    dt_b = din("dt_b", (DI, 1))
    a_log = din("a_log", (DI, DSN))
    d_vec = din("d_vec", (DI, 1))
    w_op = din("w_op", (DM, DI))
    w_l1 = din("w_l1", (DM, DM))
    b_l1 = din("b_l1", (DM, 1))
    dw_l1 = din("dw_l1", (DM, 32))
    da_l1 = din("da_l1", (1, 1))
    c_cos = din("c_cos", (128, S))
    c_sin = din("c_sin", (128, S))
    c_mask = din("c_mask", (128, 4 * TB))
    c_ident = din("c_ident", (128, 128))

    out_d = nc.dram_tensor("out", [S, DM], F32, kind="ExternalOutput").ap()

    def dscr(name, shape):
        return nc.dram_tensor(name, list(shape), F32).ap()

    sXT = dscr("sXT", (DM, S))
    sQ = dscr("sQ", (NH, DK, S))
    sK = dscr("sK", (NH, DK, S))
    sVT = dscr("sVT", (NH, S // 128, 128, DK))
    sX1 = dscr("sX1", (DM, S))
    sXQ = dscr("sXQ", (DM, S))
    sWUT = dscr("sWUT", (DM // 128, 128, FF))
    sWGT = dscr("sWGT", (DM // 128, 128, FF))
    sWDT = dscr("sWDT", (FF // 128, 128, DM))
    sUP = dscr("sUP", (FF, S))
    sGA = dscr("sGA", (FF, S))
    sUPQ = dscr("sUPQ", (FF, S))
    sG1 = dscr("sG1", (FF, S))
    sPROD = dscr("sPROD", (FF, S))
    sX2 = dscr("sX2", (DM, S))
    sWIPT = dscr("sWIPT", (DM // 128, 128, 2 * DI))
    sXC = dscr("sXC", (DI, S))
    sSZ = dscr("sSZ", (DI, S))
    sG = dscr("sG", (DI, S))
    sX3 = dscr("sX3", (DM, S))
    sXP = dscr("sXP", (33, S))
    sATT = dscr("sATT", (DM, S))
    sSC = dscr("sSC", (128, 4 * TB))
    sEX = dscr("sEX", (128, 4 * TB))
    cc_in = nc.dram_tensor("cc_in", [1, 8], F32).ap()
    cc_outs = [nc.dram_tensor(f"cc_out{i}", [1, 8], F32, addr_space="Shared").ap()
               for i in range(6)]

    dbg_shapes = {"XT": (DM, S), "Q": (NH, DK, S), "K": (NH, DK, S),
                  "VT": (NH, S // 128, 128, DK), "X1": (DM, S), "XQ": (DM, S), "ATT": (DM, S), "SC": (128, 4 * TB), "EX": (128, 4 * TB),
                  "UP": (FF, S), "GA": (FF, S), "UPQ": (FF, S), "G1": (FF, S),
                  "PROD": (FF, S), "X2": (DM, S), "XC": (DI, S), "SZ": (DI, S),
                  "G": (DI, S), "X3": (DM, S), "XP": (33, S)}
    dbg_src = {"XT": sXT, "Q": sQ, "K": sK, "VT": sVT, "X1": sX1, "XQ": sXQ, "ATT": sATT, "SC": sSC, "EX": sEX,
               "UP": sUP, "GA": sGA, "UPQ": sUPQ, "G1": sG1, "PROD": sPROD,
               "X2": sX2, "XC": sXC, "SZ": sSZ, "G": sG, "X3": sX3, "XP": sXP}
    dbg = {}
    for name in debug:
        dbg[name] = nc.dram_tensor("dbg_" + name, list(dbg_shapes[name]), F32,
                                   kind="ExternalOutput").ap()

    ident = nc.alloc_sbuf_tensor("ident", [128, 128], F32)
    pvec = nc.alloc_sbuf_tensor("pvec", [128, 512], F32)
    ones128 = nc.alloc_sbuf_tensor("ones128", [128, 1], F32)
    ones1r = nc.alloc_sbuf_tensor("ones1r", [1, 128], F32)
    smax = nc.alloc_sbuf_tensor("smax", [128, 8], F32)
    srow = nc.alloc_sbuf_tensor("srow", [1, 64], F32)
    sscal = nc.alloc_sbuf_tensor("sscal", [128, 16], F32)
    maskt = nc.alloc_sbuf_tensor("maskt", [128, 4 * TB], F32)
    mrow = nc.alloc_sbuf_tensor("mrow", [1, TB], F32)
    vrow = nc.alloc_sbuf_tensor("vrow", [1, TB], F32)
    m2row = nc.alloc_sbuf_tensor("m2row", [1, TB], F32)
    wrow = nc.alloc_sbuf_tensor("wrow", [1, TB], F32)
    trow = nc.alloc_sbuf_tensor("trow", [1, TB], F32)
    tmp128 = nc.alloc_sbuf_tensor("tmp128", [128, 1], F32)
    constv = nc.alloc_sbuf_tensor("constv", [128, 4], F32)
    ones8 = nc.alloc_sbuf_tensor("ones8", [128, 8], F32)
    sscal8 = nc.alloc_sbuf_tensor("sscal8", [128, 16], F32)

    ARENA0 = 42 * 1024
    ARENA_SZ = 183 * 1024

    class Arena:
        def __init__(self):
            self.off = 0

        def al(self, name, shape, dtype=F32):
            nbytes = int(np.prod(shape[1:])) * mybir.dt.size(dtype)
            nbytes = (nbytes + 31) & ~31
            t = nc.alloc_sbuf_tensor_at(name, list(shape), dtype,
                                        offset=ARENA0 + self.off)
            self.off += nbytes
            assert self.off <= ARENA_SZ, f"arena overflow at {name}: {self.off}"
            return t

    pb = [nc.alloc_psum_tensor(f"pb{i}", [128, TB], F32) for i in range(8)]

    cx = Ctx(nc, es)

    def ckpt(name):
        if stop_at == name:
            cx.stopped = True

    def dma(out, in_):
        return lambda e: e.dma_start(out=out, in_=in_)

    MM = dict(skip_group_check=True)

    with nc.Block("main") as blk:
        cx.block = blk

        def p0_loads(sync):
            sync.dma_start(out=ident[:], in_=c_ident[:])
            sync.dma_start(out=maskt[:], in_=c_mask[:])
            for db in range(16):
                sync.dma_start(out=pvec[:, db * 16:(db + 1) * 16],
                               in_=a_log[db * 128:(db + 1) * 128, :])
                sync.dma_start(out=pvec[:, 256 + db:257 + db],
                               in_=d_vec[db * 128:(db + 1) * 128, :])
                sync.dma_start(out=pvec[:, 272 + db * 4:276 + db * 4],
                               in_=conv_w[db * 128:(db + 1) * 128, :])
                sync.dma_start(out=pvec[:, 336 + db:337 + db],
                               in_=dt_w[db * 128:(db + 1) * 128, :])
                sync.dma_start(out=pvec[:, 352 + db:353 + db],
                               in_=dt_b[db * 128:(db + 1) * 128, :])
                sync.dma_start(out=pvec[:, 368 + db:369 + db],
                               in_=conv_b[db * 128:(db + 1) * 128, :])
            for mb in range(8):
                sync.dma_start(out=pvec[:, 384 + mb:385 + mb],
                               in_=b_l1[mb * 128:(mb + 1) * 128, :])
            return sync.dma_start(out=srow[0:1, 8:9], in_=da_l1[:])

        cx.on("sync", p0_loads, dma=True)

        def p0_memset(v):
            v.memset(ones128[:], 1.0)
            v.memset(ones1r[:], 1.0)
            v.memset(smax[:], 0.0)
            v.memset(constv[:, 0:1], LN_EPS)
            v.memset(ones8[:], 1.0)
            return v.memset(constv[:, 1:2], -2.0 * LN2)

        cx.on("vector", p0_memset)
        cx.on("scalar", lambda s: s.activation(pvec[:, 0:256], pvec[:, 0:256],
                                               AF.Exp))
        cx.on("vector", lambda v: v.tensor_scalar_mul(pvec[:, 0:256],
                                                      pvec[:, 0:256], -1.0))
        cx.on("scalar", lambda s: s.activation(srow[0:1, 9:10], srow[0:1, 8:9],
                                               AF.Tanh, scale=4.0))
        cx.on("scalar", lambda s: s.activation(srow[0:1, 9:10], srow[0:1, 9:10],
                                               AF.Exp, scale=0.5 * LN2))
        cx.on("tensor", lambda t: t.matmul(pb[0][:, 0:1], ones1r[:],
                                           srow[0:1, 9:10], start=True, stop=True,
                                           **MM))
        cx.on("scalar", lambda s: s.copy(sscal[:, 10:11], pb[0][:, 0:1]))
        cx.on("gpsimd", lambda g: g.dma_start(out=cc_in[:], in_=smax[0:1, 0:8]),
              dma=True)
        cx.on("gpsimd", lambda g: g.collective_compute(
            "AllReduce", ALU.max, replica_groups=[list(range(NCORES))],
            ins=[cc_in], outs=[cc_outs[5]]), force_wait=True)

        def wt_prep_sbuf(src, n_rows, n_cols, wt, wt_stride, stage):
            nmt, nkt = n_rows // 128, n_cols // 128
            for m in range(nmt):
                cx.on("sync", dma(stage[:, 0:n_cols], src[m * 128:(m + 1) * 128, :]),
                      dma=True)
                for kt0 in range(0, nkt, 8):
                    kts = list(range(kt0, min(kt0 + 8, nkt)))

                    def tr(t, kts=kts):
                        for kt in kts:
                            last = t.transpose(pb[kt % 8][:, 0:128],
                                               stage[:, kt * 128:kt * 128 + 128],
                                               ident[:])
                        return last

                    def cp(s, kts=kts, m=m):
                        for kt in kts:
                            last = s.copy(wt[:, kt * wt_stride + m * 128:
                                             kt * wt_stride + (m + 1) * 128],
                                          pb[kt % 8][:, 0:128])
                        return last

                    cx.on("tensor", tr)
                    cx.on("scalar", cp)

        def emit_ln(rsb_t, store_dst, jj, max_slot=None):
            def rows(v):
                v.tensor_scalar_mul(mrow[:], pb[6][0:1, 0:TB], 1.0 / DM)
                v.tensor_scalar_mul(vrow[:], pb[7][0:1, 0:TB], 1.0 / DM)
                v.tensor_tensor(m2row[:], mrow[:], mrow[:], ALU.mult)
                v.tensor_tensor(vrow[:], vrow[:], m2row[:], ALU.subtract)
                return v.tensor_copy(m2row[:], vrow[:])

            cx.on("vector", rows)
            cx.on("scalar", lambda s: s.activation(vrow[:], vrow[:], AF.Sqrt,
                                                   bias=constv[0:1, 0:1]))

            def rsqrt_ref(v):
                v.tensor_scalar_add(wrow[:], m2row[:], LN_EPS)
                v.reciprocal(vrow[:], vrow[:])
                for _ in range(2):
                    v.tensor_tensor(trow[:], vrow[:], vrow[:], ALU.mult)
                    v.tensor_tensor(trow[:], trow[:], wrow[:], ALU.mult)
                    v.tensor_scalar(trow[:], trow[:], -0.5, 1.5, ALU.mult,
                                    ALU.add)
                    last = v.tensor_tensor(vrow[:], vrow[:], trow[:], ALU.mult)
                return last

            cx.on("vector", rsqrt_ref)

            def bc(t):
                t.matmul(pb[6][:, 0:TB], ones1r[:], mrow[:], start=True, stop=True,
                         **MM)
                return t.matmul(pb[7][:, 0:TB], ones1r[:], vrow[:], start=True,
                                stop=True, **MM)

            cx.on("tensor", bc)

            def apply(v):
                for m in range(8):
                    sl = rsb_t[:, m * TB:(m + 1) * TB]
                    v.tensor_tensor(sl, sl, pb[6][:, 0:TB], ALU.subtract)
                    last = v.tensor_tensor(sl, sl, pb[7][:, 0:TB], ALU.mult)
                    if max_slot is not None:
                        v.tensor_reduce(tmp128[:], sl, AX.X, ALU.max,
                                        apply_absolute_value=True)
                        last = v.tensor_tensor(smax[:, max_slot:max_slot + 1],
                                               smax[:, max_slot:max_slot + 1],
                                               tmp128[:], ALU.max)
                return last

            cx.on("vector", apply)
            if store_dst is not None:
                cx.on("sync", lambda e: [
                    e.dma_start(out=store_dst[m * 128:(m + 1) * 128,
                                              jj * TB:(jj + 1) * TB],
                                in_=rsb_t[:, m * TB:(m + 1) * TB])
                    for m in range(8)][-1], dma=True)

        def exchange(slot, cc_idx, out_col):
            cx.on("tensor", lambda t: t.transpose(
                pb[0][0:1, 0:128], smax[:, slot:slot + 1], ident[:]))
            cx.on("vector", lambda v: v.tensor_reduce(
                srow[0:1, 0:1], pb[0][0:1, 0:128], AX.X, ALU.max))
            cx.on("gpsimd", lambda g: g.dma_start(out=cc_in[0:1, 0:1],
                                                  in_=srow[0:1, 0:1]), dma=True)
            cx.on("gpsimd", lambda g: g.collective_compute(
                "AllReduce", ALU.max, replica_groups=[list(range(NCORES))],
                ins=[cc_in], outs=[cc_outs[cc_idx]]), force_wait=True)
            cx.on("gpsimd", lambda g: g.dma_start(out=srow[0:1, 0:1],
                                                  in_=cc_outs[cc_idx][0:1, 0:1]),
                  dma=True, force_wait=True)

            cx.on("tensor", lambda t: t.matmul(pb[0][:, 0:1], ones1r[:],
                                               srow[0:1, 0:1], start=True,
                                               stop=True, **MM))
            cx.on("scalar", lambda s: s.copy(sscal[:, 14:15], pb[0][:, 0:1]))

            def sc(v):
                v.tensor_scalar(sscal8[:, 0:8], ones8[:], sscal[:, 14:15], None,
                                ALU.mult)
                v.tensor_scalar(sscal8[:, 0:8], sscal8[:, 0:8], 1.0 / QMAX0,
                                1e-12, ALU.mult, ALU.max)
                v.reciprocal(sscal8[:, 8:16], sscal8[:, 0:8])
                v.tensor_copy(sscal[:, out_col:out_col + 1], sscal8[:, 0:1])
                return v.tensor_copy(sscal[:, out_col + 1:out_col + 2],
                                     sscal8[:, 8:9])

            cx.on("vector", sc)

        def mxfp4_emit(wb, ib, amb, scb, eb, dwf_ap, ab, g2b, sgb, ncols):
            nb = ncols // 32
            wv = wb.rearrange("p (b q) -> p b q", q=32)
            amv = amb.rearrange("p (b o) -> p b o", o=1)
            scv = scb.rearrange("p (b o) -> p b o", o=1)

            def step1(v):
                v.tensor_reduce(amv, wv, AX.X, ALU.max, apply_absolute_value=True)
                return v.tensor_single_scalar(amb, amb, 1e-30, ALU.max)

            cx.on("vector", step1)
            cx.on("scalar", lambda s: s.activation(scb, amb, AF.Ln))

            def step2(v):
                v.tensor_scalar(eb, scb, 1.0 / LN2, -0.5, ALU.mult, ALU.add)
                return v.tensor_copy(scb, eb)

            cx.on("vector", step2)
            cx.on("scalar", lambda s: s.activation(scb, scb, AF.Exp, scale=LN2,
                                                   bias=constv[:, 1:2]))
            if dwf_ap is not None:
                cx.on("vector", lambda v: v.tensor_tensor(scb, scb, dwf_ap,
                                                          ALU.mult))
            else:
                cx.on("vector", lambda v: v.tensor_scalar(scb, scb,
                                                          sscal[:, 10:11], None,
                                                          ALU.mult))

            def step3a(v):
                v.reciprocal(amb, scb)
                return v.tensor_tensor(wv, wv, amv.broadcast_to([128, nb, 32]),
                                       ALU.mult)

            cx.on("vector", step3a)
            cx.on("scalar", lambda s: s.activation(ab, wb, AF.Abs))

            def step3(v):
                v.tensor_single_scalar(ab, ab, 6.0, ALU.min)
                v.tensor_single_scalar(g2b, ab, 2.0, ALU.is_ge)
                v.tensor_single_scalar(sgb, ab, 4.0, ALU.is_ge)
                v.scalar_tensor_tensor(g2b, sgb, 0.5, g2b, ALU.mult, ALU.add)
                v.tensor_scalar(g2b, g2b, -1.0, 2.0, ALU.mult, ALU.add)
                v.tensor_tensor(ib, ab, g2b, ALU.mult)
                v.reciprocal(g2b, g2b)
                v.tensor_tensor(ab, ib, g2b, ALU.mult)
                return v.tensor_single_scalar(sgb, wb, 0.0, ALU.is_ge)

            cx.on("vector", step3)

            def step4(v):
                v.tensor_scalar(sgb, sgb, 2.0, -1.0, ALU.mult, ALU.add)
                av = ab.rearrange("p (b q) -> p b q", q=32)
                v.tensor_tensor(ab, ab, sgb, ALU.mult)
                return v.tensor_tensor(wv, av, scv.broadcast_to([128, nb, 32]),
                                       ALU.mult)

            cx.on("vector", step4)

        cx.scope("PA")
        ar = Arena()
        WQKVT = ar.al("wqkvt", [128, 8 * 3 * DM])
        pa_stage = ar.al("pa_stage", [128, DM])
        pa_xtc = ar.al("pa_xtc", [128, 8 * TB])
        pa_cos = ar.al("pa_cos", [128, S])
        pa_sin = ar.al("pa_sin", [128, S])
        pa_ev = ar.al("pa_ev", [128, TB])
        pa_t1 = ar.al("pa_t1", [128, TB])
        pa_t2 = ar.al("pa_t2", [128, TB])
        pa_vt = ar.al("pa_vt", [128, 512])

        cx.on("sync", lambda e: (e.dma_start(out=pa_cos[:], in_=c_cos[:]),
                                 e.dma_start(out=pa_sin[:], in_=c_sin[:]))[-1],
              dma=True)
        wt_prep_sbuf(w_qkv, 3 * DM, DM, WQKVT, 3 * DM, pa_stage)

        pair_mask = [(i // 2) * 2 + (1 - i % 2) for i in range(32)]
        for c in range(NT):
            for i in range(4):
                cx.on("sync", dma(pa_stage[:, 0:DM],
                                  x_in[c * TB + i * 128:c * TB + (i + 1) * 128, :]),
                      dma=True)

                def trx(t):
                    for kt in range(8):
                        last = t.transpose(pb[kt][:, 0:128],
                                           pa_stage[:, kt * 128:kt * 128 + 128],
                                           ident[:])
                    return last

                def cpx(s, i=i):
                    for kt in range(8):
                        last = s.copy(
                            pa_xtc[:, kt * TB + i * 128:kt * TB + (i + 1) * 128],
                            pb[kt][:, 0:128])
                    return last

                cx.on("tensor", trx)
                cx.on("scalar", cpx)
            cx.on("sync", lambda e, c=c: [
                e.dma_start(out=sXT[pt * 128:(pt + 1) * 128, c * TB:(c + 1) * TB],
                            in_=pa_xtc[:, pt * TB:(pt + 1) * TB])
                for pt in range(8)][-1], dma=True)
            for m in range(24):
                def mm(t, m=m):
                    for k in range(8):
                        last = t.matmul(
                            pb[m % 2][:, 0:TB],
                            WQKVT[:, k * 3 * DM + m * 128:k * 3 * DM + (m + 1) * 128],
                            pa_xtc[:, k * TB:(k + 1) * TB],
                            start=(k == 0), stop=(k == 7), **MM)
                    return last

                cx.on("tensor", mm)
                cx.on("scalar", lambda s, m=m: s.copy(pa_ev[:], pb[m % 2][:, 0:TB]))
                if m < 16:
                    def rope(v, c=c):
                        v.tensor_tensor(pa_t1[:], pa_ev[:],
                                        pa_cos[:, c * TB:(c + 1) * TB], ALU.mult)
                        v.stream_shuffle(pa_t2[:], pa_ev[:], pair_mask)
                        v.tensor_tensor(pa_t2[:], pa_t2[:],
                                        pa_sin[:, c * TB:(c + 1) * TB], ALU.mult)
                        return v.tensor_add(pa_ev[:], pa_t1[:], pa_t2[:])

                    cx.on("vector", rope)
                    qk = sQ if m < 8 else sK
                    hm = (m % 8) * 2
                    cx.on("sync", lambda e, qk=qk, hm=hm, c=c: (
                        e.dma_start(out=qk[hm, :, c * TB:(c + 1) * TB],
                                    in_=pa_ev[0:64, :]),
                        e.dma_start(out=qk[hm + 1, :, c * TB:(c + 1) * TB],
                                    in_=pa_ev[64:128, :]))[-1], dma=True)
                else:
                    h2 = (m - 16) * 2

                    def trv(t):
                        for i in range(4):
                            last = t.transpose(pb[4 + i][:, 0:128],
                                               pa_ev[:, i * 128:(i + 1) * 128],
                                               ident[:])
                        return last

                    def cpv(s):
                        for i in range(4):
                            last = s.copy(pa_vt[:, i * 128:(i + 1) * 128],
                                          pb[4 + i][:, 0:128])
                        return last

                    cx.on("tensor", trv)
                    cx.on("scalar", cpv)

                    def stv(e, h2=h2, c=c):
                        for i in range(4):
                            kcg = c * 4 + i
                            e.dma_start(out=sVT[h2, kcg],
                                        in_=pa_vt[:, i * 128:i * 128 + 64])
                            last = e.dma_start(out=sVT[h2 + 1, kcg],
                                               in_=pa_vt[:, i * 128 + 64:
                                                         (i + 1) * 128])
                        return last

                    cx.on("sync", stv, dma=True)

        ckpt("PA")

        cx.scope("PB")
        ar = Arena()
        WOT = ar.al("wot", [128, 8 * DM])
        pb_stage = ar.al("pb_stage", [128, DM])
        kbuf = ar.al("kbuf", [64, 2 * S])
        qbuf = ar.al("qbuf", [64, 2 * TB])
        vtb = ar.al("vtb", [128, 2 * 32 * 65])
        expT = ar.al("expT", [128, 4 * TB])
        attn_sb = ar.al("attn_sb", [128, 8 * TB])
        rsb = ar.al("rsb", [128, 8 * TB])
        r2t8 = ar.al("r2t8", [128, 8 * TB])
        xres8 = ar.al("xres8", [128, 8 * TB])
        rbc64 = ar.al("rbc64", [64, TB])

        wt_prep_sbuf(w_o, DM, DM, WOT, DM, pb_stage)
        cx.on("vector", lambda v: v.memset(vtb[:], 1.0))

        NPair = NT * NH
        hld = [0] * NPair
        xrt = [0] * NT
        radd_mark = [0] * NT

        def hload(e, i):
            j, h = i // NH, i % NH
            nk = 4 * (j + 1)
            kb = i % 2
            e.dma_start(out=qbuf[:, kb * TB:kb * TB + TB],
                        in_=sQ[h, :, j * TB:(j + 1) * TB])
            e.dma_start(out=kbuf[:, kb * S:kb * S + nk * 128],
                        in_=sK[h, :, 0:nk * 128])
            for kc0 in range(0, nk, 8):
                kn = min(8, nk - kc0)
                e.dma_start(
                    out=vtb.ap().rearrange("p (b kc c) -> p b kc c", b=2, c=65)
                    [:, kb, kc0:kc0 + kn, 0:64],
                    in_=sVT[h, kc0:kc0 + kn].rearrange("kc p c -> p kc c"))

        hld[0] = cx.dma_async(lambda e: hload(e, 0), after=cx.mark())
        hld[1] = cx.dma_async(lambda e: hload(e, 1))
        for j in range(NT):
            nk = 4 * (j + 1)
            if j >= 1:
                xrt[j] = cx.dma_async(lambda e, j=j: [
                    e.dma_start(out=xres8[:, m * TB:(m + 1) * TB],
                                in_=sXT[m * 128:(m + 1) * 128,
                                        j * TB:(j + 1) * TB])
                    for m in range(8)], after=radd_mark[j - 1])
            for h in range(NH):
                i = j * NH + h
                ko, qo, vo = (i % 2) * S, (i % 2) * TB, (i % 2) * 32 * 65
                for kc0 in range(0, nk, 4):
                    def mm_s(t, kc0=kc0, ko=ko, qo=qo):
                        for ii in range(4):
                            last = t.matmul(
                                pb[ii][:, 0:TB],
                                kbuf[:, ko + (kc0 + ii) * 128:
                                     ko + (kc0 + ii + 1) * 128],
                                qbuf[:, qo:qo + TB], start=True, stop=True,
                                **MM)
                        return last

                    cx.on("tensor", mm_s,
                          pre_dma=hld[i] if kc0 == 0 else None)

                    def ex(s):
                        for ii in range(4):
                            last = s.activation(expT[:, ii * TB:(ii + 1) * TB],
                                                pb[ii][:, 0:TB], AF.Exp,
                                                scale=0.125)
                        return last

                    cx.on("scalar", ex)
                    if kc0 == 4 * j:
                        cx.on("vector", lambda v: v.tensor_tensor(
                            expT[:], expT[:], maskt[:], ALU.mult))

                    def mm_av(t, kc0=kc0, nk=nk, vo=vo):
                        for ii in range(4):
                            last = t.matmul(
                                pb[4][0:65, 0:TB],
                                vtb[:, vo + (kc0 + ii) * 65:
                                    vo + (kc0 + ii + 1) * 65],
                                expT[:, ii * TB:(ii + 1) * TB],
                                start=(kc0 + ii == 0),
                                stop=(kc0 + ii == nk - 1), **MM)
                        return last

                    cx.on("tensor", mm_av)
                if i + 2 < NPair:
                    hld[i + 2] = cx.dma_async(
                        lambda e, i=i: hload(e, i + 2), after=cx.mark())
                cx.on("vector", lambda v: v.reciprocal(mrow[:], pb[4][64:65, 0:TB]))
                cx.on("tensor", lambda t: t.matmul(pb[5][0:64, 0:TB],
                                                   ones1r[0:1, 0:64], mrow[:],
                                                   start=True, stop=True, **MM))
                cx.on("scalar", lambda s: s.copy(rbc64[:], pb[5][0:64, 0:TB]))
                pt, sub = h // 2, (h % 2) * 64
                cx.on("vector", lambda v, pt=pt, sub=sub: v.tensor_tensor(
                    attn_sb[sub:sub + 64, pt * TB:(pt + 1) * TB],
                    pb[4][0:64, 0:TB], rbc64[:], ALU.mult))
            if "ATT" in debug:
                cx.on("sync", lambda e, j=j: [
                    e.dma_start(out=sATT[pt * 128:(pt + 1) * 128,
                                         j * TB:(j + 1) * TB],
                                in_=attn_sb[:, pt * TB:(pt + 1) * TB])
                    for pt in range(8)], dma=True)
            if j == 0:
                cx.on("sync", lambda e: [
                    e.dma_start(out=xres8[:, m * TB:(m + 1) * TB],
                                in_=sXT[m * 128:(m + 1) * 128, 0:TB])
                    for m in range(8)][-1], dma=True)

            def mm_o(t):
                for m in range(8):
                    for k in range(8):
                        last = t.matmul(
                            pb[m][:, 0:TB],
                            WOT[:, k * DM + m * 128:k * DM + (m + 1) * 128],
                            attn_sb[:, k * TB:(k + 1) * TB],
                            start=(k == 0), stop=(k == 7), **MM)
                return last

            cx.on("tensor", mm_o)

            def radd(v):
                for m in range(8):
                    last = v.tensor_add(rsb[:, m * TB:(m + 1) * TB],
                                        pb[m][:, 0:TB],
                                        xres8[:, m * TB:(m + 1) * TB])
                return last

            cx.on("vector", radd, pre_dma=xrt[j] if j >= 1 else None)
            radd_mark[j] = cx.mark()

            def rsq(s):
                for m in range(8):
                    last = s.activation(r2t8[:, m * TB:(m + 1) * TB],
                                        rsb[:, m * TB:(m + 1) * TB], AF.Square)
                return last

            cx.on("scalar", rsq)

            def mm_st(t):
                for m in range(8):
                    t.matmul(pb[6][0:1, 0:TB], ones128[:, 0:1],
                             rsb[:, m * TB:(m + 1) * TB], start=(m == 0),
                             stop=(m == 7), **MM)
                    last = t.matmul(pb[7][0:1, 0:TB], ones128[:, 0:1],
                                    r2t8[:, m * TB:(m + 1) * TB], start=(m == 0),
                                    stop=(m == 7), **MM)
                return last

            cx.on("tensor", mm_st)
            emit_ln(rsb, sX1, j, max_slot=0)

        ckpt("PB")

        cx.scope("PC0")
        exchange(0, 0, 0)
        ar = Arena()
        c0_x = ar.al("c0_x", [128, 2 * S])
        c0_i = ar.al("c0_i", [128, S], I32)
        ld_t = [0] * 8
        qmark = [0] * 8
        pc0_st = [0] * 8
        ld_t[0] = cx.dma_async(lambda e: e.dma_start(
            out=c0_x[:, 0:S], in_=sX1[0:128, :]), after=cx.mark())
        for pt in range(8):
            if pt + 1 < 8:
                ld_t[pt + 1] = cx.dma_async(
                    lambda e, pt=pt: e.dma_start(
                        out=c0_x[:, ((pt + 1) % 2) * S:((pt + 1) % 2) * S + S],
                        in_=sX1[(pt + 1) * 128:(pt + 2) * 128, :]),
                    after=qmark[pt - 1] if pt >= 1 else None,
                    after_d2=pc0_st[pt - 1] if pt >= 1 else None)

            def qdq(v, pt=pt):
                b = (pt % 2) * S
                v.tensor_scalar(c0_i[:], c0_x[:, b:b + S], sscal[:, 1:2], None,
                                ALU.mult)
                return v.tensor_scalar(c0_x[:, b:b + S], c0_i[:],
                                       sscal[:, 0:1], None, ALU.mult)

            cx.on("vector", qdq, pre_dma=ld_t[pt])
            qmark[pt] = cx.mark()
            pc0_st[pt] = cx.dma_async(lambda e, pt=pt: e.dma_start(
                out=sXQ[pt * 128:(pt + 1) * 128, :],
                in_=c0_x[:, (pt % 2) * S:(pt % 2) * S + S]), after=qmark[pt])

        ckpt("PC0")

        cx.scope("PC1")
        cx.on("vector", lambda v: v.memset(tmp128[:], 0.0), pre_dma=cx.dn)
        ar = Arena()
        p1_w = ar.al("p1_w", [128, DM])
        p1_i = ar.al("p1_i", [128, DM], I32)
        p1_am = ar.al("p1_am", [128, 32])
        p1_rs = ar.al("p1_rs", [128, 32])
        p1_wt = ar.al("p1_wt", [128, 8 * 128])
        xq_sb = ar.al("xq_sb", [128, 8 * S])
        p1_st = ar.al("p1_st", [128, 2 * 8 * 128])
        p1_ev = ar.al("p1_ev", [128, 2 * S])

        def lpbq_quant(v, wt_, it_, amt, rst, nb):
            wv = wt_.rearrange("p (b q) -> p b q", q=32)
            iv = it_.rearrange("p (b q) -> p b q", q=32)
            amv = amt.rearrange("p (b o) -> p b o", o=1)
            rsv = rst.rearrange("p (b o) -> p b o", o=1)
            v.tensor_reduce(amv, wv, AX.X, ALU.max, apply_absolute_value=True)
            v.tensor_scalar(amt, amt, 1.0 / 7.0, 1e-12, ALU.mult, ALU.max)
            v.reciprocal(rst, amt)
            v.tensor_tensor(iv, wv, rsv.broadcast_to([128, nb, 32]), ALU.mult)
            return v.tensor_tensor(wv, iv, amv.broadcast_to([128, nb, 32]),
                                   ALU.mult)

        def lpbq_prep(src, dstT, n_rows, n_cols, wbuf, ibuf, ambuf, rsbuf, wtbuf):
            nmt, nkt, nb = n_rows // 128, n_cols // 128, n_cols // 32
            for m in range(nmt):
                cx.on("sync", dma(wbuf[:, 0:n_cols], src[m * 128:(m + 1) * 128, :]),
                      dma=True)
                cx.on("vector", lambda v, nb=nb, ncc=n_cols: lpbq_quant(
                    v, wbuf[:, 0:ncc], ibuf[:, 0:ncc], ambuf[:, 0:nb],
                    rsbuf[:, 0:nb], nb))
                for kt0 in range(0, nkt, 8):
                    kts = list(range(kt0, min(kt0 + 8, nkt)))

                    def tr(t, kts=kts):
                        for kt in kts:
                            last = t.transpose(pb[kt % 8][:, 0:128],
                                               wbuf[:, kt * 128:kt * 128 + 128],
                                               ident[:])
                        return last

                    def cp(s, kts=kts):
                        for kt in kts:
                            last = s.copy(wtbuf[:, kt * 128:(kt + 1) * 128],
                                          pb[kt % 8][:, 0:128])
                        return last

                    cx.on("tensor", tr)
                    cx.on("scalar", cp)
                cx.on("sync", lambda e, m=m, nkt=nkt, dstT=dstT, wtbuf=wtbuf: [
                    e.dma_start(out=dstT[kt, :, m * 128:(m + 1) * 128],
                                in_=wtbuf[:, kt * 128:(kt + 1) * 128])
                    for kt in range(nkt)][-1], dma=True)

        lpbq_prep(w_up, sWUT, FF, DM, p1_w, p1_i, p1_am, p1_rs, p1_wt)
        lpbq_prep(w_gate, sWGT, FF, DM, p1_w, p1_i, p1_am, p1_rs, p1_wt)

        cx.on("sync", lambda e: [
            e.dma_start(out=xq_sb[:, pt * S:(pt + 1) * S],
                        in_=sXQ[pt * 128:(pt + 1) * 128, :]) for pt in range(8)][-1],
            dma=True)

        def big_mm(dstD, srcWT, max_slot, carry=None):
            NM = FF // 128
            ld_t = [0] * NM
            mm_mark = [0] * NM
            ev_mark = [0] * NM
            st_t = [0] * NM
            prev_mm, prev_st = carry if carry else (None, [None, None])

            def ld(e, m, srcWT=srcWT):
                b = (m % 2) * 1024
                for k in range(8):
                    e.dma_start(out=p1_st[:, b + k * 128:b + (k + 1) * 128],
                                in_=srcWT[k, :, m * 128:(m + 1) * 128])

            ld_t[0] = cx.dma_async(lambda e: ld(e, 0), after=prev_mm)
            for m in range(NM):
                if m + 1 < NM:
                    anchor = mm_mark[m - 1] if m >= 1 else prev_mm
                    ld_t[m + 1] = cx.dma_async(
                        lambda e, m=m: ld(e, m + 1), after=anchor)

                def mm(t, m=m):
                    b = (m % 2) * 1024
                    for k in range(8):
                        for tt in range(8):
                            last = t.matmul(pb[tt][:, 0:TB],
                                            p1_st[:, b + k * 128:b + (k + 1) * 128],
                                            xq_sb[:, k * S + tt * TB:
                                                  k * S + (tt + 1) * TB],
                                            start=(k == 0), stop=(k == 7), **MM)
                    return last

                cx.on("tensor", mm, pre_dma=ld_t[m])
                mm_mark[m] = cx.mark()

                def evac(s, m=m):
                    b = (m % 2) * S
                    for tt in range(8):
                        last = s.copy(p1_ev[:, b + tt * TB:b + (tt + 1) * TB],
                                      pb[tt][:, 0:TB])
                    return last

                cx.on("scalar", evac,
                      pre_dma=st_t[m - 2] if m >= 2 else prev_st[m])

                def mx(v, m=m):
                    b = (m % 2) * S
                    v.tensor_reduce(tmp128[:], p1_ev[:, b:b + S], AX.X, ALU.max,
                                    apply_absolute_value=True)
                    return v.tensor_tensor(smax[:, max_slot:max_slot + 1],
                                           smax[:, max_slot:max_slot + 1],
                                           tmp128[:], ALU.max)

                cx.on("vector", mx)
                ev_mark[m] = cx.mark()
                st_t[m] = cx.dma_async(
                    lambda e, m=m: e.dma_start(
                        out=dstD[m * 128:(m + 1) * 128, :],
                        in_=p1_ev[:, (m % 2) * S:(m % 2) * S + S]),
                    after=ev_mark[m])
            return (mm_mark[NM - 1], [st_t[NM - 2], st_t[NM - 1]])

        up_tail = big_mm(sUP, sWUT, 1)
        big_mm(sGA, sWGT, 2, carry=up_tail)

        ckpt("PC1")

        cx.scope("PC2")
        exchange(1, 1, 2)
        exchange(2, 2, 4)
        ar = Arena()
        c2_u = ar.al("c2_u", [128, 2 * S])
        c2_g = ar.al("c2_g", [128, 2 * S])
        c2_i = ar.al("c2_i", [128, S], I32)
        c2_s = ar.al("c2_s", [128, S])
        NM2 = FF // 128
        ld_t = [0] * NM2
        q3m = [0] * NM2
        stg1 = [0] * NM2

        def ld2(e, mt):
            b = (mt % 2) * S
            e.dma_start(out=c2_u[:, b:b + S], in_=sUP[mt * 128:(mt + 1) * 128, :])
            e.dma_start(out=c2_g[:, b:b + S], in_=sGA[mt * 128:(mt + 1) * 128, :])

        ld_t[0] = cx.dma_async(lambda e: ld2(e, 0), after_d2=cx.dn)
        for mt in range(NM2):
            b = (mt % 2) * S
            if mt + 1 < NM2:
                ld_t[mt + 1] = cx.dma_async(
                    lambda e, mt=mt: ld2(e, mt + 1),
                    after=q3m[mt - 1] if mt >= 1 else None,
                    after_d2=stg1[mt - 1] if mt >= 1 else None)
            cu, cg = c2_u[:, b:b + S], c2_g[:, b:b + S]

            def q2(v, cu=cu, cg=cg):
                v.tensor_scalar(c2_i[:], cu, sscal[:, 3:4], None, ALU.mult)
                v.tensor_scalar(cu, c2_i[:], sscal[:, 2:3], None, ALU.mult)
                v.tensor_scalar(c2_i[:], cg, sscal[:, 5:6], None, ALU.mult)
                return v.tensor_scalar(cg, c2_i[:], sscal[:, 4:5], None, ALU.mult)

            cx.on("vector", q2, pre_dma=ld_t[mt])
            q2m = cx.mark()
            cx.dma_async(lambda e, mt=mt, cu=cu: e.dma_start(
                out=sUPQ[mt * 128:(mt + 1) * 128, :], in_=cu), after=q2m)
            cx.on("scalar", lambda s, cg=cg: s.activation(c2_s[:], cg, AF.Sigmoid))

            def q3(v, cg=cg):
                v.tensor_scalar(c2_i[:], c2_s[:], 65536.0, None, ALU.mult)
                v.tensor_scalar(c2_s[:], c2_i[:], 65535.0, 1.0 / 65536.0,
                                ALU.min, ALU.mult)
                v.tensor_tensor(cg, cg, c2_s[:], ALU.mult)
                v.tensor_reduce(tmp128[:], cg, AX.X, ALU.max,
                                apply_absolute_value=True)
                return v.tensor_tensor(smax[:, 3:4], smax[:, 3:4], tmp128[:],
                                       ALU.max)

            cx.on("vector", q3)
            q3m[mt] = cx.mark()
            stg1[mt] = cx.dma_async(lambda e, mt=mt, cg=cg: e.dma_start(
                out=sG1[mt * 128:(mt + 1) * 128, :], in_=cg), after=q3m[mt])

        ckpt("PC2")

        cx.scope("PC3")
        exchange(3, 3, 6)
        ld_t = [0] * NM2
        q4m = [0] * NM2
        stpr = [0] * NM2

        def ld3(e, mt):
            b = (mt % 2) * S
            e.dma_start(out=c2_g[:, b:b + S], in_=sG1[mt * 128:(mt + 1) * 128, :])
            e.dma_start(out=c2_u[:, b:b + S], in_=sUPQ[mt * 128:(mt + 1) * 128, :])

        ld_t[0] = cx.dma_async(lambda e: ld3(e, 0), after_d2=cx.dn)
        for mt in range(NM2):
            b = (mt % 2) * S
            if mt + 1 < NM2:
                ld_t[mt + 1] = cx.dma_async(
                    lambda e, mt=mt: ld3(e, mt + 1),
                    after=q4m[mt - 1] if mt >= 1 else None,
                    after_d2=stpr[mt - 1] if mt >= 1 else None)
            cu, cg = c2_u[:, b:b + S], c2_g[:, b:b + S]

            def q4(v, cu=cu, cg=cg):
                v.tensor_scalar(c2_i[:], cg, sscal[:, 7:8], None, ALU.mult)
                v.tensor_scalar(cg, c2_i[:], sscal[:, 6:7], None, ALU.mult)
                v.tensor_tensor(cg, cg, cu, ALU.mult)
                v.tensor_reduce(tmp128[:], cg, AX.X, ALU.max,
                                apply_absolute_value=True)
                return v.tensor_tensor(smax[:, 4:5], smax[:, 4:5], tmp128[:],
                                       ALU.max)

            cx.on("vector", q4, pre_dma=ld_t[mt])
            q4m[mt] = cx.mark()
            stpr[mt] = cx.dma_async(lambda e, mt=mt, cg=cg: e.dma_start(
                out=sPROD[mt * 128:(mt + 1) * 128, :], in_=cg), after=q4m[mt])

        ckpt("PC3")

        cx.scope("PC4")
        exchange(4, 4, 8)
        cx.on("vector", lambda v: v.memset(tmp128[:], 0.0), pre_dma=cx.dn)
        ar = Arena()
        c4_w = ar.al("c4_w", [128, FF])
        c4_i = ar.al("c4_i", [128, FF], I32)
        c4_am = ar.al("c4_am", [128, 96])
        c4_rs = ar.al("c4_rs", [128, 96])
        c4_wt = ar.al("c4_wt", [128, 24 * 128])
        c4_st = ar.al("c4_st", [128, 8 * DM])
        c4_pq = ar.al("c4_pq", [128, 8 * TB])
        c4_oi = ar.al("c4_oi", [128, TB], I32)
        c4_rsb = ar.al("c4_rsb", [128, 8 * TB])
        c4_r2 = ar.al("c4_r2", [128, 8 * TB])
        c4_xr = ar.al("c4_xr", [128, 8 * TB])

        lpbq_prep(w_down, sWDT, DM, FF, c4_w, c4_i, c4_am, c4_rs, c4_wt)

        for tt in range(NT):
            for kg in range(3):
                def ld(e, tt=tt, kg=kg):
                    for kk in range(8):
                        k = kg * 8 + kk
                        e.dma_start(out=c4_st[:, kk * DM:(kk + 1) * DM],
                                    in_=sWDT[k, :, :])
                        last = e.dma_start(
                            out=c4_pq[:, kk * TB:(kk + 1) * TB],
                            in_=sPROD[k * 128:(k + 1) * 128,
                                      tt * TB:(tt + 1) * TB])
                    return last

                cx.on("sync", ld, dma=True)

                def oq(v):
                    for kk in range(8):
                        sl = c4_pq[:, kk * TB:(kk + 1) * TB]
                        v.tensor_scalar(c4_oi[:], sl, sscal[:, 9:10], None,
                                        ALU.mult)
                        last = v.tensor_scalar(sl, c4_oi[:], sscal[:, 8:9], None,
                                               ALU.mult)
                    return last

                cx.on("vector", oq)

                def mm(t, kg=kg):
                    for kk in range(8):
                        k = kg * 8 + kk
                        for m in range(8):
                            last = t.matmul(
                                pb[m][:, 0:TB],
                                c4_st[:, kk * DM + m * 128:kk * DM + (m + 1) * 128],
                                c4_pq[:, kk * TB:(kk + 1) * TB],
                                start=(k == 0), stop=(k == 23), **MM)
                    return last

                cx.on("tensor", mm)
            cx.on("sync", lambda e, tt=tt: [
                e.dma_start(out=c4_xr[:, m * TB:(m + 1) * TB],
                            in_=sX1[m * 128:(m + 1) * 128, tt * TB:(tt + 1) * TB])
                for m in range(8)][-1], dma=True)

            def radd(v):
                for m in range(8):
                    last = v.tensor_add(c4_rsb[:, m * TB:(m + 1) * TB],
                                        pb[m][:, 0:TB],
                                        c4_xr[:, m * TB:(m + 1) * TB])
                return last

            cx.on("vector", radd)

            def rsq(s):
                for m in range(8):
                    last = s.activation(c4_r2[:, m * TB:(m + 1) * TB],
                                        c4_rsb[:, m * TB:(m + 1) * TB], AF.Square)
                return last

            cx.on("scalar", rsq)

            def mm_st(t):
                for m in range(8):
                    t.matmul(pb[6][0:1, 0:TB], ones128[:, 0:1],
                             c4_rsb[:, m * TB:(m + 1) * TB], start=(m == 0),
                             stop=(m == 7), **MM)
                    last = t.matmul(pb[7][0:1, 0:TB], ones128[:, 0:1],
                                    c4_r2[:, m * TB:(m + 1) * TB], start=(m == 0),
                                    stop=(m == 7), **MM)
                return last

            cx.on("tensor", mm_st)
            emit_ln(c4_rsb, sX2, tt)

        ckpt("PC4")

        cx.scope("PD")
        ar = Arena()
        d_wst = ar.al("d_wst", [128, DM])
        d_wt = ar.al("d_wt", [128, 8 * 128])
        x2_sb = ar.al("x2_sb", [128, 8 * S])
        d_st = ar.al("d_st", [128, 2 * 8 * 128])
        d_cb = ar.al("d_cb", [128, S + 3])
        d_acc = ar.al("d_acc", [128, S])
        xpt = ar.al("xpt", [128, 16 * 33])
        xps = ar.al("xps", [33, TB])

        for m in range(2 * DI // 128):
            cx.on("sync", dma(d_wst[:, 0:DM], w_ip[m * 128:(m + 1) * 128, :]),
                  dma=True)
            for kt0 in range(0, 8, 4):
                kts = list(range(kt0, kt0 + 4))

                def tr(t, kts=kts):
                    for kt in kts:
                        last = t.transpose(pb[4 + kt % 4][:, 0:128],
                                           d_wst[:, kt * 128:kt * 128 + 128],
                                           ident[:])
                    return last

                def cp(s, kts=kts):
                    for kt in kts:
                        last = s.copy(d_wt[:, kt * 128:(kt + 1) * 128],
                                      pb[4 + kt % 4][:, 0:128])
                    return last

                cx.on("tensor", tr)
                cx.on("scalar", cp)
            cx.on("sync", lambda e, m=m: [
                e.dma_start(out=sWIPT[kt, :, m * 128:(m + 1) * 128],
                            in_=d_wt[:, kt * 128:(kt + 1) * 128])
                for kt in range(8)][-1], dma=True)

        cx.on("sync", lambda e: [
            e.dma_start(out=x2_sb[:, pt * S:(pt + 1) * S],
                        in_=sX2[pt * 128:(pt + 1) * 128, :]) for pt in range(8)][-1],
            dma=True)

        NMD = 2 * DI // 128
        ld_t = [0] * NMD
        mm_mark = [0] * NMD
        st_t = [0] * NMD

        def ldst(e, m):
            b = (m % 2) * 1024
            for k in range(8):
                e.dma_start(out=d_st[:, b + k * 128:b + (k + 1) * 128],
                            in_=sWIPT[k, :, m * 128:(m + 1) * 128])

        ld_t[0] = cx.dma_async(lambda e: ldst(e, 0))
        for m in range(NMD):
            if m + 1 < NMD:
                ld_t[m + 1] = cx.dma_async(
                    lambda e, m=m: ldst(e, m + 1),
                    after=mm_mark[m - 1] if m >= 1 else None)

            def mm(t, m=m):
                b = (m % 2) * 1024
                for k in range(8):
                    for tt in range(8):
                        last = t.matmul(pb[tt][:, 0:TB],
                                        d_st[:, b + k * 128:b + (k + 1) * 128],
                                        x2_sb[:, k * S + tt * TB:k * S + (tt + 1) * TB],
                                        start=(k == 0), stop=(k == 7), **MM)
                return last

            cx.on("tensor", mm, pre_dma=ld_t[m])
            mm_mark[m] = cx.mark()
            if m < 16:
                db = m

                def evac(s):
                    for tt in range(8):
                        last = s.copy(d_cb[:, 3 + tt * TB:3 + (tt + 1) * TB],
                                      pb[tt][:, 0:TB])
                    return last

                cx.on("scalar", evac,
                      pre_dma=st_t[m - 1] if m >= 1 else None)

                def conv(v, db=db):
                    v.memset(d_cb[:, 0:3], 0.0)
                    v.tensor_scalar(d_acc[:], d_cb[:, 0:S],
                                    pvec[:, 272 + db * 4:273 + db * 4], None,
                                    ALU.mult)
                    for kk in range(1, 4):
                        last = v.scalar_tensor_tensor(
                            d_acc[:], d_cb[:, kk:kk + S],
                            pvec[:, 272 + db * 4 + kk:273 + db * 4 + kk],
                            d_acc[:], ALU.mult, ALU.add)
                    return last

                cx.on("vector", conv)
                cx.on("scalar", lambda s, db=db: s.activation(
                    d_cb[:, 0:S], d_acc[:], AF.Silu,
                    bias=pvec[:, 368 + db:369 + db]))
                st_t[m] = cx.dma_async(lambda e, db=db: e.dma_start(
                    out=sXC[db * 128:(db + 1) * 128, :], in_=d_cb[:, 0:S]),
                    after=cx.mark())
            else:
                db = m - 16

                def zsilu(s):
                    for tt in range(8):
                        last = s.activation(d_acc[:, tt * TB:(tt + 1) * TB],
                                            pb[tt][:, 0:TB], AF.Silu)
                    return last

                cx.on("scalar", zsilu,
                      pre_dma=st_t[m - 1] if m >= 1 else None)
                st_t[m] = cx.dma_async(lambda e, db=db: e.dma_start(
                    out=sSZ[db * 128:(db + 1) * 128, :], in_=d_acc[:, 0:S]),
                    after=cx.mark())

        cx.on("vector", lambda v: v.memset(tmp128[:], 0.0), pre_dma=cx.dn)
        for kt in range(16):
            cx.on("sync", dma(d_wst[0:33, 0:128], w_xp[:, kt * 128:(kt + 1) * 128]),
                  dma=True)
            cx.on("tensor", lambda t: t.transpose(pb[4][0:128, 0:33],
                                                  d_wst[0:33, 0:128],
                                                  ident[0:33, 0:33]))
            cx.on("scalar", lambda s, kt=kt: s.copy(xpt[:, kt * 33:(kt + 1) * 33],
                                                    pb[4][0:128, 0:33]))
        for tt in range(8):
            cx.on("sync", lambda e, tt=tt: [
                e.dma_start(out=d_acc[:, k * 256:(k + 1) * 256],
                            in_=sXC[k * 128:(k + 1) * 128,
                                    tt * TB:tt * TB + 256])
                for k in range(16)][-1], dma=True)
            cx.on("sync", lambda e, tt=tt: [
                e.dma_start(out=d_cb[:, k * 256:(k + 1) * 256],
                            in_=sXC[k * 128:(k + 1) * 128,
                                    tt * TB + 256:(tt + 1) * TB])
                for k in range(16)][-1], dma=True)

            def mmx(t, tt=tt):
                for k in range(16):
                    t.matmul(pb[6][0:33, 0:256], xpt[:, k * 33:(k + 1) * 33],
                             d_acc[:, k * 256:(k + 1) * 256],
                             start=(k == 0), stop=(k == 15), **MM)
                    last = t.matmul(pb[7][0:33, 0:256],
                                    xpt[:, k * 33:(k + 1) * 33],
                                    d_cb[:, k * 256:(k + 1) * 256],
                                    start=(k == 0), stop=(k == 15), **MM)
                return last

            cx.on("tensor", mmx)

            def xpev(s):
                s.copy(xps[:, 0:256], pb[6][0:33, 0:256])
                return s.copy(xps[:, 256:512], pb[7][0:33, 0:256])

            cx.on("scalar", xpev)
            cx.on("sync", lambda e, tt=tt: e.dma_start(
                out=sXP[:, tt * TB:(tt + 1) * TB], in_=xps[:]), dma=True)

        ckpt("PD")

        cx.scope("PE")
        ar = Arena()
        e_bbc = ar.al("e_bbc", [128, 16 * TB])
        e_cbc = ar.al("e_cbc", [128, 16 * TB])
        e_da = ar.al("e_da", [128, 16 * TB])
        e_m = ar.al("e_m", [128, 16 * TB])
        e_dtb = ar.al("e_dtb", [128, TB])
        e_row = ar.al("e_row", [1, 4 * TB])
        e_dt = ar.al("e_dt", [128, TB])
        e_sp1 = ar.al("e_sp1", [128, TB])
        e_u = ar.al("e_u", [128, TB])
        e_xc = ar.al("e_xc", [128, 2 * TB])
        e_sz = ar.al("e_sz", [128, 2 * TB])
        e_y = ar.al("e_y", [128, 2 * TB])
        e_st = ar.al("e_st", [128, 16 * 16])

        cx.on("vector", lambda v: v.memset(e_st[:], 0.0))
        pe_ld = [0] * 16
        pe_gm = [0] * 16
        pe_st = [0] * 16
        for c in range(NT):
            cx.on("sync", lambda e, c=c: e.dma_start(
                out=e_row[0:1, 0:TB], in_=sXP[0:1, c * TB:(c + 1) * TB]), dma=True)
            cx.on("tensor", lambda t: t.matmul(pb[4][:, 0:TB], ones1r[:],
                                               e_row[0:1, 0:TB], start=True,
                                               stop=True, **MM))
            cx.on("scalar", lambda s: s.copy(e_dtb[:], pb[4][:, 0:TB]))
            for grp, dst in ((1, e_bbc), (17, e_cbc)):
                for n0 in range(0, 16, 4):
                    cx.on("sync", lambda e, c=c, grp=grp, n0=n0: e.dma_start(
                        out=e_row.ap().rearrange("p (n t) -> p n t", n=4),
                        in_=sXP[grp + n0:grp + n0 + 4, c * TB:(c + 1) * TB]),
                        dma=True)

                    def bmm(t):
                        for i in range(4):
                            last = t.matmul(pb[i][:, 0:TB], ones1r[:],
                                            e_row[0:1, i * TB:(i + 1) * TB],
                                            start=True, stop=True, **MM)
                        return last

                    def bcp(s, n0=n0, dst=dst):
                        for i in range(4):
                            last = s.copy(dst[:, (n0 + i) * TB:(n0 + i + 1) * TB],
                                          pb[i][:, 0:TB])
                        return last

                    cx.on("tensor", bmm)
                    cx.on("scalar", bcp)
            for db in range(16):
                def ldx(e, db=db, c=c):
                    b = (db % 2) * TB
                    e.dma_start(out=e_xc[:, b:b + TB],
                                in_=sXC[db * 128:(db + 1) * 128,
                                        c * TB:(c + 1) * TB])
                    e.dma_start(out=e_sz[:, b:b + TB],
                                in_=sSZ[db * 128:(db + 1) * 128,
                                        c * TB:(c + 1) * TB])

                if db == 0:
                    pe_ld[0] = cx.dma_async(ldx, after=pe_gm[14] or None)
                if db + 1 < 16:
                    anc = pe_gm[db - 1] if db >= 1 else (pe_gm[15] or None)
                    pe_ld[db + 1] = cx.dma_async(
                        lambda e, db=db, c=c: ldx(e, db + 1, c), after=anc)

                def sp(s, db=db):
                    s.activation(e_sp1[:], e_dtb[:], AF.Abs,
                                 scale=pvec[:, 336 + db:337 + db],
                                 bias=pvec[:, 352 + db:353 + db])
                    s.activation(e_sp1[:], e_sp1[:], AF.Exp, scale=-1.0)
                    s.activation(e_sp1[:], e_sp1[:], AF.Ln, bias=1.0)
                    return s.activation(e_dt[:], e_dtb[:], AF.Relu,
                                        scale=pvec[:, 336 + db:337 + db],
                                        bias=pvec[:, 352 + db:353 + db])

                cx.on("scalar", sp)

                def mkdt(v, db=db):
                    b = (db % 2) * TB
                    v.tensor_add(e_dt[:], e_dt[:], e_sp1[:])
                    return v.tensor_tensor(e_u[:], e_dt[:], e_xc[:, b:b + TB],
                                           ALU.mult)

                cx.on("vector", mkdt, pre_dma=pe_ld[db])

                def dA(s, db=db):
                    for n in range(16):
                        last = s.activation(e_da[:, n * TB:(n + 1) * TB], e_dt[:],
                                            AF.Exp,
                                            scale=pvec[:, db * 16 + n:
                                                       db * 16 + n + 1])
                    return last

                def dbx(v):
                    return v.tensor_tensor(
                        e_m.ap().rearrange("p (n t) -> p n t", n=16),
                        e_u.ap().rearrange("p (o t) -> p o t", o=1)
                        .broadcast_to([128, 16, TB]),
                        e_bbc.ap().rearrange("p (n t) -> p n t", n=16),
                        ALU.mult)

                cx.par([("scalar", dA), ("vector", dbx)])

                def scan(v, db=db):
                    for n in range(16):
                        sl = slice(n * TB, (n + 1) * TB)
                        last = v.tensor_tensor_scan(
                            e_m[:, sl], e_da[:, sl], e_m[:, sl],
                            e_st[:, db * 16 + n:db * 16 + n + 1],
                            ALU.mult, ALU.add)
                    return last

                cx.on("vector", scan)
                cx.on("vector", lambda v, db=db: v.tensor_copy(
                    e_st[:, db * 16:(db + 1) * 16],
                    e_m.ap().rearrange("p (n t) -> p n t", n=16)[:, :, TB - 1:TB]))

                def redv(v):
                    v.tensor_tensor(e_m[:, 0:8 * TB], e_m[:, 0:8 * TB],
                                    e_cbc[:, 0:8 * TB], ALU.mult)
                    v.tensor_tensor(e_m[:, 0:4 * TB], e_m[:, 0:4 * TB],
                                    e_m[:, 4 * TB:8 * TB], ALU.add)
                    v.tensor_tensor(e_m[:, 0:2 * TB], e_m[:, 0:2 * TB],
                                    e_m[:, 2 * TB:4 * TB], ALU.add)
                    return v.tensor_tensor(e_m[:, 0:TB], e_m[:, 0:TB],
                                           e_m[:, TB:2 * TB], ALU.add)

                def redg(g):
                    o = 8 * TB
                    g.tensor_tensor(e_m[:, o:o + 8 * TB], e_m[:, o:o + 8 * TB],
                                    e_cbc[:, o:o + 8 * TB], ALU.mult)
                    g.tensor_tensor(e_m[:, o:o + 4 * TB], e_m[:, o:o + 4 * TB],
                                    e_m[:, o + 4 * TB:o + 8 * TB], ALU.add)
                    g.tensor_tensor(e_m[:, o:o + 2 * TB], e_m[:, o:o + 2 * TB],
                                    e_m[:, o + 2 * TB:o + 4 * TB], ALU.add)
                    return g.tensor_tensor(e_m[:, o:o + TB], e_m[:, o:o + TB],
                                           e_m[:, o + TB:o + 2 * TB], ALU.add)

                cx.par([("vector", redv), ("gpsimd", redg)])

                def gate(v, db=db):
                    b = (db % 2) * TB
                    v.tensor_tensor(e_y[:, b:b + TB], e_m[:, 0:TB],
                                    e_m[:, 8 * TB:9 * TB], ALU.add)
                    v.scalar_tensor_tensor(e_y[:, b:b + TB], e_xc[:, b:b + TB],
                                           pvec[:, 256 + db:257 + db],
                                           e_y[:, b:b + TB], ALU.mult, ALU.add)
                    return v.tensor_tensor(e_y[:, b:b + TB], e_y[:, b:b + TB],
                                           e_sz[:, b:b + TB], ALU.mult)

                cx.on("vector", gate,
                      pre_dma=pe_st[db - 2] if db >= 2
                      else (pe_st[14 + db] or None))
                pe_gm[db] = cx.mark()
                pe_st[db] = cx.dma_async(lambda e, db=db, c=c: e.dma_start(
                    out=sG[db * 128:(db + 1) * 128, c * TB:(c + 1) * TB],
                    in_=e_y[:, (db % 2) * TB:(db % 2) * TB + TB]),
                    after=pe_gm[db])

        ckpt("PE")

        cx.scope("PF")
        cx.on("vector", lambda v: v.memset(tmp128[:], 0.0), pre_dma=cx.dn)
        ar = Arena()
        WOPT = ar.al("wopt", [128, 16 * DM])
        f_wst = ar.al("f_wst", [128, DI])
        f_g16 = ar.al("f_g16", [128, 16 * TB])
        f_rsb = ar.al("f_rsb", [128, 8 * TB])
        f_r2 = ar.al("f_r2", [128, 8 * TB])
        f_xr = ar.al("f_xr", [128, 8 * TB])

        for m in range(8):
            cx.on("sync", dma(f_wst[:, 0:DI], w_op[m * 128:(m + 1) * 128, :]),
                  dma=True)
            for kt0 in range(0, 16, 4):
                kts = list(range(kt0, kt0 + 4))

                def tr(t, kts=kts):
                    for kt in kts:
                        last = t.transpose(pb[4 + kt % 4][:, 0:128],
                                           f_wst[:, kt * 128:kt * 128 + 128],
                                           ident[:])
                    return last

                def cp(s, kts=kts, m=m):
                    for kt in kts:
                        last = s.copy(WOPT[:, kt * DM + m * 128:kt * DM + (m + 1) * 128],
                                      pb[4 + kt % 4][:, 0:128])
                    return last

                cx.on("tensor", tr)
                cx.on("scalar", cp)

        pf_ld = [0] * NT
        pf_rm = [0] * NT
        pf_st = [0] * NT

        def gld(e, tt):
            for k in range(16):
                e.dma_start(
                    out=f_g16[:, k * TB:(k + 1) * TB],
                    in_=sG[k * 128:(k + 1) * 128, tt * TB:(tt + 1) * TB])
            for m in range(8):
                e.dma_start(
                    out=f_xr[:, m * TB:(m + 1) * TB],
                    in_=sX2[m * 128:(m + 1) * 128, tt * TB:(tt + 1) * TB])

        pf_ld[0] = cx.dma_async(lambda e: gld(e, 0))
        for tt in range(NT):
            def mm(t):
                for k in range(16):
                    for m in range(8):
                        last = t.matmul(pb[m][:, 0:TB],
                                        WOPT[:, k * DM + m * 128:k * DM + (m + 1) * 128],
                                        f_g16[:, k * TB:(k + 1) * TB],
                                        start=(k == 0), stop=(k == 15), **MM)
                return last

            cx.on("tensor", mm, pre_dma=pf_ld[tt])
            mmk = cx.mark()

            def radd(v):
                for m in range(8):
                    last = v.tensor_add(f_rsb[:, m * TB:(m + 1) * TB],
                                        pb[m][:, 0:TB],
                                        f_xr[:, m * TB:(m + 1) * TB])
                return last

            cx.on("vector", radd,
                  pre_dma=pf_st[tt - 1] if tt >= 1 else None)
            pf_rm[tt] = cx.mark()
            if tt + 1 < NT:
                pf_ld[tt + 1] = cx.dma_async(
                    lambda e, tt=tt: gld(e, tt + 1), after=pf_rm[tt])

            def rsq(s):
                for m in range(8):
                    last = s.activation(f_r2[:, m * TB:(m + 1) * TB],
                                        f_rsb[:, m * TB:(m + 1) * TB], AF.Square)
                return last

            cx.on("scalar", rsq)

            def mm_st(t):
                for m in range(8):
                    t.matmul(pb[6][0:1, 0:TB], ones128[:, 0:1],
                             f_rsb[:, m * TB:(m + 1) * TB], start=(m == 0),
                             stop=(m == 7), **MM)
                    last = t.matmul(pb[7][0:1, 0:TB], ones128[:, 0:1],
                                    f_r2[:, m * TB:(m + 1) * TB], start=(m == 0),
                                    stop=(m == 7), **MM)
                return last

            cx.on("tensor", mm_st)
            emit_ln(f_rsb, None, tt)
            pf_st[tt] = cx.dma_async(lambda e, tt=tt: [
                e.dma_start(out=sX3[m * 128:(m + 1) * 128,
                                    tt * TB:(tt + 1) * TB],
                            in_=f_rsb[:, m * TB:(m + 1) * TB])
                for m in range(8)], after=cx.mark())

        ckpt("PF")

        cx.scope("PG")
        cx.on("vector", lambda v: v.memset(tmp128[:], 0.0), pre_dma=cx.dn)
        ar = Arena()
        WLT = ar.al("wlt", [128, 8 * DM])
        g_w = ar.al("g_w", [128, DM])
        g_i = ar.al("g_i", [128, S], I32)
        g_am = ar.al("g_am", [128, 128])
        g_sc = ar.al("g_sc", [128, 128])
        g_e = ar.al("g_e", [128, 128], I32)
        g_dw = ar.al("g_dw", [128, 32])
        g_a = ar.al("g_a", [128, S])
        g_g2 = ar.al("g_g2", [128, S])
        g_sg = ar.al("g_sg", [128, S])
        g_x3 = ar.al("g_x3", [128, 8 * TB])
        g_xt = ar.al("g_xt", [128, 4 * DM])
        g_rsb = ar.al("g_rsb", [128, 8 * TB])
        g_r2 = ar.al("g_r2", [128, 8 * TB])

        for m in range(8):
            cx.on("sync", dma(g_dw[:, 0:32], dw_l1[m * 128:(m + 1) * 128, :]),
                  dma=True)
            cx.on("scalar", lambda s: s.activation(g_dw[:, 0:32], g_dw[:, 0:32],
                                                   AF.Tanh, scale=4.0))
            cx.on("scalar", lambda s: s.activation(g_dw[:, 0:32], g_dw[:, 0:32],
                                                   AF.Exp, scale=0.5 * LN2))
            cx.on("sync", dma(g_w[:, 0:DM], w_l1[m * 128:(m + 1) * 128, :]),
                  dma=True)
            mxfp4_emit(g_w[:, 0:DM], g_i[:, 0:DM], g_am[:, 0:32], g_sc[:, 0:32],
                       g_e[:, 0:32], g_dw[:, 0:32], g_a[:, 0:DM], g_g2[:, 0:DM],
                       g_sg[:, 0:DM], DM)

            def trw(t):
                for kt in range(8):
                    last = t.transpose(pb[kt][:, 0:128],
                                       g_w[:, kt * 128:kt * 128 + 128], ident[:])
                return last

            def cpw(s, m=m):
                for kt in range(8):
                    last = s.copy(WLT[:, kt * DM + m * 128:kt * DM + (m + 1) * 128],
                                  pb[kt][:, 0:128])
                return last

            cx.on("tensor", trw)
            cx.on("scalar", cpw)

        def tr8(src_t, src_base, dst_t, dst_base, src_stride, dst_stride):
            def tr(t):
                for kt in range(8):
                    last = t.transpose(
                        pb[kt][:, 0:128],
                        src_t[:, src_base + kt * src_stride:
                              src_base + kt * src_stride + 128], ident[:])
                return last

            def cp(s):
                for kt in range(8):
                    last = s.copy(dst_t[:, dst_base + kt * dst_stride:
                                        dst_base + kt * dst_stride + 128],
                                  pb[kt][:, 0:128])
                return last

            cx.on("tensor", tr)
            cx.on("scalar", cp)

        for c in range(NT):
            cx.on("sync", lambda e, c=c: [
                e.dma_start(out=g_x3[:, pt * TB:(pt + 1) * TB],
                            in_=sX3[pt * 128:(pt + 1) * 128, c * TB:(c + 1) * TB])
                for pt in range(8)][-1], dma=True)
            for i in range(4):
                tr8(g_x3, i * 128, g_xt, i * DM, TB, 128)
            mxfp4_emit(g_xt[:, 0:4 * DM], g_i[:, 0:4 * DM], g_am[:, 0:128],
                       g_sc[:, 0:128], g_e[:, 0:128], None, g_a[:, 0:4 * DM],
                       g_g2[:, 0:4 * DM], g_sg[:, 0:4 * DM], 4 * DM)
            for i in range(4):
                tr8(g_xt, i * DM, g_x3, i * 128, 128, TB)

            def mm(t):
                for k in range(8):
                    for m in range(8):
                        last = t.matmul(pb[m][:, 0:TB],
                                        WLT[:, k * DM + m * 128:k * DM + (m + 1) * 128],
                                        g_x3[:, k * TB:(k + 1) * TB],
                                        start=(k == 0), stop=(k == 7), **MM)
                return last

            cx.on("tensor", mm)
            cx.on("sync", lambda e, c=c: [
                e.dma_start(out=g_r2[:, m * TB:(m + 1) * TB],
                            in_=sX3[m * 128:(m + 1) * 128, c * TB:(c + 1) * TB])
                for m in range(8)][-1], dma=True)

            def radd(v):
                for m in range(8):
                    last = v.scalar_tensor_tensor(
                        g_rsb[:, m * TB:(m + 1) * TB], pb[m][:, 0:TB],
                        pvec[:, 384 + m:385 + m], g_r2[:, m * TB:(m + 1) * TB],
                        ALU.add, ALU.add)
                return last

            cx.on("vector", radd)

            def rsq(s):
                for m in range(8):
                    last = s.activation(g_r2[:, m * TB:(m + 1) * TB],
                                        g_rsb[:, m * TB:(m + 1) * TB], AF.Square)
                return last

            cx.on("scalar", rsq)

            def mm_st(t):
                for m in range(8):
                    t.matmul(pb[6][0:1, 0:TB], ones128[:, 0:1],
                             g_rsb[:, m * TB:(m + 1) * TB], start=(m == 0),
                             stop=(m == 7), **MM)
                    last = t.matmul(pb[7][0:1, 0:TB], ones128[:, 0:1],
                                    g_r2[:, m * TB:(m + 1) * TB], start=(m == 0),
                                    stop=(m == 7), **MM)
                return last

            cx.on("tensor", mm_st)
            emit_ln(g_rsb, None, c)
            for i in range(4):
                tr8(g_rsb, i * 128, g_xt, i * DM, TB, 128)
            cx.on("sync", lambda e, c=c: [
                e.dma_start(out=out_d[c * TB + i * 128:c * TB + (i + 1) * 128, :],
                            in_=g_xt[:, i * DM:(i + 1) * DM]) for i in range(4)][-1],
                dma=True)

        ckpt("PG")

        cx.stopped = False
        for name in debug:
            cx.on("sync", dma(dbg[name], dbg_src[name]), dma=True)

        cx.final_wait("sync")

    auto_end = 16512 + (212863 - nc.sbuf_bytes_remaining)
    assert auto_end <= ARENA0, f"auto sbuf end {auto_end} exceeds ARENA0 {ARENA0}"
    return nc, dbg


_BUILT = {}


def _get_built(debug=(), stop_at=None):
    key = (tuple(debug), stop_at)
    if key not in _BUILT:
        _BUILT[key] = build(debug, stop_at)
    return _BUILT[key]


def kernel(**inputs):
    nc, _ = _get_built(DEBUG, STOP_AT)
    consts = _host_constants()
    x = np.ascontiguousarray(np.asarray(inputs["L0_attn_x"], np.float32))
    base = {
        "w_qkv": np.ascontiguousarray(inputs["L0_attn_W_qkv"], np.float32),
        "w_o": np.ascontiguousarray(inputs["L0_attn_W_o"], np.float32),
        "w_gate": np.ascontiguousarray(inputs["L0_ffn_w_gate"], np.float32),
        "w_up": np.ascontiguousarray(inputs["L0_ffn_w_up"], np.float32),
        "w_down": np.ascontiguousarray(inputs["L0_ffn_w_down"], np.float32),
        "w_ip": np.ascontiguousarray(inputs["L1_attn_in_proj_w"], np.float32),
        "conv_w": np.ascontiguousarray(
            np.asarray(inputs["L1_attn_conv_w"], np.float32).reshape(DI, 4)),
        "conv_b": np.ascontiguousarray(
            np.asarray(inputs["L1_attn_conv_b"], np.float32).reshape(DI, 1)),
        "w_xp": np.ascontiguousarray(inputs["L1_attn_x_proj_w"], np.float32),
        "dt_w": np.ascontiguousarray(
            np.asarray(inputs["L1_attn_dt_proj_w"], np.float32).reshape(DI, 1)),
        "dt_b": np.ascontiguousarray(
            np.asarray(inputs["L1_attn_dt_proj_b"], np.float32).reshape(DI, 1)),
        "a_log": np.ascontiguousarray(inputs["L1_attn_A_log"], np.float32),
        "d_vec": np.ascontiguousarray(
            np.asarray(inputs["L1_attn_D"], np.float32).reshape(DI, 1)),
        "w_op": np.ascontiguousarray(inputs["L1_attn_out_proj_w"], np.float32),
        "w_l1": np.ascontiguousarray(inputs["L1_ffn_weight"], np.float32),
        "b_l1": np.ascontiguousarray(
            np.asarray(inputs["L1_ffn_bias"], np.float32).reshape(DM, 1)),
        "dw_l1": np.ascontiguousarray(inputs["L1_ffn_delta_w"], np.float32),
        "da_l1": np.ascontiguousarray(
            np.asarray(inputs["L1_ffn_delta_act"], np.float32).reshape(1, 1)),
    }
    base.update(consts)
    in_maps = []
    for b in range(NCORES):
        m = dict(base)
        m["x"] = np.ascontiguousarray(x[b])
        in_maps.append(m)
    res = run_bass_kernel_spmd(nc, in_maps, core_ids=list(range(NCORES)))
    out = np.stack([np.asarray(res.results[b]["out"]) for b in range(NCORES)],
                   axis=0)
    if DEBUG:
        kernel._dbg = [{k: np.asarray(res.results[b]["dbg_" + k]) for k in DEBUG}
                       for b in range(NCORES)]
    return out


# version 07
# speedup vs baseline: 1.1306x; 1.0177x over previous
import numpy as np
import concourse.bass as bass
import concourse.mybir as mybir
from concourse.bass_utils import run_bass_kernel_spmd
from contextlib import ExitStack

F32 = mybir.dt.float32
I32 = mybir.dt.int32
AF = mybir.ActivationFunctionType
ALU = mybir.AluOpType
AX = mybir.AxisListType

S = 4096
DM = 1024
FF = 3072
DI = 2048
DSN = 16
NH = 16
DK = 64
TB = 512
NT = S // TB
LN_EPS = 1e-5
QMAX0 = 32767.0
LN2 = float(np.log(2.0))
NCORES = 8

DEBUG = ()
STOP_AT = None


class Ctx:

    def __init__(self, nc, es):
        self.nc = nc
        self.ch = es.enter_context(nc.semaphore("chain"))
        self.d2 = es.enter_context(nc.semaphore("dma2"))
        self.t = 0
        self.dn = 0
        self.block = None
        self.stopped = False
        self._eng = None
        self._dma = False
        self._fns = []
        self._pre = None
        self.scope_name = "P0"

    def scope(self, name):
        self._flush()
        self.scope_name = name

    class _SmallGuard:

        SMALL = 512
        _PASS = ("drain", "wait_ge", "nop", "engine_nop")

        def __init__(self, eng):
            self._e = eng

        def __getattr__(self, name):
            f = getattr(self._e, name)
            if name in Ctx._SmallGuard._PASS or not callable(f):
                return f

            def wrap(*a, **kw):
                r = f(*a, **kw)
                out = a[0] if a else kw.get("out")
                fs = None
                try:
                    fs = out.free_size()
                except Exception:
                    pass
                if fs is not None and fs < Ctx._SmallGuard.SMALL:
                    self._e.drain()
                return r

            return wrap

    class _DmaRec:

        def __init__(self, eng):
            self._eng = eng
            self.insts = []

        def dma_start(self, **kw):
            i = self._eng.dma_start(**kw)
            self.insts.append(i)
            return i

        def wait_ge(self, *a):
            return self._eng.wait_ge(*a)

    def _flush(self):
        if not self._fns:
            return
        dec = getattr(self.block, self._eng)
        t0 = self.t
        fns = self._fns
        self._fns = []
        is_dma = self._dma
        eng_name = self._eng
        n = [0]

        pre = self._pre
        self._pre = None
        scname = self.scope_name

        def body(eng):
          with self.nc.named_scope(scname):
            if pre is not None and pre > 0:
                eng.wait_ge(self.d2, pre)
            if t0 > 0:
                eng.wait_ge(self.ch, t0)
            if is_dma:
                rec = Ctx._DmaRec(eng)
                for fn in fns:
                    fn(rec)
                assert rec.insts
                for i in rec.insts:
                    i.then_inc(self.ch, 16)
                n[0] = 16 * len(rec.insts)
            else:
                geng = eng if eng_name in ("tensor", "sync") \
                    else Ctx._SmallGuard(eng)
                for fn in fns:
                    last = fn(geng)
                assert last is not None, "step fn must return its last instruction"
                last.then_inc(self.ch, 1)
                n[0] = 1

        dec(body)
        assert n[0] > 0
        self.t = t0 + n[0]

    def on(self, eng_name, fn, dma=False, force_wait=False, pre_dma=None):
        if self.stopped:
            return
        if pre_dma is None and self._fns and self._eng == eng_name \
                and self._dma == dma and not force_wait:
            self._fns.append(fn)
            return
        self._flush()
        self._eng = eng_name
        self._dma = dma
        self._fns = [fn]
        self._pre = pre_dma

    def dma_async(self, fn, after=None, after_d2=None):
        if self.stopped:
            return self.dn
        self._flush()
        dec = self.block.sync
        n = [0]
        d2 = self.d2

        scname = self.scope_name

        def body(eng):
          with self.nc.named_scope(scname):
            if after_d2 is not None and after_d2 > 0:
                eng.wait_ge(self.d2, after_d2)
            if after is not None and after > 0:
                eng.wait_ge(self.ch, after)
            rec = Ctx._DmaRec(eng)
            fn(rec)
            assert rec.insts
            for i in rec.insts:
                i.then_inc(d2, 16)
            n[0] = 16 * len(rec.insts)

        dec(body)
        assert n[0] > 0
        self.dn += n[0]
        self._eng = None
        return self.dn

    def par(self, specs):
        if self.stopped:
            return
        self._flush()
        t0 = self.t
        scname = self.scope_name
        for spec in specs:
            eng_name, fn = spec[0], spec[1]
            pre = spec[2] if len(spec) > 2 else None
            dec = getattr(self.block, eng_name)

            def body(eng, fn=fn, pre=pre, eng_name=eng_name):
                with self.nc.named_scope(scname):
                    if pre is not None and pre > 0:
                        eng.wait_ge(self.d2, pre)
                    if t0 > 0:
                        eng.wait_ge(self.ch, t0)
                    geng = eng if eng_name in ("tensor", "sync") \
                        else Ctx._SmallGuard(eng)
                    last = fn(geng)
                    assert last is not None
                    last.then_inc(self.ch, 1)

            dec(body)
        self.t = t0 + len(specs)

    def mark(self):
        self._flush()
        return self.t

    def final_wait(self, eng_name):
        self._flush()
        dec = getattr(self.block, eng_name)
        t0 = self.t

        def body(eng):
            eng.wait_ge(self.ch, t0)

        dec(body)


def _host_constants():
    dk = DK
    inv_freq = (np.float32(10000.0) ** (np.arange(0, dk, 2, dtype=np.float32)
                                        / np.float32(dk))).astype(np.float32)
    ang = (np.arange(S, dtype=np.float32)[:, None] / inv_freq[None, :]).astype(np.float32)
    cos = np.cos(ang).astype(np.float32)
    sin = np.sin(ang).astype(np.float32)
    cos64 = np.repeat(cos.T, 2, axis=0)
    sin64 = np.repeat(sin.T, 2, axis=0)
    cos128 = np.concatenate([cos64, cos64], axis=0)
    sgn = np.where(np.arange(64) % 2 == 0, -1.0, 1.0).astype(np.float32)[:, None]
    sin64s = (sin64 * sgn).astype(np.float32)
    sin128 = np.concatenate([sin64s, sin64s], axis=0)
    masks = []
    for r in range(4):
        kk = np.arange(128)[:, None] + 128 * r
        qq = np.arange(TB)[None, :]
        masks.append((kk <= qq).astype(np.float32))
    return {
        "c_cos": np.ascontiguousarray(cos128),
        "c_sin": np.ascontiguousarray(sin128),
        "c_mask": np.ascontiguousarray(np.concatenate(masks, axis=1)),
        "c_ident": np.eye(128, dtype=np.float32),
    }


def build(debug=(), stop_at=None):
    nc = bass.Bass("TRN2", target_bir_lowering=False, detect_race_conditions=False)
    es = ExitStack()

    def din(name, shape):
        return nc.dram_tensor(name, list(shape), F32, kind="ExternalInput").ap()

    x_in = din("x", (S, DM))
    w_qkv = din("w_qkv", (3 * DM, DM))
    w_o = din("w_o", (DM, DM))
    w_gate = din("w_gate", (FF, DM))
    w_up = din("w_up", (FF, DM))
    w_down = din("w_down", (DM, FF))
    w_ip = din("w_ip", (2 * DI, DM))
    conv_w = din("conv_w", (DI, 4))
    conv_b = din("conv_b", (DI, 1))
    w_xp = din("w_xp", (2 * DSN + 1, DI))
    dt_w = din("dt_w", (DI, 1))
    dt_b = din("dt_b", (DI, 1))
    a_log = din("a_log", (DI, DSN))
    d_vec = din("d_vec", (DI, 1))
    w_op = din("w_op", (DM, DI))
    w_l1 = din("w_l1", (DM, DM))
    b_l1 = din("b_l1", (DM, 1))
    dw_l1 = din("dw_l1", (DM, 32))
    da_l1 = din("da_l1", (1, 1))
    c_cos = din("c_cos", (128, S))
    c_sin = din("c_sin", (128, S))
    c_mask = din("c_mask", (128, 4 * TB))
    c_ident = din("c_ident", (128, 128))

    out_d = nc.dram_tensor("out", [S, DM], F32, kind="ExternalOutput").ap()

    def dscr(name, shape):
        return nc.dram_tensor(name, list(shape), F32).ap()

    sXT = dscr("sXT", (DM, S))
    sQ = dscr("sQ", (NH, DK, S))
    sK = dscr("sK", (NH, DK, S))
    sVT = dscr("sVT", (NH, S // 128, 128, DK))
    sX1 = dscr("sX1", (DM, S))
    sXQ = dscr("sXQ", (DM, S))
    sWUT = dscr("sWUT", (DM // 128, 128, FF))
    sWGT = dscr("sWGT", (DM // 128, 128, FF))
    sWDT = dscr("sWDT", (FF // 128, 128, DM))
    sUP = dscr("sUP", (FF, S))
    sGA = dscr("sGA", (FF, S))
    sUPQ = dscr("sUPQ", (FF, S))
    sG1 = dscr("sG1", (FF, S))
    sPROD = dscr("sPROD", (FF, S))
    sX2 = dscr("sX2", (DM, S))
    sWIPT = dscr("sWIPT", (DM // 128, 128, 2 * DI))
    sXC = dscr("sXC", (DI, S))
    sSZ = dscr("sSZ", (DI, S))
    sG = dscr("sG", (DI, S))
    sX3 = dscr("sX3", (DM, S))
    sXP = dscr("sXP", (33, S))
    sATT = dscr("sATT", (DM, S))
    sSC = dscr("sSC", (128, 4 * TB))
    sEX = dscr("sEX", (128, 4 * TB))
    cc_in = nc.dram_tensor("cc_in", [1, 8], F32).ap()
    cc_outs = [nc.dram_tensor(f"cc_out{i}", [1, 8], F32, addr_space="Shared").ap()
               for i in range(6)]

    dbg_shapes = {"XT": (DM, S), "Q": (NH, DK, S), "K": (NH, DK, S),
                  "VT": (NH, S // 128, 128, DK), "X1": (DM, S), "XQ": (DM, S), "ATT": (DM, S), "SC": (128, 4 * TB), "EX": (128, 4 * TB),
                  "UP": (FF, S), "GA": (FF, S), "UPQ": (FF, S), "G1": (FF, S),
                  "PROD": (FF, S), "X2": (DM, S), "XC": (DI, S), "SZ": (DI, S),
                  "G": (DI, S), "X3": (DM, S), "XP": (33, S)}
    dbg_src = {"XT": sXT, "Q": sQ, "K": sK, "VT": sVT, "X1": sX1, "XQ": sXQ, "ATT": sATT, "SC": sSC, "EX": sEX,
               "UP": sUP, "GA": sGA, "UPQ": sUPQ, "G1": sG1, "PROD": sPROD,
               "X2": sX2, "XC": sXC, "SZ": sSZ, "G": sG, "X3": sX3, "XP": sXP}
    dbg = {}
    for name in debug:
        dbg[name] = nc.dram_tensor("dbg_" + name, list(dbg_shapes[name]), F32,
                                   kind="ExternalOutput").ap()

    ident = nc.alloc_sbuf_tensor("ident", [128, 128], F32)
    pvec = nc.alloc_sbuf_tensor("pvec", [128, 512], F32)
    ones128 = nc.alloc_sbuf_tensor("ones128", [128, 1], F32)
    ones1r = nc.alloc_sbuf_tensor("ones1r", [1, 128], F32)
    smax = nc.alloc_sbuf_tensor("smax", [128, 8], F32)
    srow = nc.alloc_sbuf_tensor("srow", [1, 64], F32)
    sscal = nc.alloc_sbuf_tensor("sscal", [128, 16], F32)
    maskt = nc.alloc_sbuf_tensor("maskt", [128, 4 * TB], F32)
    mrow = nc.alloc_sbuf_tensor("mrow", [1, TB], F32)
    vrow = nc.alloc_sbuf_tensor("vrow", [1, TB], F32)
    m2row = nc.alloc_sbuf_tensor("m2row", [1, TB], F32)
    wrow = nc.alloc_sbuf_tensor("wrow", [1, TB], F32)
    trow = nc.alloc_sbuf_tensor("trow", [1, TB], F32)
    tmp128 = nc.alloc_sbuf_tensor("tmp128", [128, 1], F32)
    constv = nc.alloc_sbuf_tensor("constv", [128, 4], F32)
    ones8 = nc.alloc_sbuf_tensor("ones8", [128, 8], F32)
    sscal8 = nc.alloc_sbuf_tensor("sscal8", [128, 16], F32)

    ARENA0 = 42 * 1024
    ARENA_SZ = 183 * 1024

    class Arena:
        def __init__(self):
            self.off = 0

        def al(self, name, shape, dtype=F32):
            nbytes = int(np.prod(shape[1:])) * mybir.dt.size(dtype)
            nbytes = (nbytes + 31) & ~31
            t = nc.alloc_sbuf_tensor_at(name, list(shape), dtype,
                                        offset=ARENA0 + self.off)
            self.off += nbytes
            assert self.off <= ARENA_SZ, f"arena overflow at {name}: {self.off}"
            return t

    pb = [nc.alloc_psum_tensor(f"pb{i}", [128, TB], F32) for i in range(8)]

    cx = Ctx(nc, es)

    def ckpt(name):
        if stop_at == name:
            cx.stopped = True

    def dma(out, in_):
        return lambda e: e.dma_start(out=out, in_=in_)

    MM = dict(skip_group_check=True)

    with nc.Block("main") as blk:
        cx.block = blk

        def p0_loads(sync):
            sync.dma_start(out=ident[:], in_=c_ident[:])
            sync.dma_start(out=maskt[:], in_=c_mask[:])
            for db in range(16):
                sync.dma_start(out=pvec[:, db * 16:(db + 1) * 16],
                               in_=a_log[db * 128:(db + 1) * 128, :])
                sync.dma_start(out=pvec[:, 256 + db:257 + db],
                               in_=d_vec[db * 128:(db + 1) * 128, :])
                sync.dma_start(out=pvec[:, 272 + db * 4:276 + db * 4],
                               in_=conv_w[db * 128:(db + 1) * 128, :])
                sync.dma_start(out=pvec[:, 336 + db:337 + db],
                               in_=dt_w[db * 128:(db + 1) * 128, :])
                sync.dma_start(out=pvec[:, 352 + db:353 + db],
                               in_=dt_b[db * 128:(db + 1) * 128, :])
                sync.dma_start(out=pvec[:, 368 + db:369 + db],
                               in_=conv_b[db * 128:(db + 1) * 128, :])
            for mb in range(8):
                sync.dma_start(out=pvec[:, 384 + mb:385 + mb],
                               in_=b_l1[mb * 128:(mb + 1) * 128, :])
            return sync.dma_start(out=srow[0:1, 8:9], in_=da_l1[:])

        cx.on("sync", p0_loads, dma=True)

        def p0_memset(v):
            v.memset(ones128[:], 1.0)
            v.memset(ones1r[:], 1.0)
            v.memset(smax[:], 0.0)
            v.memset(constv[:, 0:1], LN_EPS)
            v.memset(ones8[:], 1.0)
            return v.memset(constv[:, 1:2], -2.0 * LN2)

        cx.on("vector", p0_memset)
        cx.on("scalar", lambda s: s.activation(pvec[:, 0:256], pvec[:, 0:256],
                                               AF.Exp))
        cx.on("vector", lambda v: v.tensor_scalar_mul(pvec[:, 0:256],
                                                      pvec[:, 0:256], -1.0))
        cx.on("scalar", lambda s: s.activation(srow[0:1, 9:10], srow[0:1, 8:9],
                                               AF.Tanh, scale=4.0))
        cx.on("scalar", lambda s: s.activation(srow[0:1, 9:10], srow[0:1, 9:10],
                                               AF.Exp, scale=0.5 * LN2))
        cx.on("tensor", lambda t: t.matmul(pb[0][:, 0:1], ones1r[:],
                                           srow[0:1, 9:10], start=True, stop=True,
                                           **MM))
        cx.on("scalar", lambda s: s.copy(sscal[:, 10:11], pb[0][:, 0:1]))
        cx.on("gpsimd", lambda g: g.dma_start(out=cc_in[:], in_=smax[0:1, 0:8]),
              dma=True)
        cx.on("gpsimd", lambda g: g.collective_compute(
            "AllReduce", ALU.max, replica_groups=[list(range(NCORES))],
            ins=[cc_in], outs=[cc_outs[5]]), force_wait=True)

        def wt_prep_sbuf(src, n_rows, n_cols, wt, wt_stride, stage):
            nmt, nkt = n_rows // 128, n_cols // 128
            for m in range(nmt):
                cx.on("sync", dma(stage[:, 0:n_cols], src[m * 128:(m + 1) * 128, :]),
                      dma=True)
                for kt0 in range(0, nkt, 8):
                    kts = list(range(kt0, min(kt0 + 8, nkt)))

                    def tr(t, kts=kts):
                        for kt in kts:
                            last = t.transpose(pb[kt % 8][:, 0:128],
                                               stage[:, kt * 128:kt * 128 + 128],
                                               ident[:])
                        return last

                    def cp(s, kts=kts, m=m):
                        for kt in kts:
                            last = s.copy(wt[:, kt * wt_stride + m * 128:
                                             kt * wt_stride + (m + 1) * 128],
                                          pb[kt % 8][:, 0:128])
                        return last

                    cx.on("tensor", tr)
                    cx.on("scalar", cp)

        def emit_ln(rsb_t, store_dst, jj, max_slot=None):
            def rows(v):
                v.tensor_scalar_mul(mrow[:], pb[6][0:1, 0:TB], 1.0 / DM)
                v.tensor_scalar_mul(vrow[:], pb[7][0:1, 0:TB], 1.0 / DM)
                v.tensor_tensor(m2row[:], mrow[:], mrow[:], ALU.mult)
                v.tensor_tensor(vrow[:], vrow[:], m2row[:], ALU.subtract)
                return v.tensor_copy(m2row[:], vrow[:])

            cx.on("vector", rows)
            cx.on("scalar", lambda s: s.activation(vrow[:], vrow[:], AF.Sqrt,
                                                   bias=constv[0:1, 0:1]))

            def rsqrt_ref(v):
                v.tensor_scalar_add(wrow[:], m2row[:], LN_EPS)
                v.reciprocal(vrow[:], vrow[:])
                for _ in range(2):
                    v.tensor_tensor(trow[:], vrow[:], vrow[:], ALU.mult)
                    v.tensor_tensor(trow[:], trow[:], wrow[:], ALU.mult)
                    v.tensor_scalar(trow[:], trow[:], -0.5, 1.5, ALU.mult,
                                    ALU.add)
                    last = v.tensor_tensor(vrow[:], vrow[:], trow[:], ALU.mult)
                return last

            cx.on("vector", rsqrt_ref)

            def bc(t):
                t.matmul(pb[6][:, 0:TB], ones1r[:], mrow[:], start=True, stop=True,
                         **MM)
                return t.matmul(pb[7][:, 0:TB], ones1r[:], vrow[:], start=True,
                                stop=True, **MM)

            cx.on("tensor", bc)

            def apply(v):
                for m in range(8):
                    sl = rsb_t[:, m * TB:(m + 1) * TB]
                    v.tensor_tensor(sl, sl, pb[6][:, 0:TB], ALU.subtract)
                    last = v.tensor_tensor(sl, sl, pb[7][:, 0:TB], ALU.mult)
                    if max_slot is not None:
                        v.tensor_reduce(tmp128[:], sl, AX.X, ALU.max,
                                        apply_absolute_value=True)
                        last = v.tensor_tensor(smax[:, max_slot:max_slot + 1],
                                               smax[:, max_slot:max_slot + 1],
                                               tmp128[:], ALU.max)
                return last

            cx.on("vector", apply)
            if store_dst is not None:
                cx.on("sync", lambda e: [
                    e.dma_start(out=store_dst[m * 128:(m + 1) * 128,
                                              jj * TB:(jj + 1) * TB],
                                in_=rsb_t[:, m * TB:(m + 1) * TB])
                    for m in range(8)][-1], dma=True)

        def exchange(slot, cc_idx, out_col):
            cx.on("tensor", lambda t: t.transpose(
                pb[0][0:1, 0:128], smax[:, slot:slot + 1], ident[:]))
            cx.on("vector", lambda v: v.tensor_reduce(
                srow[0:1, 0:1], pb[0][0:1, 0:128], AX.X, ALU.max))
            cx.on("gpsimd", lambda g: g.dma_start(out=cc_in[0:1, 0:1],
                                                  in_=srow[0:1, 0:1]), dma=True)
            cx.on("gpsimd", lambda g: g.collective_compute(
                "AllReduce", ALU.max, replica_groups=[list(range(NCORES))],
                ins=[cc_in], outs=[cc_outs[cc_idx]]), force_wait=True)
            cx.on("gpsimd", lambda g: g.dma_start(out=srow[0:1, 0:1],
                                                  in_=cc_outs[cc_idx][0:1, 0:1]),
                  dma=True, force_wait=True)

            cx.on("tensor", lambda t: t.matmul(pb[0][:, 0:1], ones1r[:],
                                               srow[0:1, 0:1], start=True,
                                               stop=True, **MM))
            cx.on("scalar", lambda s: s.copy(sscal[:, 14:15], pb[0][:, 0:1]))

            def sc(v):
                v.tensor_scalar(sscal8[:, 0:8], ones8[:], sscal[:, 14:15], None,
                                ALU.mult)
                v.tensor_scalar(sscal8[:, 0:8], sscal8[:, 0:8], 1.0 / QMAX0,
                                1e-12, ALU.mult, ALU.max)
                v.reciprocal(sscal8[:, 8:16], sscal8[:, 0:8])
                v.tensor_copy(sscal[:, out_col:out_col + 1], sscal8[:, 0:1])
                return v.tensor_copy(sscal[:, out_col + 1:out_col + 2],
                                     sscal8[:, 8:9])

            cx.on("vector", sc)

        def mxfp4_emit(wb, ib, amb, scb, eb, dwf_ap, ab, g2b, sgb, ncols):
            nb = ncols // 32
            wv = wb.rearrange("p (b q) -> p b q", q=32)
            amv = amb.rearrange("p (b o) -> p b o", o=1)
            scv = scb.rearrange("p (b o) -> p b o", o=1)

            def step1(v):
                v.tensor_reduce(amv, wv, AX.X, ALU.max, apply_absolute_value=True)
                return v.tensor_single_scalar(amb, amb, 1e-30, ALU.max)

            cx.on("vector", step1)
            cx.on("scalar", lambda s: s.activation(scb, amb, AF.Ln))

            def step2(v):
                v.tensor_scalar(eb, scb, 1.0 / LN2, -0.5, ALU.mult, ALU.add)
                return v.tensor_copy(scb, eb)

            cx.on("vector", step2)
            cx.on("scalar", lambda s: s.activation(scb, scb, AF.Exp, scale=LN2,
                                                   bias=constv[:, 1:2]))
            if dwf_ap is not None:
                cx.on("vector", lambda v: v.tensor_tensor(scb, scb, dwf_ap,
                                                          ALU.mult))
            else:
                cx.on("vector", lambda v: v.tensor_scalar(scb, scb,
                                                          sscal[:, 10:11], None,
                                                          ALU.mult))

            def step3a(v):
                v.reciprocal(amb, scb)
                return v.tensor_tensor(wv, wv, amv.broadcast_to([128, nb, 32]),
                                       ALU.mult)

            cx.on("vector", step3a)
            cx.on("scalar", lambda s: s.activation(ab, wb, AF.Abs))

            def step3(v):
                v.tensor_single_scalar(ab, ab, 6.0, ALU.min)
                v.tensor_single_scalar(g2b, ab, 2.0, ALU.is_ge)
                v.tensor_single_scalar(sgb, ab, 4.0, ALU.is_ge)
                v.scalar_tensor_tensor(g2b, sgb, 0.5, g2b, ALU.mult, ALU.add)
                v.tensor_scalar(g2b, g2b, -1.0, 2.0, ALU.mult, ALU.add)
                v.tensor_tensor(ib, ab, g2b, ALU.mult)
                v.reciprocal(g2b, g2b)
                v.tensor_tensor(ab, ib, g2b, ALU.mult)
                return v.tensor_single_scalar(sgb, wb, 0.0, ALU.is_ge)

            cx.on("vector", step3)

            def step4(v):
                v.tensor_scalar(sgb, sgb, 2.0, -1.0, ALU.mult, ALU.add)
                av = ab.rearrange("p (b q) -> p b q", q=32)
                v.tensor_tensor(ab, ab, sgb, ALU.mult)
                return v.tensor_tensor(wv, av, scv.broadcast_to([128, nb, 32]),
                                       ALU.mult)

            cx.on("vector", step4)

        cx.scope("PA")
        ar = Arena()
        WQKVT = ar.al("wqkvt", [128, 8 * 3 * DM])
        pa_stage = ar.al("pa_stage", [128, DM])
        pa_xtc = ar.al("pa_xtc", [128, 8 * TB])
        pa_cos = ar.al("pa_cos", [128, S])
        pa_sin = ar.al("pa_sin", [128, S])
        pa_ev = ar.al("pa_ev", [128, TB])
        pa_t1 = ar.al("pa_t1", [128, TB])
        pa_t2 = ar.al("pa_t2", [128, TB])
        pa_vt = ar.al("pa_vt", [128, 512])

        cx.on("sync", lambda e: (e.dma_start(out=pa_cos[:], in_=c_cos[:]),
                                 e.dma_start(out=pa_sin[:], in_=c_sin[:]))[-1],
              dma=True)
        wt_prep_sbuf(w_qkv, 3 * DM, DM, WQKVT, 3 * DM, pa_stage)

        pair_mask = [(i // 2) * 2 + (1 - i % 2) for i in range(32)]
        for c in range(NT):
            for i in range(4):
                cx.on("sync", dma(pa_stage[:, 0:DM],
                                  x_in[c * TB + i * 128:c * TB + (i + 1) * 128, :]),
                      dma=True)

                def trx(t):
                    for kt in range(8):
                        last = t.transpose(pb[kt][:, 0:128],
                                           pa_stage[:, kt * 128:kt * 128 + 128],
                                           ident[:])
                    return last

                def cpx(s, i=i):
                    for kt in range(8):
                        last = s.copy(
                            pa_xtc[:, kt * TB + i * 128:kt * TB + (i + 1) * 128],
                            pb[kt][:, 0:128])
                    return last

                cx.on("tensor", trx)
                cx.on("scalar", cpx)
            cx.on("sync", lambda e, c=c: [
                e.dma_start(out=sXT[pt * 128:(pt + 1) * 128, c * TB:(c + 1) * TB],
                            in_=pa_xtc[:, pt * TB:(pt + 1) * TB])
                for pt in range(8)][-1], dma=True)
            for m in range(24):
                def mm(t, m=m):
                    for k in range(8):
                        last = t.matmul(
                            pb[m % 2][:, 0:TB],
                            WQKVT[:, k * 3 * DM + m * 128:k * 3 * DM + (m + 1) * 128],
                            pa_xtc[:, k * TB:(k + 1) * TB],
                            start=(k == 0), stop=(k == 7), **MM)
                    return last

                cx.on("tensor", mm)
                cx.on("scalar", lambda s, m=m: s.copy(pa_ev[:], pb[m % 2][:, 0:TB]))
                if m < 16:
                    def rope(v, c=c):
                        v.tensor_tensor(pa_t1[:], pa_ev[:],
                                        pa_cos[:, c * TB:(c + 1) * TB], ALU.mult)
                        v.stream_shuffle(pa_t2[:], pa_ev[:], pair_mask)
                        v.tensor_tensor(pa_t2[:], pa_t2[:],
                                        pa_sin[:, c * TB:(c + 1) * TB], ALU.mult)
                        return v.tensor_add(pa_ev[:], pa_t1[:], pa_t2[:])

                    cx.on("vector", rope)
                    qk = sQ if m < 8 else sK
                    hm = (m % 8) * 2
                    cx.on("sync", lambda e, qk=qk, hm=hm, c=c: (
                        e.dma_start(out=qk[hm, :, c * TB:(c + 1) * TB],
                                    in_=pa_ev[0:64, :]),
                        e.dma_start(out=qk[hm + 1, :, c * TB:(c + 1) * TB],
                                    in_=pa_ev[64:128, :]))[-1], dma=True)
                else:
                    h2 = (m - 16) * 2

                    def trv(t):
                        for i in range(4):
                            last = t.transpose(pb[4 + i][:, 0:128],
                                               pa_ev[:, i * 128:(i + 1) * 128],
                                               ident[:])
                        return last

                    def cpv(s):
                        for i in range(4):
                            last = s.copy(pa_vt[:, i * 128:(i + 1) * 128],
                                          pb[4 + i][:, 0:128])
                        return last

                    cx.on("tensor", trv)
                    cx.on("scalar", cpv)

                    def stv(e, h2=h2, c=c):
                        for i in range(4):
                            kcg = c * 4 + i
                            e.dma_start(out=sVT[h2, kcg],
                                        in_=pa_vt[:, i * 128:i * 128 + 64])
                            last = e.dma_start(out=sVT[h2 + 1, kcg],
                                               in_=pa_vt[:, i * 128 + 64:
                                                         (i + 1) * 128])
                        return last

                    cx.on("sync", stv, dma=True)

        ckpt("PA")

        cx.scope("PB")
        ar = Arena()
        WOT = ar.al("wot", [128, 8 * DM])
        pb_stage = ar.al("pb_stage", [128, DM])
        kbuf = ar.al("kbuf", [64, 2 * S])
        qbuf = ar.al("qbuf", [64, 2 * TB])
        vtb = ar.al("vtb", [128, 2 * 32 * 65])
        expT = ar.al("expT", [128, 4 * TB])
        attn_sb = ar.al("attn_sb", [128, 8 * TB])
        rsb = ar.al("rsb", [128, 8 * TB])
        r2t8 = ar.al("r2t8", [128, 8 * TB])
        xres8 = ar.al("xres8", [128, 8 * TB])
        rbc64 = ar.al("rbc64", [64, TB])

        wt_prep_sbuf(w_o, DM, DM, WOT, DM, pb_stage)
        cx.on("vector", lambda v: v.memset(vtb[:], 1.0))

        NPair = NT * NH
        hld = [0] * NPair
        xrt = [0] * NT
        radd_mark = [0] * NT

        def hload(e, i):
            j, h = i // NH, i % NH
            nk = 4 * (j + 1)
            kb = i % 2
            e.dma_start(out=qbuf[:, kb * TB:kb * TB + TB],
                        in_=sQ[h, :, j * TB:(j + 1) * TB])
            e.dma_start(out=kbuf[:, kb * S:kb * S + nk * 128],
                        in_=sK[h, :, 0:nk * 128])
            for kc0 in range(0, nk, 8):
                kn = min(8, nk - kc0)
                e.dma_start(
                    out=vtb.ap().rearrange("p (b kc c) -> p b kc c", b=2, c=65)
                    [:, kb, kc0:kc0 + kn, 0:64],
                    in_=sVT[h, kc0:kc0 + kn].rearrange("kc p c -> p kc c"))

        hld[0] = cx.dma_async(lambda e: hload(e, 0), after=cx.mark())
        hld[1] = cx.dma_async(lambda e: hload(e, 1))
        for j in range(NT):
            nk = 4 * (j + 1)
            if j >= 1:
                xrt[j] = cx.dma_async(lambda e, j=j: [
                    e.dma_start(out=xres8[:, m * TB:(m + 1) * TB],
                                in_=sXT[m * 128:(m + 1) * 128,
                                        j * TB:(j + 1) * TB])
                    for m in range(8)], after=radd_mark[j - 1])
            for h in range(NH):
                i = j * NH + h
                ko, qo, vo = (i % 2) * S, (i % 2) * TB, (i % 2) * 32 * 65
                for kc0 in range(0, nk, 4):
                    def mm_s(t, kc0=kc0, ko=ko, qo=qo):
                        for ii in range(4):
                            last = t.matmul(
                                pb[ii][:, 0:TB],
                                kbuf[:, ko + (kc0 + ii) * 128:
                                     ko + (kc0 + ii + 1) * 128],
                                qbuf[:, qo:qo + TB], start=True, stop=True,
                                **MM)
                        return last

                    cx.on("tensor", mm_s,
                          pre_dma=hld[i] if kc0 == 0 else None)

                    def ex(s):
                        for ii in range(4):
                            last = s.activation(expT[:, ii * TB:(ii + 1) * TB],
                                                pb[ii][:, 0:TB], AF.Exp,
                                                scale=0.125)
                        return last

                    cx.on("scalar", ex)
                    if kc0 == 4 * j:
                        cx.on("vector", lambda v: v.tensor_tensor(
                            expT[:], expT[:], maskt[:], ALU.mult))

                    def mm_av(t, kc0=kc0, nk=nk, vo=vo):
                        for ii in range(4):
                            last = t.matmul(
                                pb[4][0:65, 0:TB],
                                vtb[:, vo + (kc0 + ii) * 65:
                                    vo + (kc0 + ii + 1) * 65],
                                expT[:, ii * TB:(ii + 1) * TB],
                                start=(kc0 + ii == 0),
                                stop=(kc0 + ii == nk - 1), **MM)
                        return last

                    cx.on("tensor", mm_av)
                if i + 2 < NPair:
                    hld[i + 2] = cx.dma_async(
                        lambda e, i=i: hload(e, i + 2), after=cx.mark())
                cx.on("vector", lambda v: v.reciprocal(mrow[:], pb[4][64:65, 0:TB]))
                cx.on("tensor", lambda t: t.matmul(pb[5][0:64, 0:TB],
                                                   ones1r[0:1, 0:64], mrow[:],
                                                   start=True, stop=True, **MM))
                cx.on("scalar", lambda s: s.copy(rbc64[:], pb[5][0:64, 0:TB]))
                pt, sub = h // 2, (h % 2) * 64
                cx.on("vector", lambda v, pt=pt, sub=sub: v.tensor_tensor(
                    attn_sb[sub:sub + 64, pt * TB:(pt + 1) * TB],
                    pb[4][0:64, 0:TB], rbc64[:], ALU.mult))
            if "ATT" in debug:
                cx.on("sync", lambda e, j=j: [
                    e.dma_start(out=sATT[pt * 128:(pt + 1) * 128,
                                         j * TB:(j + 1) * TB],
                                in_=attn_sb[:, pt * TB:(pt + 1) * TB])
                    for pt in range(8)], dma=True)
            if j == 0:
                cx.on("sync", lambda e: [
                    e.dma_start(out=xres8[:, m * TB:(m + 1) * TB],
                                in_=sXT[m * 128:(m + 1) * 128, 0:TB])
                    for m in range(8)][-1], dma=True)

            def mm_o(t):
                for m in range(8):
                    for k in range(8):
                        last = t.matmul(
                            pb[m][:, 0:TB],
                            WOT[:, k * DM + m * 128:k * DM + (m + 1) * 128],
                            attn_sb[:, k * TB:(k + 1) * TB],
                            start=(k == 0), stop=(k == 7), **MM)
                return last

            cx.on("tensor", mm_o)

            def radd(v):
                for m in range(8):
                    last = v.tensor_add(rsb[:, m * TB:(m + 1) * TB],
                                        pb[m][:, 0:TB],
                                        xres8[:, m * TB:(m + 1) * TB])
                return last

            cx.on("vector", radd, pre_dma=xrt[j] if j >= 1 else None)
            radd_mark[j] = cx.mark()

            def rsq(s):
                for m in range(8):
                    last = s.activation(r2t8[:, m * TB:(m + 1) * TB],
                                        rsb[:, m * TB:(m + 1) * TB], AF.Square)
                return last

            cx.on("scalar", rsq)

            def mm_st(t):
                for m in range(8):
                    t.matmul(pb[6][0:1, 0:TB], ones128[:, 0:1],
                             rsb[:, m * TB:(m + 1) * TB], start=(m == 0),
                             stop=(m == 7), **MM)
                    last = t.matmul(pb[7][0:1, 0:TB], ones128[:, 0:1],
                                    r2t8[:, m * TB:(m + 1) * TB], start=(m == 0),
                                    stop=(m == 7), **MM)
                return last

            cx.on("tensor", mm_st)
            emit_ln(rsb, sX1, j, max_slot=0)

        ckpt("PB")

        cx.scope("PC0")
        exchange(0, 0, 0)
        ar = Arena()
        c0_x = ar.al("c0_x", [128, 2 * S])
        c0_i = ar.al("c0_i", [128, S], I32)
        ld_t = [0] * 8
        qmark = [0] * 8
        pc0_st = [0] * 8
        ld_t[0] = cx.dma_async(lambda e: e.dma_start(
            out=c0_x[:, 0:S], in_=sX1[0:128, :]), after=cx.mark())
        for pt in range(8):
            if pt + 1 < 8:
                ld_t[pt + 1] = cx.dma_async(
                    lambda e, pt=pt: e.dma_start(
                        out=c0_x[:, ((pt + 1) % 2) * S:((pt + 1) % 2) * S + S],
                        in_=sX1[(pt + 1) * 128:(pt + 2) * 128, :]),
                    after=qmark[pt - 1] if pt >= 1 else None,
                    after_d2=pc0_st[pt - 1] if pt >= 1 else None)

            def qdq(v, pt=pt):
                b = (pt % 2) * S
                v.tensor_scalar(c0_i[:], c0_x[:, b:b + S], sscal[:, 1:2], None,
                                ALU.mult)
                return v.tensor_scalar(c0_x[:, b:b + S], c0_i[:],
                                       sscal[:, 0:1], None, ALU.mult)

            cx.on("vector", qdq, pre_dma=ld_t[pt])
            qmark[pt] = cx.mark()
            pc0_st[pt] = cx.dma_async(lambda e, pt=pt: e.dma_start(
                out=sXQ[pt * 128:(pt + 1) * 128, :],
                in_=c0_x[:, (pt % 2) * S:(pt % 2) * S + S]), after=qmark[pt])

        ckpt("PC0")

        cx.scope("PC1")
        cx.on("vector", lambda v: v.memset(tmp128[:], 0.0), pre_dma=cx.dn)
        ar = Arena()
        p1_w = ar.al("p1_w", [128, 2 * DM])
        p1_i = ar.al("p1_i", [128, DM], I32)
        p1_am = ar.al("p1_am", [128, 32])
        p1_rs = ar.al("p1_rs", [128, 32])
        xq_sb = ar.al("xq_sb", [128, 8 * S])
        p1_st = ar.al("p1_st", [128, 2 * 8 * 128])
        ev_off = ar.off
        p1_ev = ar.al("p1_ev", [128, 2 * S])
        p1_wt = nc.alloc_sbuf_tensor_at("p1_wt", [128, 2 * 8 * 128], F32,
                                        offset=ARENA0 + ev_off)

        def lpbq_quant(v, wt_, it_, amt, rst, nb):
            wv = wt_.rearrange("p (b q) -> p b q", q=32)
            iv = it_.rearrange("p (b q) -> p b q", q=32)
            amv = amt.rearrange("p (b o) -> p b o", o=1)
            rsv = rst.rearrange("p (b o) -> p b o", o=1)
            v.tensor_reduce(amv, wv, AX.X, ALU.max, apply_absolute_value=True)
            v.tensor_scalar(amt, amt, 1.0 / 7.0, 1e-12, ALU.mult, ALU.max)
            v.reciprocal(rst, amt)
            v.tensor_tensor(iv, wv, rsv.broadcast_to([128, nb, 32]), ALU.mult)
            return v.tensor_tensor(wv, iv, amv.broadcast_to([128, nb, 32]),
                                   ALU.mult)

        def lpbq_prep(src, dstT, n_rows, n_cols, wbuf, ibuf, ambuf, rsbuf,
                      wtbuf, carry=None):
            nmt, nkt, nb = n_rows // 128, n_cols // 128, n_cols // 32
            ld_t = [0] * nmt
            tile_trm = [0] * nmt
            st_t = [0] * nmt
            prev_st = carry if carry else [None, None]
            ld_t[0] = cx.dma_async(lambda e: e.dma_start(
                out=wbuf[:, 0:n_cols], in_=src[0:128, :]), after=cx.mark())
            for m in range(nmt):
                b = (m % 2) * n_cols
                wb = (m % 2) * nkt * 128
                if m + 1 < nmt:
                    ld_t[m + 1] = cx.dma_async(lambda e, m=m: e.dma_start(
                        out=wbuf[:, ((m + 1) % 2) * n_cols:
                                 ((m + 1) % 2) * n_cols + n_cols],
                        in_=src[(m + 1) * 128:(m + 2) * 128, :]),
                        after=tile_trm[m - 1] if m >= 1 else None)
                cx.on("vector", lambda v, b=b, nb=nb, ncc=n_cols: lpbq_quant(
                    v, wbuf[:, b:b + ncc], ibuf[:, 0:ncc], ambuf[:, 0:nb],
                    rsbuf[:, 0:nb], nb), pre_dma=ld_t[m])
                for g in range(nkt // 8):
                    def tr(t, b=b, g=g):
                        for kk in range(8):
                            kt = g * 8 + kk
                            last = t.transpose(
                                pb[kk][:, 0:128],
                                wbuf[:, b + kt * 128:b + kt * 128 + 128],
                                ident[:])
                        return last

                    cx.on("tensor", tr)

                    def cp(s, wb=wb, g=g):
                        for kk in range(8):
                            last = s.copy(
                                wtbuf[:, wb + (g * 8 + kk) * 128:
                                      wb + (g * 8 + kk + 1) * 128],
                                pb[kk][:, 0:128])
                        return last

                    cx.on("scalar", cp,
                          pre_dma=(st_t[m - 2] if m >= 2 else prev_st[m])
                          if g == 0 else None)
                tile_trm[m] = cx.mark()
                st_t[m] = cx.dma_async(
                    lambda e, m=m, wb=wb, nkt=nkt, dstT=dstT: [
                        e.dma_start(
                            out=dstT[kt, :, m * 128:(m + 1) * 128],
                            in_=wtbuf[:, wb + kt * 128:wb + (kt + 1) * 128])
                        for kt in range(nkt)],
                    after=tile_trm[m])
            return [st_t[nmt - 2], st_t[nmt - 1]]

        st1 = lpbq_prep(w_up, sWUT, FF, DM, p1_w, p1_i, p1_am, p1_rs, p1_wt)
        lpbq_prep(w_gate, sWGT, FF, DM, p1_w, p1_i, p1_am, p1_rs, p1_wt,
                  carry=st1)

        cx.on("sync", lambda e: [
            e.dma_start(out=xq_sb[:, pt * S:(pt + 1) * S],
                        in_=sXQ[pt * 128:(pt + 1) * 128, :]) for pt in range(8)][-1],
            dma=True)

        def big_mm(dstD, srcWT, max_slot, carry=None):
            NM = FF // 128
            ld_t = [0] * NM
            mm_mark = [0] * NM
            ev_mark = [0] * NM
            st_t = [0] * NM
            prev_mm, prev_st = carry if carry else (None, [None, None])

            def ld(e, m, srcWT=srcWT):
                b = (m % 2) * 1024
                for k in range(8):
                    e.dma_start(out=p1_st[:, b + k * 128:b + (k + 1) * 128],
                                in_=srcWT[k, :, m * 128:(m + 1) * 128])

            ld_t[0] = cx.dma_async(lambda e: ld(e, 0), after=prev_mm)
            for m in range(NM):
                if m + 1 < NM:
                    anchor = mm_mark[m - 1] if m >= 1 else prev_mm
                    ld_t[m + 1] = cx.dma_async(
                        lambda e, m=m: ld(e, m + 1), after=anchor)

                def mm(t, m=m):
                    b = (m % 2) * 1024
                    for k in range(8):
                        for tt in range(8):
                            last = t.matmul(pb[tt][:, 0:TB],
                                            p1_st[:, b + k * 128:b + (k + 1) * 128],
                                            xq_sb[:, k * S + tt * TB:
                                                  k * S + (tt + 1) * TB],
                                            start=(k == 0), stop=(k == 7), **MM)
                    return last

                cx.on("tensor", mm, pre_dma=ld_t[m])
                mm_mark[m] = cx.mark()

                def evac(s, m=m):
                    b = (m % 2) * S
                    for tt in range(8):
                        last = s.copy(p1_ev[:, b + tt * TB:b + (tt + 1) * TB],
                                      pb[tt][:, 0:TB])
                    return last

                cx.on("scalar", evac,
                      pre_dma=st_t[m - 2] if m >= 2 else prev_st[m])

                def mx(v, m=m):
                    b = (m % 2) * S
                    v.tensor_reduce(tmp128[:], p1_ev[:, b:b + S], AX.X, ALU.max,
                                    apply_absolute_value=True)
                    return v.tensor_tensor(smax[:, max_slot:max_slot + 1],
                                           smax[:, max_slot:max_slot + 1],
                                           tmp128[:], ALU.max)

                cx.on("vector", mx)
                ev_mark[m] = cx.mark()
                st_t[m] = cx.dma_async(
                    lambda e, m=m: e.dma_start(
                        out=dstD[m * 128:(m + 1) * 128, :],
                        in_=p1_ev[:, (m % 2) * S:(m % 2) * S + S]),
                    after=ev_mark[m])
            return (mm_mark[NM - 1], [st_t[NM - 2], st_t[NM - 1]])

        up_tail = big_mm(sUP, sWUT, 1)
        big_mm(sGA, sWGT, 2, carry=up_tail)

        ckpt("PC1")

        cx.scope("PC2")
        exchange(1, 1, 2)
        exchange(2, 2, 4)
        ar = Arena()
        c2_u = ar.al("c2_u", [128, 2 * S])
        c2_g = ar.al("c2_g", [128, 2 * S])
        c2_i = ar.al("c2_i", [128, S], I32)
        c2_s = ar.al("c2_s", [128, S])
        NM2 = FF // 128
        ld_t = [0] * NM2
        q3m = [0] * NM2
        stg1 = [0] * NM2

        def ld2(e, mt):
            b = (mt % 2) * S
            e.dma_start(out=c2_u[:, b:b + S], in_=sUP[mt * 128:(mt + 1) * 128, :])
            e.dma_start(out=c2_g[:, b:b + S], in_=sGA[mt * 128:(mt + 1) * 128, :])

        ld_t[0] = cx.dma_async(lambda e: ld2(e, 0), after_d2=cx.dn)
        for mt in range(NM2):
            b = (mt % 2) * S
            if mt + 1 < NM2:
                ld_t[mt + 1] = cx.dma_async(
                    lambda e, mt=mt: ld2(e, mt + 1),
                    after=q3m[mt - 1] if mt >= 1 else None,
                    after_d2=stg1[mt - 1] if mt >= 1 else None)
            cu, cg = c2_u[:, b:b + S], c2_g[:, b:b + S]

            def q2(v, cu=cu, cg=cg):
                v.tensor_scalar(c2_i[:], cu, sscal[:, 3:4], None, ALU.mult)
                v.tensor_scalar(cu, c2_i[:], sscal[:, 2:3], None, ALU.mult)
                v.tensor_scalar(c2_i[:], cg, sscal[:, 5:6], None, ALU.mult)
                return v.tensor_scalar(cg, c2_i[:], sscal[:, 4:5], None, ALU.mult)

            cx.on("vector", q2, pre_dma=ld_t[mt])
            q2m = cx.mark()
            cx.dma_async(lambda e, mt=mt, cu=cu: e.dma_start(
                out=sUPQ[mt * 128:(mt + 1) * 128, :], in_=cu), after=q2m)
            cx.on("scalar", lambda s, cg=cg: s.activation(c2_s[:], cg, AF.Sigmoid))

            def q3(v, cg=cg):
                v.tensor_scalar(c2_i[:], c2_s[:], 65536.0, None, ALU.mult)
                v.tensor_scalar(c2_s[:], c2_i[:], 65535.0, 1.0 / 65536.0,
                                ALU.min, ALU.mult)
                v.tensor_tensor(cg, cg, c2_s[:], ALU.mult)
                v.tensor_reduce(tmp128[:], cg, AX.X, ALU.max,
                                apply_absolute_value=True)
                return v.tensor_tensor(smax[:, 3:4], smax[:, 3:4], tmp128[:],
                                       ALU.max)

            cx.on("vector", q3)
            q3m[mt] = cx.mark()
            stg1[mt] = cx.dma_async(lambda e, mt=mt, cg=cg: e.dma_start(
                out=sG1[mt * 128:(mt + 1) * 128, :], in_=cg), after=q3m[mt])

        ckpt("PC2")

        cx.scope("PC3")
        exchange(3, 3, 6)
        ld_t = [0] * NM2
        q4m = [0] * NM2
        stpr = [0] * NM2

        def ld3(e, mt):
            b = (mt % 2) * S
            e.dma_start(out=c2_g[:, b:b + S], in_=sG1[mt * 128:(mt + 1) * 128, :])
            e.dma_start(out=c2_u[:, b:b + S], in_=sUPQ[mt * 128:(mt + 1) * 128, :])

        ld_t[0] = cx.dma_async(lambda e: ld3(e, 0), after_d2=cx.dn)
        for mt in range(NM2):
            b = (mt % 2) * S
            if mt + 1 < NM2:
                ld_t[mt + 1] = cx.dma_async(
                    lambda e, mt=mt: ld3(e, mt + 1),
                    after=q4m[mt - 1] if mt >= 1 else None,
                    after_d2=stpr[mt - 1] if mt >= 1 else None)
            cu, cg = c2_u[:, b:b + S], c2_g[:, b:b + S]

            def q4(v, cu=cu, cg=cg):
                v.tensor_scalar(c2_i[:], cg, sscal[:, 7:8], None, ALU.mult)
                v.tensor_scalar(cg, c2_i[:], sscal[:, 6:7], None, ALU.mult)
                v.tensor_tensor(cg, cg, cu, ALU.mult)
                v.tensor_reduce(tmp128[:], cg, AX.X, ALU.max,
                                apply_absolute_value=True)
                return v.tensor_tensor(smax[:, 4:5], smax[:, 4:5], tmp128[:],
                                       ALU.max)

            cx.on("vector", q4, pre_dma=ld_t[mt])
            q4m[mt] = cx.mark()
            stpr[mt] = cx.dma_async(lambda e, mt=mt, cg=cg: e.dma_start(
                out=sPROD[mt * 128:(mt + 1) * 128, :], in_=cg), after=q4m[mt])

        ckpt("PC3")

        cx.scope("PC4")
        exchange(4, 4, 8)
        cx.on("vector", lambda v: v.memset(tmp128[:], 0.0), pre_dma=cx.dn)
        ar = Arena()
        c4_w = ar.al("c4_w", [128, 2 * FF])
        c4_i = ar.al("c4_i", [128, FF], I32)
        c4_am = ar.al("c4_am", [128, 96])
        c4_rs = ar.al("c4_rs", [128, 96])
        st_off = ar.off
        c4_st = ar.al("c4_st", [128, 8 * DM])
        c4_wt = nc.alloc_sbuf_tensor_at("c4_wt", [128, 2 * 24 * 128], F32,
                                        offset=ARENA0 + st_off)
        c4_pq = ar.al("c4_pq", [128, 8 * TB])
        c4_oi = ar.al("c4_oi", [128, TB], I32)
        c4_rsb = ar.al("c4_rsb", [128, 8 * TB])
        c4_r2 = ar.al("c4_r2", [128, 8 * TB])
        c4_xr = ar.al("c4_xr", [128, 8 * TB])

        lpbq_prep(w_down, sWDT, DM, FF, c4_w, c4_i, c4_am, c4_rs, c4_wt)
        d2_c4 = cx.dn

        for tt in range(NT):
            for kg in range(3):
                def ld(e, tt=tt, kg=kg):
                    for kk in range(8):
                        k = kg * 8 + kk
                        e.dma_start(out=c4_st[:, kk * DM:(kk + 1) * DM],
                                    in_=sWDT[k, :, :])
                        last = e.dma_start(
                            out=c4_pq[:, kk * TB:(kk + 1) * TB],
                            in_=sPROD[k * 128:(k + 1) * 128,
                                      tt * TB:(tt + 1) * TB])
                    return last

                cx.on("sync", ld, dma=True,
                      pre_dma=d2_c4 if (tt == 0 and kg == 0) else None)

                def oq(v):
                    for kk in range(8):
                        sl = c4_pq[:, kk * TB:(kk + 1) * TB]
                        v.tensor_scalar(c4_oi[:], sl, sscal[:, 9:10], None,
                                        ALU.mult)
                        last = v.tensor_scalar(sl, c4_oi[:], sscal[:, 8:9], None,
                                               ALU.mult)
                    return last

                cx.on("vector", oq)

                def mm(t, kg=kg):
                    for kk in range(8):
                        k = kg * 8 + kk
                        for m in range(8):
                            last = t.matmul(
                                pb[m][:, 0:TB],
                                c4_st[:, kk * DM + m * 128:kk * DM + (m + 1) * 128],
                                c4_pq[:, kk * TB:(kk + 1) * TB],
                                start=(k == 0), stop=(k == 23), **MM)
                    return last

                cx.on("tensor", mm)
            cx.on("sync", lambda e, tt=tt: [
                e.dma_start(out=c4_xr[:, m * TB:(m + 1) * TB],
                            in_=sX1[m * 128:(m + 1) * 128, tt * TB:(tt + 1) * TB])
                for m in range(8)][-1], dma=True)

            def radd(v):
                for m in range(8):
                    last = v.tensor_add(c4_rsb[:, m * TB:(m + 1) * TB],
                                        pb[m][:, 0:TB],
                                        c4_xr[:, m * TB:(m + 1) * TB])
                return last

            cx.on("vector", radd)

            def rsq(s):
                for m in range(8):
                    last = s.activation(c4_r2[:, m * TB:(m + 1) * TB],
                                        c4_rsb[:, m * TB:(m + 1) * TB], AF.Square)
                return last

            cx.on("scalar", rsq)

            def mm_st(t):
                for m in range(8):
                    t.matmul(pb[6][0:1, 0:TB], ones128[:, 0:1],
                             c4_rsb[:, m * TB:(m + 1) * TB], start=(m == 0),
                             stop=(m == 7), **MM)
                    last = t.matmul(pb[7][0:1, 0:TB], ones128[:, 0:1],
                                    c4_r2[:, m * TB:(m + 1) * TB], start=(m == 0),
                                    stop=(m == 7), **MM)
                return last

            cx.on("tensor", mm_st)
            emit_ln(c4_rsb, sX2, tt)

        ckpt("PC4")

        cx.scope("PD")
        ar = Arena()
        d_wst = ar.al("d_wst", [128, DM])
        d_wt = ar.al("d_wt", [128, 8 * 128])
        x2_sb = ar.al("x2_sb", [128, 8 * S])
        d_st = ar.al("d_st", [128, 2 * 8 * 128])
        d_cb = ar.al("d_cb", [128, S + 3])
        d_acc = ar.al("d_acc", [128, S])
        xpt = ar.al("xpt", [128, 16 * 33])
        xps = ar.al("xps", [33, TB])

        for m in range(2 * DI // 128):
            cx.on("sync", dma(d_wst[:, 0:DM], w_ip[m * 128:(m + 1) * 128, :]),
                  dma=True)
            for kt0 in range(0, 8, 4):
                kts = list(range(kt0, kt0 + 4))

                def tr(t, kts=kts):
                    for kt in kts:
                        last = t.transpose(pb[4 + kt % 4][:, 0:128],
                                           d_wst[:, kt * 128:kt * 128 + 128],
                                           ident[:])
                    return last

                def cp(s, kts=kts):
                    for kt in kts:
                        last = s.copy(d_wt[:, kt * 128:(kt + 1) * 128],
                                      pb[4 + kt % 4][:, 0:128])
                    return last

                cx.on("tensor", tr)
                cx.on("scalar", cp)
            cx.on("sync", lambda e, m=m: [
                e.dma_start(out=sWIPT[kt, :, m * 128:(m + 1) * 128],
                            in_=d_wt[:, kt * 128:(kt + 1) * 128])
                for kt in range(8)][-1], dma=True)

        cx.on("sync", lambda e: [
            e.dma_start(out=x2_sb[:, pt * S:(pt + 1) * S],
                        in_=sX2[pt * 128:(pt + 1) * 128, :]) for pt in range(8)][-1],
            dma=True)

        NMD = 2 * DI // 128
        ld_t = [0] * NMD
        mm_mark = [0] * NMD
        st_t = [0] * NMD

        def ldst(e, m):
            b = (m % 2) * 1024
            for k in range(8):
                e.dma_start(out=d_st[:, b + k * 128:b + (k + 1) * 128],
                            in_=sWIPT[k, :, m * 128:(m + 1) * 128])

        ld_t[0] = cx.dma_async(lambda e: ldst(e, 0))
        for m in range(NMD):
            if m + 1 < NMD:
                ld_t[m + 1] = cx.dma_async(
                    lambda e, m=m: ldst(e, m + 1),
                    after=mm_mark[m - 1] if m >= 1 else None)

            def mm(t, m=m):
                b = (m % 2) * 1024
                for k in range(8):
                    for tt in range(8):
                        last = t.matmul(pb[tt][:, 0:TB],
                                        d_st[:, b + k * 128:b + (k + 1) * 128],
                                        x2_sb[:, k * S + tt * TB:k * S + (tt + 1) * TB],
                                        start=(k == 0), stop=(k == 7), **MM)
                return last

            cx.on("tensor", mm, pre_dma=ld_t[m])
            mm_mark[m] = cx.mark()
            if m < 16:
                db = m

                def evac(s):
                    for tt in range(8):
                        last = s.copy(d_cb[:, 3 + tt * TB:3 + (tt + 1) * TB],
                                      pb[tt][:, 0:TB])
                    return last

                cx.on("scalar", evac,
                      pre_dma=st_t[m - 1] if m >= 1 else None)

                def conv(v, db=db):
                    v.memset(d_cb[:, 0:3], 0.0)
                    v.tensor_scalar(d_acc[:], d_cb[:, 0:S],
                                    pvec[:, 272 + db * 4:273 + db * 4], None,
                                    ALU.mult)
                    for kk in range(1, 4):
                        last = v.scalar_tensor_tensor(
                            d_acc[:], d_cb[:, kk:kk + S],
                            pvec[:, 272 + db * 4 + kk:273 + db * 4 + kk],
                            d_acc[:], ALU.mult, ALU.add)
                    return last

                cx.on("vector", conv)
                cx.on("scalar", lambda s, db=db: s.activation(
                    d_cb[:, 0:S], d_acc[:], AF.Silu,
                    bias=pvec[:, 368 + db:369 + db]))
                st_t[m] = cx.dma_async(lambda e, db=db: e.dma_start(
                    out=sXC[db * 128:(db + 1) * 128, :], in_=d_cb[:, 0:S]),
                    after=cx.mark())
            else:
                db = m - 16

                def zsilu(s):
                    for tt in range(8):
                        last = s.activation(d_acc[:, tt * TB:(tt + 1) * TB],
                                            pb[tt][:, 0:TB], AF.Silu)
                    return last

                cx.on("scalar", zsilu,
                      pre_dma=st_t[m - 1] if m >= 1 else None)
                st_t[m] = cx.dma_async(lambda e, db=db: e.dma_start(
                    out=sSZ[db * 128:(db + 1) * 128, :], in_=d_acc[:, 0:S]),
                    after=cx.mark())

        cx.on("vector", lambda v: v.memset(tmp128[:], 0.0), pre_dma=cx.dn)
        for kt in range(16):
            cx.on("sync", dma(d_wst[0:33, 0:128], w_xp[:, kt * 128:(kt + 1) * 128]),
                  dma=True)
            cx.on("tensor", lambda t: t.transpose(pb[4][0:128, 0:33],
                                                  d_wst[0:33, 0:128],
                                                  ident[0:33, 0:33]))
            cx.on("scalar", lambda s, kt=kt: s.copy(xpt[:, kt * 33:(kt + 1) * 33],
                                                    pb[4][0:128, 0:33]))
        for tt in range(8):
            cx.on("sync", lambda e, tt=tt: [
                e.dma_start(out=d_acc[:, k * 256:(k + 1) * 256],
                            in_=sXC[k * 128:(k + 1) * 128,
                                    tt * TB:tt * TB + 256])
                for k in range(16)][-1], dma=True)
            cx.on("sync", lambda e, tt=tt: [
                e.dma_start(out=d_cb[:, k * 256:(k + 1) * 256],
                            in_=sXC[k * 128:(k + 1) * 128,
                                    tt * TB + 256:(tt + 1) * TB])
                for k in range(16)][-1], dma=True)

            def mmx(t, tt=tt):
                for k in range(16):
                    t.matmul(pb[6][0:33, 0:256], xpt[:, k * 33:(k + 1) * 33],
                             d_acc[:, k * 256:(k + 1) * 256],
                             start=(k == 0), stop=(k == 15), **MM)
                    last = t.matmul(pb[7][0:33, 0:256],
                                    xpt[:, k * 33:(k + 1) * 33],
                                    d_cb[:, k * 256:(k + 1) * 256],
                                    start=(k == 0), stop=(k == 15), **MM)
                return last

            cx.on("tensor", mmx)

            def xpev(s):
                s.copy(xps[:, 0:256], pb[6][0:33, 0:256])
                return s.copy(xps[:, 256:512], pb[7][0:33, 0:256])

            cx.on("scalar", xpev)
            cx.on("sync", lambda e, tt=tt: e.dma_start(
                out=sXP[:, tt * TB:(tt + 1) * TB], in_=xps[:]), dma=True)

        ckpt("PD")

        cx.scope("PE")
        ar = Arena()
        e_bbc = ar.al("e_bbc", [128, 16 * TB])
        e_cbc = ar.al("e_cbc", [128, 16 * TB])
        e_da = ar.al("e_da", [128, 16 * TB])
        e_m = ar.al("e_m", [128, 16 * TB])
        e_dtb = ar.al("e_dtb", [128, TB])
        e_row = ar.al("e_row", [1, 4 * TB])
        e_dt = ar.al("e_dt", [128, TB])
        e_sp1 = ar.al("e_sp1", [128, TB])
        e_u = ar.al("e_u", [128, TB])
        e_xc = ar.al("e_xc", [128, 2 * TB])
        e_sz = ar.al("e_sz", [128, 2 * TB])
        e_y = ar.al("e_y", [128, 2 * TB])
        e_st = ar.al("e_st", [128, 16 * 16])

        cx.on("vector", lambda v: v.memset(e_st[:], 0.0))
        pe_ld = [0] * 16
        pe_gm = [0] * 16
        pe_st = [0] * 16
        for c in range(NT):
            cx.on("sync", lambda e, c=c: e.dma_start(
                out=e_row[0:1, 0:TB], in_=sXP[0:1, c * TB:(c + 1) * TB]), dma=True)
            cx.on("tensor", lambda t: t.matmul(pb[4][:, 0:TB], ones1r[:],
                                               e_row[0:1, 0:TB], start=True,
                                               stop=True, **MM))
            cx.on("scalar", lambda s: s.copy(e_dtb[:], pb[4][:, 0:TB]))
            for grp, dst in ((1, e_bbc), (17, e_cbc)):
                for n0 in range(0, 16, 4):
                    cx.on("sync", lambda e, c=c, grp=grp, n0=n0: e.dma_start(
                        out=e_row.ap().rearrange("p (n t) -> p n t", n=4),
                        in_=sXP[grp + n0:grp + n0 + 4, c * TB:(c + 1) * TB]),
                        dma=True)

                    def bmm(t):
                        for i in range(4):
                            last = t.matmul(pb[i][:, 0:TB], ones1r[:],
                                            e_row[0:1, i * TB:(i + 1) * TB],
                                            start=True, stop=True, **MM)
                        return last

                    def bcp(s, n0=n0, dst=dst):
                        for i in range(4):
                            last = s.copy(dst[:, (n0 + i) * TB:(n0 + i + 1) * TB],
                                          pb[i][:, 0:TB])
                        return last

                    cx.on("tensor", bmm)
                    cx.on("scalar", bcp)
            for db in range(16):
                def ldx(e, db=db, c=c):
                    b = (db % 2) * TB
                    e.dma_start(out=e_xc[:, b:b + TB],
                                in_=sXC[db * 128:(db + 1) * 128,
                                        c * TB:(c + 1) * TB])
                    e.dma_start(out=e_sz[:, b:b + TB],
                                in_=sSZ[db * 128:(db + 1) * 128,
                                        c * TB:(c + 1) * TB])

                if db == 0:
                    pe_ld[0] = cx.dma_async(ldx, after=pe_gm[14] or None)
                if db + 1 < 16:
                    anc = pe_gm[db - 1] if db >= 1 else (pe_gm[15] or None)
                    pe_ld[db + 1] = cx.dma_async(
                        lambda e, db=db, c=c: ldx(e, db + 1, c), after=anc)

                def sp(s, db=db):
                    s.activation(e_sp1[:], e_dtb[:], AF.Abs,
                                 scale=pvec[:, 336 + db:337 + db],
                                 bias=pvec[:, 352 + db:353 + db])
                    s.activation(e_sp1[:], e_sp1[:], AF.Exp, scale=-1.0)
                    s.activation(e_sp1[:], e_sp1[:], AF.Ln, bias=1.0)
                    return s.activation(e_dt[:], e_dtb[:], AF.Relu,
                                        scale=pvec[:, 336 + db:337 + db],
                                        bias=pvec[:, 352 + db:353 + db])

                cx.on("scalar", sp)

                def mkdt(v, db=db):
                    b = (db % 2) * TB
                    v.tensor_add(e_dt[:], e_dt[:], e_sp1[:])
                    return v.tensor_tensor(e_u[:], e_dt[:], e_xc[:, b:b + TB],
                                           ALU.mult)

                cx.on("vector", mkdt, pre_dma=pe_ld[db])

                def dA(s, db=db):
                    for n in range(16):
                        last = s.activation(e_da[:, n * TB:(n + 1) * TB], e_dt[:],
                                            AF.Exp,
                                            scale=pvec[:, db * 16 + n:
                                                       db * 16 + n + 1])
                    return last

                def dbx(v):
                    return v.tensor_tensor(
                        e_m.ap().rearrange("p (n t) -> p n t", n=16),
                        e_u.ap().rearrange("p (o t) -> p o t", o=1)
                        .broadcast_to([128, 16, TB]),
                        e_bbc.ap().rearrange("p (n t) -> p n t", n=16),
                        ALU.mult)

                cx.par([("scalar", dA), ("vector", dbx)])

                def scan(v, db=db):
                    for n in range(16):
                        sl = slice(n * TB, (n + 1) * TB)
                        last = v.tensor_tensor_scan(
                            e_m[:, sl], e_da[:, sl], e_m[:, sl],
                            e_st[:, db * 16 + n:db * 16 + n + 1],
                            ALU.mult, ALU.add)
                    return last

                cx.on("vector", scan)
                cx.on("vector", lambda v, db=db: v.tensor_copy(
                    e_st[:, db * 16:(db + 1) * 16],
                    e_m.ap().rearrange("p (n t) -> p n t", n=16)[:, :, TB - 1:TB]))

                def redv(v):
                    v.tensor_tensor(e_m[:, 0:8 * TB], e_m[:, 0:8 * TB],
                                    e_cbc[:, 0:8 * TB], ALU.mult)
                    v.tensor_tensor(e_m[:, 0:4 * TB], e_m[:, 0:4 * TB],
                                    e_m[:, 4 * TB:8 * TB], ALU.add)
                    v.tensor_tensor(e_m[:, 0:2 * TB], e_m[:, 0:2 * TB],
                                    e_m[:, 2 * TB:4 * TB], ALU.add)
                    return v.tensor_tensor(e_m[:, 0:TB], e_m[:, 0:TB],
                                           e_m[:, TB:2 * TB], ALU.add)

                def redg(g):
                    o = 8 * TB
                    g.tensor_tensor(e_m[:, o:o + 8 * TB], e_m[:, o:o + 8 * TB],
                                    e_cbc[:, o:o + 8 * TB], ALU.mult)
                    g.tensor_tensor(e_m[:, o:o + 4 * TB], e_m[:, o:o + 4 * TB],
                                    e_m[:, o + 4 * TB:o + 8 * TB], ALU.add)
                    g.tensor_tensor(e_m[:, o:o + 2 * TB], e_m[:, o:o + 2 * TB],
                                    e_m[:, o + 2 * TB:o + 4 * TB], ALU.add)
                    return g.tensor_tensor(e_m[:, o:o + TB], e_m[:, o:o + TB],
                                           e_m[:, o + TB:o + 2 * TB], ALU.add)

                cx.par([("vector", redv), ("gpsimd", redg)])

                def gate(v, db=db):
                    b = (db % 2) * TB
                    v.tensor_tensor(e_y[:, b:b + TB], e_m[:, 0:TB],
                                    e_m[:, 8 * TB:9 * TB], ALU.add)
                    v.scalar_tensor_tensor(e_y[:, b:b + TB], e_xc[:, b:b + TB],
                                           pvec[:, 256 + db:257 + db],
                                           e_y[:, b:b + TB], ALU.mult, ALU.add)
                    return v.tensor_tensor(e_y[:, b:b + TB], e_y[:, b:b + TB],
                                           e_sz[:, b:b + TB], ALU.mult)

                cx.on("vector", gate,
                      pre_dma=pe_st[db - 2] if db >= 2
                      else (pe_st[14 + db] or None))
                pe_gm[db] = cx.mark()
                pe_st[db] = cx.dma_async(lambda e, db=db, c=c: e.dma_start(
                    out=sG[db * 128:(db + 1) * 128, c * TB:(c + 1) * TB],
                    in_=e_y[:, (db % 2) * TB:(db % 2) * TB + TB]),
                    after=pe_gm[db])

        ckpt("PE")

        cx.scope("PF")
        cx.on("vector", lambda v: v.memset(tmp128[:], 0.0), pre_dma=cx.dn)
        ar = Arena()
        WOPT = ar.al("wopt", [128, 16 * DM])
        f_wst = ar.al("f_wst", [128, DI])
        f_g16 = ar.al("f_g16", [128, 16 * TB])
        f_rsb = ar.al("f_rsb", [128, 8 * TB])
        f_r2 = ar.al("f_r2", [128, 8 * TB])
        f_xr = ar.al("f_xr", [128, 8 * TB])

        for m in range(8):
            cx.on("sync", dma(f_wst[:, 0:DI], w_op[m * 128:(m + 1) * 128, :]),
                  dma=True)
            for kt0 in range(0, 16, 4):
                kts = list(range(kt0, kt0 + 4))

                def tr(t, kts=kts):
                    for kt in kts:
                        last = t.transpose(pb[4 + kt % 4][:, 0:128],
                                           f_wst[:, kt * 128:kt * 128 + 128],
                                           ident[:])
                    return last

                def cp(s, kts=kts, m=m):
                    for kt in kts:
                        last = s.copy(WOPT[:, kt * DM + m * 128:kt * DM + (m + 1) * 128],
                                      pb[4 + kt % 4][:, 0:128])
                    return last

                cx.on("tensor", tr)
                cx.on("scalar", cp)

        pf_ld = [0] * NT
        pf_rm = [0] * NT
        pf_st = [0] * NT

        def gld(e, tt):
            for k in range(16):
                e.dma_start(
                    out=f_g16[:, k * TB:(k + 1) * TB],
                    in_=sG[k * 128:(k + 1) * 128, tt * TB:(tt + 1) * TB])
            for m in range(8):
                e.dma_start(
                    out=f_xr[:, m * TB:(m + 1) * TB],
                    in_=sX2[m * 128:(m + 1) * 128, tt * TB:(tt + 1) * TB])

        pf_ld[0] = cx.dma_async(lambda e: gld(e, 0))
        for tt in range(NT):
            def mm(t):
                for k in range(16):
                    for m in range(8):
                        last = t.matmul(pb[m][:, 0:TB],
                                        WOPT[:, k * DM + m * 128:k * DM + (m + 1) * 128],
                                        f_g16[:, k * TB:(k + 1) * TB],
                                        start=(k == 0), stop=(k == 15), **MM)
                return last

            cx.on("tensor", mm, pre_dma=pf_ld[tt])
            mmk = cx.mark()

            def radd(v):
                for m in range(8):
                    last = v.tensor_add(f_rsb[:, m * TB:(m + 1) * TB],
                                        pb[m][:, 0:TB],
                                        f_xr[:, m * TB:(m + 1) * TB])
                return last

            cx.on("vector", radd,
                  pre_dma=pf_st[tt - 1] if tt >= 1 else None)
            pf_rm[tt] = cx.mark()
            if tt + 1 < NT:
                pf_ld[tt + 1] = cx.dma_async(
                    lambda e, tt=tt: gld(e, tt + 1), after=pf_rm[tt])

            def rsq(s):
                for m in range(8):
                    last = s.activation(f_r2[:, m * TB:(m + 1) * TB],
                                        f_rsb[:, m * TB:(m + 1) * TB], AF.Square)
                return last

            cx.on("scalar", rsq)

            def mm_st(t):
                for m in range(8):
                    t.matmul(pb[6][0:1, 0:TB], ones128[:, 0:1],
                             f_rsb[:, m * TB:(m + 1) * TB], start=(m == 0),
                             stop=(m == 7), **MM)
                    last = t.matmul(pb[7][0:1, 0:TB], ones128[:, 0:1],
                                    f_r2[:, m * TB:(m + 1) * TB], start=(m == 0),
                                    stop=(m == 7), **MM)
                return last

            cx.on("tensor", mm_st)
            emit_ln(f_rsb, None, tt)
            pf_st[tt] = cx.dma_async(lambda e, tt=tt: [
                e.dma_start(out=sX3[m * 128:(m + 1) * 128,
                                    tt * TB:(tt + 1) * TB],
                            in_=f_rsb[:, m * TB:(m + 1) * TB])
                for m in range(8)], after=cx.mark())

        ckpt("PF")

        cx.scope("PG")
        cx.on("vector", lambda v: v.memset(tmp128[:], 0.0), pre_dma=cx.dn)
        ar = Arena()
        WLT = ar.al("wlt", [128, 8 * DM])
        g_w = ar.al("g_w", [128, DM])
        g_i = ar.al("g_i", [128, S], I32)
        g_am = ar.al("g_am", [128, 128])
        g_sc = ar.al("g_sc", [128, 128])
        g_e = ar.al("g_e", [128, 128], I32)
        g_dw = ar.al("g_dw", [128, 32])
        g_a = ar.al("g_a", [128, S])
        g_g2 = ar.al("g_g2", [128, S])
        g_sg = ar.al("g_sg", [128, S])
        g_x3 = ar.al("g_x3", [128, 8 * TB])
        g_xt = ar.al("g_xt", [128, 4 * DM])
        g_rsb = ar.al("g_rsb", [128, 8 * TB])
        g_r2 = ar.al("g_r2", [128, 8 * TB])

        for m in range(8):
            cx.on("sync", dma(g_dw[:, 0:32], dw_l1[m * 128:(m + 1) * 128, :]),
                  dma=True)
            cx.on("scalar", lambda s: s.activation(g_dw[:, 0:32], g_dw[:, 0:32],
                                                   AF.Tanh, scale=4.0))
            cx.on("scalar", lambda s: s.activation(g_dw[:, 0:32], g_dw[:, 0:32],
                                                   AF.Exp, scale=0.5 * LN2))
            cx.on("sync", dma(g_w[:, 0:DM], w_l1[m * 128:(m + 1) * 128, :]),
                  dma=True)
            mxfp4_emit(g_w[:, 0:DM], g_i[:, 0:DM], g_am[:, 0:32], g_sc[:, 0:32],
                       g_e[:, 0:32], g_dw[:, 0:32], g_a[:, 0:DM], g_g2[:, 0:DM],
                       g_sg[:, 0:DM], DM)

            def trw(t):
                for kt in range(8):
                    last = t.transpose(pb[kt][:, 0:128],
                                       g_w[:, kt * 128:kt * 128 + 128], ident[:])
                return last

            def cpw(s, m=m):
                for kt in range(8):
                    last = s.copy(WLT[:, kt * DM + m * 128:kt * DM + (m + 1) * 128],
                                  pb[kt][:, 0:128])
                return last

            cx.on("tensor", trw)
            cx.on("scalar", cpw)

        def tr8(src_t, src_base, dst_t, dst_base, src_stride, dst_stride):
            def tr(t):
                for kt in range(8):
                    last = t.transpose(
                        pb[kt][:, 0:128],
                        src_t[:, src_base + kt * src_stride:
                              src_base + kt * src_stride + 128], ident[:])
                return last

            def cp(s):
                for kt in range(8):
                    last = s.copy(dst_t[:, dst_base + kt * dst_stride:
                                        dst_base + kt * dst_stride + 128],
                                  pb[kt][:, 0:128])
                return last

            cx.on("tensor", tr)
            cx.on("scalar", cp)

        for c in range(NT):
            cx.on("sync", lambda e, c=c: [
                e.dma_start(out=g_x3[:, pt * TB:(pt + 1) * TB],
                            in_=sX3[pt * 128:(pt + 1) * 128, c * TB:(c + 1) * TB])
                for pt in range(8)][-1], dma=True)
            for i in range(4):
                tr8(g_x3, i * 128, g_xt, i * DM, TB, 128)
            mxfp4_emit(g_xt[:, 0:4 * DM], g_i[:, 0:4 * DM], g_am[:, 0:128],
                       g_sc[:, 0:128], g_e[:, 0:128], None, g_a[:, 0:4 * DM],
                       g_g2[:, 0:4 * DM], g_sg[:, 0:4 * DM], 4 * DM)
            for i in range(4):
                tr8(g_xt, i * DM, g_x3, i * 128, 128, TB)

            def mm(t):
                for k in range(8):
                    for m in range(8):
                        last = t.matmul(pb[m][:, 0:TB],
                                        WLT[:, k * DM + m * 128:k * DM + (m + 1) * 128],
                                        g_x3[:, k * TB:(k + 1) * TB],
                                        start=(k == 0), stop=(k == 7), **MM)
                return last

            cx.on("tensor", mm)
            cx.on("sync", lambda e, c=c: [
                e.dma_start(out=g_r2[:, m * TB:(m + 1) * TB],
                            in_=sX3[m * 128:(m + 1) * 128, c * TB:(c + 1) * TB])
                for m in range(8)][-1], dma=True)

            def radd(v):
                for m in range(8):
                    last = v.scalar_tensor_tensor(
                        g_rsb[:, m * TB:(m + 1) * TB], pb[m][:, 0:TB],
                        pvec[:, 384 + m:385 + m], g_r2[:, m * TB:(m + 1) * TB],
                        ALU.add, ALU.add)
                return last

            cx.on("vector", radd)

            def rsq(s):
                for m in range(8):
                    last = s.activation(g_r2[:, m * TB:(m + 1) * TB],
                                        g_rsb[:, m * TB:(m + 1) * TB], AF.Square)
                return last

            cx.on("scalar", rsq)

            def mm_st(t):
                for m in range(8):
                    t.matmul(pb[6][0:1, 0:TB], ones128[:, 0:1],
                             g_rsb[:, m * TB:(m + 1) * TB], start=(m == 0),
                             stop=(m == 7), **MM)
                    last = t.matmul(pb[7][0:1, 0:TB], ones128[:, 0:1],
                                    g_r2[:, m * TB:(m + 1) * TB], start=(m == 0),
                                    stop=(m == 7), **MM)
                return last

            cx.on("tensor", mm_st)
            emit_ln(g_rsb, None, c)
            for i in range(4):
                tr8(g_rsb, i * 128, g_xt, i * DM, TB, 128)
            cx.on("sync", lambda e, c=c: [
                e.dma_start(out=out_d[c * TB + i * 128:c * TB + (i + 1) * 128, :],
                            in_=g_xt[:, i * DM:(i + 1) * DM]) for i in range(4)][-1],
                dma=True)

        ckpt("PG")

        cx.stopped = False
        for name in debug:
            cx.on("sync", dma(dbg[name], dbg_src[name]), dma=True)

        cx.final_wait("sync")

    auto_end = 16512 + (212863 - nc.sbuf_bytes_remaining)
    assert auto_end <= ARENA0, f"auto sbuf end {auto_end} exceeds ARENA0 {ARENA0}"
    return nc, dbg


_BUILT = {}


def _get_built(debug=(), stop_at=None):
    key = (tuple(debug), stop_at)
    if key not in _BUILT:
        _BUILT[key] = build(debug, stop_at)
    return _BUILT[key]


def kernel(**inputs):
    nc, _ = _get_built(DEBUG, STOP_AT)
    consts = _host_constants()
    x = np.ascontiguousarray(np.asarray(inputs["L0_attn_x"], np.float32))
    base = {
        "w_qkv": np.ascontiguousarray(inputs["L0_attn_W_qkv"], np.float32),
        "w_o": np.ascontiguousarray(inputs["L0_attn_W_o"], np.float32),
        "w_gate": np.ascontiguousarray(inputs["L0_ffn_w_gate"], np.float32),
        "w_up": np.ascontiguousarray(inputs["L0_ffn_w_up"], np.float32),
        "w_down": np.ascontiguousarray(inputs["L0_ffn_w_down"], np.float32),
        "w_ip": np.ascontiguousarray(inputs["L1_attn_in_proj_w"], np.float32),
        "conv_w": np.ascontiguousarray(
            np.asarray(inputs["L1_attn_conv_w"], np.float32).reshape(DI, 4)),
        "conv_b": np.ascontiguousarray(
            np.asarray(inputs["L1_attn_conv_b"], np.float32).reshape(DI, 1)),
        "w_xp": np.ascontiguousarray(inputs["L1_attn_x_proj_w"], np.float32),
        "dt_w": np.ascontiguousarray(
            np.asarray(inputs["L1_attn_dt_proj_w"], np.float32).reshape(DI, 1)),
        "dt_b": np.ascontiguousarray(
            np.asarray(inputs["L1_attn_dt_proj_b"], np.float32).reshape(DI, 1)),
        "a_log": np.ascontiguousarray(inputs["L1_attn_A_log"], np.float32),
        "d_vec": np.ascontiguousarray(
            np.asarray(inputs["L1_attn_D"], np.float32).reshape(DI, 1)),
        "w_op": np.ascontiguousarray(inputs["L1_attn_out_proj_w"], np.float32),
        "w_l1": np.ascontiguousarray(inputs["L1_ffn_weight"], np.float32),
        "b_l1": np.ascontiguousarray(
            np.asarray(inputs["L1_ffn_bias"], np.float32).reshape(DM, 1)),
        "dw_l1": np.ascontiguousarray(inputs["L1_ffn_delta_w"], np.float32),
        "da_l1": np.ascontiguousarray(
            np.asarray(inputs["L1_ffn_delta_act"], np.float32).reshape(1, 1)),
    }
    base.update(consts)
    in_maps = []
    for b in range(NCORES):
        m = dict(base)
        m["x"] = np.ascontiguousarray(x[b])
        in_maps.append(m)
    res = run_bass_kernel_spmd(nc, in_maps, core_ids=list(range(NCORES)))
    out = np.stack([np.asarray(res.results[b]["out"]) for b in range(NCORES)],
                   axis=0)
    if DEBUG:
        kernel._dbg = [{k: np.asarray(res.results[b]["dbg_" + k]) for k in DEBUG}
                       for b in range(NCORES)]
    return out


# version 08
# speedup vs baseline: 1.1796x; 1.0434x over previous
import numpy as np
import concourse.bass as bass
import concourse.mybir as mybir
from concourse.bass_utils import run_bass_kernel_spmd
from contextlib import ExitStack

F32 = mybir.dt.float32
I32 = mybir.dt.int32
AF = mybir.ActivationFunctionType
ALU = mybir.AluOpType
AX = mybir.AxisListType

S = 4096
DM = 1024
FF = 3072
DI = 2048
DSN = 16
NH = 16
DK = 64
TB = 512
NT = S // TB
LN_EPS = 1e-5
QMAX0 = 32767.0
LN2 = float(np.log(2.0))
NCORES = 8

DEBUG = ()
STOP_AT = None


class Ctx:

    def __init__(self, nc, es):
        self.nc = nc
        self.ch = es.enter_context(nc.semaphore("chain"))
        self.d2 = es.enter_context(nc.semaphore("dma2"))
        self.t = 0
        self.dn = 0
        self.block = None
        self.stopped = False
        self._eng = None
        self._dma = False
        self._fns = []
        self._pre = None
        self.scope_name = "P0"

    def scope(self, name):
        self._flush()
        self.scope_name = name

    class _SmallGuard:

        SMALL = 512
        _PASS = ("drain", "wait_ge", "nop", "engine_nop")

        def __init__(self, eng):
            self._e = eng

        def __getattr__(self, name):
            f = getattr(self._e, name)
            if name in Ctx._SmallGuard._PASS or not callable(f):
                return f

            def wrap(*a, **kw):
                r = f(*a, **kw)
                out = a[0] if a else kw.get("out")
                fs = None
                try:
                    fs = out.free_size()
                except Exception:
                    pass
                if fs is not None and fs < Ctx._SmallGuard.SMALL:
                    self._e.drain()
                return r

            return wrap

    class _DmaRec:

        def __init__(self, eng):
            self._eng = eng
            self.insts = []

        def dma_start(self, **kw):
            i = self._eng.dma_start(**kw)
            self.insts.append(i)
            return i

        def wait_ge(self, *a):
            return self._eng.wait_ge(*a)

    def _flush(self):
        if not self._fns:
            return
        dec = getattr(self.block, self._eng)
        t0 = self.t
        fns = self._fns
        self._fns = []
        is_dma = self._dma
        eng_name = self._eng
        n = [0]

        pre = self._pre
        self._pre = None
        scname = self.scope_name

        def body(eng):
          with self.nc.named_scope(scname):
            if pre is not None and pre > 0:
                eng.wait_ge(self.d2, pre)
            if t0 > 0:
                eng.wait_ge(self.ch, t0)
            if is_dma:
                rec = Ctx._DmaRec(eng)
                for fn in fns:
                    fn(rec)
                assert rec.insts
                for i in rec.insts:
                    i.then_inc(self.ch, 16)
                n[0] = 16 * len(rec.insts)
            else:
                geng = eng if eng_name in ("tensor", "sync") \
                    else Ctx._SmallGuard(eng)
                for fn in fns:
                    last = fn(geng)
                assert last is not None, "step fn must return its last instruction"
                last.then_inc(self.ch, 1)
                n[0] = 1

        dec(body)
        assert n[0] > 0
        self.t = t0 + n[0]

    def on(self, eng_name, fn, dma=False, force_wait=False, pre_dma=None):
        if self.stopped:
            return
        if pre_dma is None and self._fns and self._eng == eng_name \
                and self._dma == dma and not force_wait:
            self._fns.append(fn)
            return
        self._flush()
        self._eng = eng_name
        self._dma = dma
        self._fns = [fn]
        self._pre = pre_dma

    def dma_async(self, fn, after=None, after_d2=None):
        if self.stopped:
            return self.dn
        self._flush()
        dec = self.block.sync
        n = [0]
        d2 = self.d2

        scname = self.scope_name

        def body(eng):
          with self.nc.named_scope(scname):
            if after_d2 is not None and after_d2 > 0:
                eng.wait_ge(self.d2, after_d2)
            if after is not None and after > 0:
                eng.wait_ge(self.ch, after)
            rec = Ctx._DmaRec(eng)
            fn(rec)
            assert rec.insts
            for i in rec.insts:
                i.then_inc(d2, 16)
            n[0] = 16 * len(rec.insts)

        dec(body)
        assert n[0] > 0
        self.dn += n[0]
        self._eng = None
        return self.dn

    def par(self, specs):
        if self.stopped:
            return
        self._flush()
        t0 = self.t
        scname = self.scope_name
        for spec in specs:
            eng_name, fn = spec[0], spec[1]
            pre = spec[2] if len(spec) > 2 else None
            dec = getattr(self.block, eng_name)

            def body(eng, fn=fn, pre=pre, eng_name=eng_name):
                with self.nc.named_scope(scname):
                    if pre is not None and pre > 0:
                        eng.wait_ge(self.d2, pre)
                    if t0 > 0:
                        eng.wait_ge(self.ch, t0)
                    geng = eng if eng_name in ("tensor", "sync") \
                        else Ctx._SmallGuard(eng)
                    last = fn(geng)
                    assert last is not None
                    last.then_inc(self.ch, 1)

            dec(body)
        self.t = t0 + len(specs)

    def mark(self):
        self._flush()
        return self.t

    def final_wait(self, eng_name):
        self._flush()
        dec = getattr(self.block, eng_name)
        t0 = self.t

        def body(eng):
            eng.wait_ge(self.ch, t0)

        dec(body)


def _host_constants():
    dk = DK
    inv_freq = (np.float32(10000.0) ** (np.arange(0, dk, 2, dtype=np.float32)
                                        / np.float32(dk))).astype(np.float32)
    ang = (np.arange(S, dtype=np.float32)[:, None] / inv_freq[None, :]).astype(np.float32)
    cos = np.cos(ang).astype(np.float32)
    sin = np.sin(ang).astype(np.float32)
    cos64 = np.repeat(cos.T, 2, axis=0)
    sin64 = np.repeat(sin.T, 2, axis=0)
    cos128 = np.concatenate([cos64, cos64], axis=0)
    sgn = np.where(np.arange(64) % 2 == 0, -1.0, 1.0).astype(np.float32)[:, None]
    sin64s = (sin64 * sgn).astype(np.float32)
    sin128 = np.concatenate([sin64s, sin64s], axis=0)
    masks = []
    for r in range(4):
        kk = np.arange(128)[:, None] + 128 * r
        qq = np.arange(TB)[None, :]
        masks.append((kk <= qq).astype(np.float32))
    return {
        "c_cos": np.ascontiguousarray(cos128),
        "c_sin": np.ascontiguousarray(sin128),
        "c_mask": np.ascontiguousarray(np.concatenate(masks, axis=1)),
        "c_ident": np.eye(128, dtype=np.float32),
    }


def build(debug=(), stop_at=None):
    nc = bass.Bass("TRN2", target_bir_lowering=False, detect_race_conditions=False)
    es = ExitStack()

    def din(name, shape):
        return nc.dram_tensor(name, list(shape), F32, kind="ExternalInput").ap()

    x_in = din("x", (S, DM))
    w_qkv = din("w_qkv", (3 * DM, DM))
    w_o = din("w_o", (DM, DM))
    w_gate = din("w_gate", (FF, DM))
    w_up = din("w_up", (FF, DM))
    w_down = din("w_down", (DM, FF))
    w_ip = din("w_ip", (2 * DI, DM))
    conv_w = din("conv_w", (DI, 4))
    conv_b = din("conv_b", (DI, 1))
    w_xp = din("w_xp", (2 * DSN + 1, DI))
    dt_w = din("dt_w", (DI, 1))
    dt_b = din("dt_b", (DI, 1))
    a_log = din("a_log", (DI, DSN))
    d_vec = din("d_vec", (DI, 1))
    w_op = din("w_op", (DM, DI))
    w_l1 = din("w_l1", (DM, DM))
    b_l1 = din("b_l1", (DM, 1))
    dw_l1 = din("dw_l1", (DM, 32))
    da_l1 = din("da_l1", (1, 1))
    c_cos = din("c_cos", (128, S))
    c_sin = din("c_sin", (128, S))
    c_mask = din("c_mask", (128, 4 * TB))
    c_ident = din("c_ident", (128, 128))

    out_d = nc.dram_tensor("out", [S, DM], F32, kind="ExternalOutput").ap()

    def dscr(name, shape):
        return nc.dram_tensor(name, list(shape), F32).ap()

    sXT = dscr("sXT", (DM, S))
    sQ = dscr("sQ", (NH, DK, S))
    sK = dscr("sK", (NH, DK, S))
    sVT = dscr("sVT", (NH, S // 128, 128, DK))
    sX1 = dscr("sX1", (DM, S))
    sXQ = dscr("sXQ", (DM, S))
    sWUT = dscr("sWUT", (DM // 128, 128, FF))
    sWGT = dscr("sWGT", (DM // 128, 128, FF))
    sWDT = dscr("sWDT", (FF // 128, 128, DM))
    sUP = dscr("sUP", (FF, S))
    sGA = dscr("sGA", (FF, S))
    sUPQ = dscr("sUPQ", (FF, S))
    sG1 = dscr("sG1", (FF, S))
    sPROD = dscr("sPROD", (FF, S))
    sX2 = dscr("sX2", (DM, S))
    sWIPT = dscr("sWIPT", (DM // 128, 128, 2 * DI))
    sXC = dscr("sXC", (DI, S))
    sSZ = dscr("sSZ", (DI, S))
    sG = dscr("sG", (DI, S))
    sX3 = dscr("sX3", (DM, S))
    sXP = dscr("sXP", (33, S))
    sATT = dscr("sATT", (DM, S))
    sSC = dscr("sSC", (128, 4 * TB))
    sEX = dscr("sEX", (128, 4 * TB))
    cc_in = nc.dram_tensor("cc_in", [1, 8], F32).ap()
    cc_outs = [nc.dram_tensor(f"cc_out{i}", [1, 8], F32, addr_space="Shared").ap()
               for i in range(6)]

    dbg_shapes = {"XT": (DM, S), "Q": (NH, DK, S), "K": (NH, DK, S),
                  "VT": (NH, S // 128, 128, DK), "X1": (DM, S), "XQ": (DM, S), "ATT": (DM, S), "SC": (128, 4 * TB), "EX": (128, 4 * TB),
                  "UP": (FF, S), "GA": (FF, S), "UPQ": (FF, S), "G1": (FF, S),
                  "PROD": (FF, S), "X2": (DM, S), "XC": (DI, S), "SZ": (DI, S),
                  "G": (DI, S), "X3": (DM, S), "XP": (33, S)}
    dbg_src = {"XT": sXT, "Q": sQ, "K": sK, "VT": sVT, "X1": sX1, "XQ": sXQ, "ATT": sATT, "SC": sSC, "EX": sEX,
               "UP": sUP, "GA": sGA, "UPQ": sUPQ, "G1": sG1, "PROD": sPROD,
               "X2": sX2, "XC": sXC, "SZ": sSZ, "G": sG, "X3": sX3, "XP": sXP}
    dbg = {}
    for name in debug:
        dbg[name] = nc.dram_tensor("dbg_" + name, list(dbg_shapes[name]), F32,
                                   kind="ExternalOutput").ap()

    ident = nc.alloc_sbuf_tensor("ident", [128, 128], F32)
    pvec = nc.alloc_sbuf_tensor("pvec", [128, 512], F32)
    ones128 = nc.alloc_sbuf_tensor("ones128", [128, 1], F32)
    ones1r = nc.alloc_sbuf_tensor("ones1r", [1, 128], F32)
    smax = nc.alloc_sbuf_tensor("smax", [128, 8], F32)
    srow = nc.alloc_sbuf_tensor("srow", [1, 64], F32)
    sscal = nc.alloc_sbuf_tensor("sscal", [128, 16], F32)
    maskt = nc.alloc_sbuf_tensor("maskt", [128, 4 * TB], F32)
    mrow = nc.alloc_sbuf_tensor("mrow", [1, TB], F32)
    vrow = nc.alloc_sbuf_tensor("vrow", [1, TB], F32)
    m2row = nc.alloc_sbuf_tensor("m2row", [1, TB], F32)
    wrow = nc.alloc_sbuf_tensor("wrow", [1, TB], F32)
    trow = nc.alloc_sbuf_tensor("trow", [1, TB], F32)
    tmp128 = nc.alloc_sbuf_tensor("tmp128", [128, 1], F32)
    constv = nc.alloc_sbuf_tensor("constv", [128, 4], F32)
    ones8 = nc.alloc_sbuf_tensor("ones8", [128, 8], F32)
    sscal8 = nc.alloc_sbuf_tensor("sscal8", [128, 16], F32)

    ARENA0 = 42 * 1024
    ARENA_SZ = 183 * 1024

    class Arena:
        def __init__(self):
            self.off = 0

        def al(self, name, shape, dtype=F32):
            nbytes = int(np.prod(shape[1:])) * mybir.dt.size(dtype)
            nbytes = (nbytes + 31) & ~31
            t = nc.alloc_sbuf_tensor_at(name, list(shape), dtype,
                                        offset=ARENA0 + self.off)
            self.off += nbytes
            assert self.off <= ARENA_SZ, f"arena overflow at {name}: {self.off}"
            return t

    pb = [nc.alloc_psum_tensor(f"pb{i}", [128, TB], F32) for i in range(8)]

    cx = Ctx(nc, es)

    def ckpt(name):
        if stop_at == name:
            cx.stopped = True

    def dma(out, in_):
        return lambda e: e.dma_start(out=out, in_=in_)

    MM = dict(skip_group_check=True)

    with nc.Block("main") as blk:
        cx.block = blk

        def p0_loads(sync):
            sync.dma_start(out=ident[:], in_=c_ident[:])
            sync.dma_start(out=maskt[:], in_=c_mask[:])
            for db in range(16):
                sync.dma_start(out=pvec[:, db * 16:(db + 1) * 16],
                               in_=a_log[db * 128:(db + 1) * 128, :])
                sync.dma_start(out=pvec[:, 256 + db:257 + db],
                               in_=d_vec[db * 128:(db + 1) * 128, :])
                sync.dma_start(out=pvec[:, 272 + db * 4:276 + db * 4],
                               in_=conv_w[db * 128:(db + 1) * 128, :])
                sync.dma_start(out=pvec[:, 336 + db:337 + db],
                               in_=dt_w[db * 128:(db + 1) * 128, :])
                sync.dma_start(out=pvec[:, 352 + db:353 + db],
                               in_=dt_b[db * 128:(db + 1) * 128, :])
                sync.dma_start(out=pvec[:, 368 + db:369 + db],
                               in_=conv_b[db * 128:(db + 1) * 128, :])
            for mb in range(8):
                sync.dma_start(out=pvec[:, 384 + mb:385 + mb],
                               in_=b_l1[mb * 128:(mb + 1) * 128, :])
            return sync.dma_start(out=srow[0:1, 8:9], in_=da_l1[:])

        cx.on("sync", p0_loads, dma=True)

        def p0_memset(v):
            v.memset(ones128[:], 1.0)
            v.memset(ones1r[:], 1.0)
            v.memset(smax[:], 0.0)
            v.memset(constv[:, 0:1], LN_EPS)
            v.memset(ones8[:], 1.0)
            return v.memset(constv[:, 1:2], -2.0 * LN2)

        cx.on("vector", p0_memset)
        cx.on("scalar", lambda s: s.activation(pvec[:, 0:256], pvec[:, 0:256],
                                               AF.Exp))
        cx.on("vector", lambda v: v.tensor_scalar_mul(pvec[:, 0:256],
                                                      pvec[:, 0:256], -1.0))
        cx.on("scalar", lambda s: s.activation(srow[0:1, 9:10], srow[0:1, 8:9],
                                               AF.Tanh, scale=4.0))
        cx.on("scalar", lambda s: s.activation(srow[0:1, 9:10], srow[0:1, 9:10],
                                               AF.Exp, scale=0.5 * LN2))
        cx.on("tensor", lambda t: t.matmul(pb[0][:, 0:1], ones1r[:],
                                           srow[0:1, 9:10], start=True, stop=True,
                                           **MM))
        cx.on("scalar", lambda s: s.copy(sscal[:, 10:11], pb[0][:, 0:1]))
        cx.on("gpsimd", lambda g: g.dma_start(out=cc_in[:], in_=smax[0:1, 0:8]),
              dma=True)
        cx.on("gpsimd", lambda g: g.collective_compute(
            "AllReduce", ALU.max, replica_groups=[list(range(NCORES))],
            ins=[cc_in], outs=[cc_outs[5]]), force_wait=True)

        def wt_prep_sbuf(src, n_rows, n_cols, wt, wt_stride, stage):
            nmt, nkt = n_rows // 128, n_cols // 128
            for m in range(nmt):
                cx.on("sync", dma(stage[:, 0:n_cols], src[m * 128:(m + 1) * 128, :]),
                      dma=True)
                for kt0 in range(0, nkt, 8):
                    kts = list(range(kt0, min(kt0 + 8, nkt)))

                    def tr(t, kts=kts):
                        for kt in kts:
                            last = t.transpose(pb[kt % 8][:, 0:128],
                                               stage[:, kt * 128:kt * 128 + 128],
                                               ident[:])
                        return last

                    def cp(s, kts=kts, m=m):
                        for kt in kts:
                            last = s.copy(wt[:, kt * wt_stride + m * 128:
                                             kt * wt_stride + (m + 1) * 128],
                                          pb[kt % 8][:, 0:128])
                        return last

                    cx.on("tensor", tr)
                    cx.on("scalar", cp)

        def emit_ln(rsb_t, store_dst, jj, max_slot=None):
            def rows(v):
                v.tensor_scalar_mul(mrow[:], pb[6][0:1, 0:TB], 1.0 / DM)
                v.tensor_scalar_mul(vrow[:], pb[7][0:1, 0:TB], 1.0 / DM)
                v.tensor_tensor(m2row[:], mrow[:], mrow[:], ALU.mult)
                v.tensor_tensor(vrow[:], vrow[:], m2row[:], ALU.subtract)
                return v.tensor_copy(m2row[:], vrow[:])

            cx.on("vector", rows)
            cx.on("scalar", lambda s: s.activation(vrow[:], vrow[:], AF.Sqrt,
                                                   bias=constv[0:1, 0:1]))

            def rsqrt_ref(v):
                v.tensor_scalar_add(wrow[:], m2row[:], LN_EPS)
                v.reciprocal(vrow[:], vrow[:])
                for _ in range(2):
                    v.tensor_tensor(trow[:], vrow[:], vrow[:], ALU.mult)
                    v.tensor_tensor(trow[:], trow[:], wrow[:], ALU.mult)
                    v.tensor_scalar(trow[:], trow[:], -0.5, 1.5, ALU.mult,
                                    ALU.add)
                    last = v.tensor_tensor(vrow[:], vrow[:], trow[:], ALU.mult)
                return last

            cx.on("vector", rsqrt_ref)

            def bc(t):
                t.matmul(pb[6][:, 0:TB], ones1r[:], mrow[:], start=True, stop=True,
                         **MM)
                return t.matmul(pb[7][:, 0:TB], ones1r[:], vrow[:], start=True,
                                stop=True, **MM)

            cx.on("tensor", bc)

            def apply(v):
                for m in range(8):
                    sl = rsb_t[:, m * TB:(m + 1) * TB]
                    v.tensor_tensor(sl, sl, pb[6][:, 0:TB], ALU.subtract)
                    last = v.tensor_tensor(sl, sl, pb[7][:, 0:TB], ALU.mult)
                    if max_slot is not None:
                        v.tensor_reduce(tmp128[:], sl, AX.X, ALU.max,
                                        apply_absolute_value=True)
                        last = v.tensor_tensor(smax[:, max_slot:max_slot + 1],
                                               smax[:, max_slot:max_slot + 1],
                                               tmp128[:], ALU.max)
                return last

            cx.on("vector", apply)
            if store_dst is not None:
                cx.on("sync", lambda e: [
                    e.dma_start(out=store_dst[m * 128:(m + 1) * 128,
                                              jj * TB:(jj + 1) * TB],
                                in_=rsb_t[:, m * TB:(m + 1) * TB])
                    for m in range(8)][-1], dma=True)

        def exchange(slot, cc_idx, out_col):
            cx.on("tensor", lambda t: t.transpose(
                pb[0][0:1, 0:128], smax[:, slot:slot + 1], ident[:]))
            cx.on("vector", lambda v: v.tensor_reduce(
                srow[0:1, 0:1], pb[0][0:1, 0:128], AX.X, ALU.max))
            cx.on("gpsimd", lambda g: g.dma_start(out=cc_in[0:1, 0:1],
                                                  in_=srow[0:1, 0:1]), dma=True)
            cx.on("gpsimd", lambda g: g.collective_compute(
                "AllReduce", ALU.max, replica_groups=[list(range(NCORES))],
                ins=[cc_in], outs=[cc_outs[cc_idx]]), force_wait=True)
            cx.on("gpsimd", lambda g: g.dma_start(out=srow[0:1, 0:1],
                                                  in_=cc_outs[cc_idx][0:1, 0:1]),
                  dma=True, force_wait=True)

            cx.on("tensor", lambda t: t.matmul(pb[0][:, 0:1], ones1r[:],
                                               srow[0:1, 0:1], start=True,
                                               stop=True, **MM))
            cx.on("scalar", lambda s: s.copy(sscal[:, 14:15], pb[0][:, 0:1]))

            def sc(v):
                v.tensor_scalar(sscal8[:, 0:8], ones8[:], sscal[:, 14:15], None,
                                ALU.mult)
                v.tensor_scalar(sscal8[:, 0:8], sscal8[:, 0:8], 1.0 / QMAX0,
                                1e-12, ALU.mult, ALU.max)
                v.reciprocal(sscal8[:, 8:16], sscal8[:, 0:8])
                v.tensor_copy(sscal[:, out_col:out_col + 1], sscal8[:, 0:1])
                return v.tensor_copy(sscal[:, out_col + 1:out_col + 2],
                                     sscal8[:, 8:9])

            cx.on("vector", sc)

        def mxfp4_emit(wb, ib, amb, scb, eb, dwf_ap, ab, g2b, sgb, ncols):
            nb = ncols // 32
            wv = wb.rearrange("p (b q) -> p b q", q=32)
            amv = amb.rearrange("p (b o) -> p b o", o=1)
            scv = scb.rearrange("p (b o) -> p b o", o=1)

            def step1(v):
                v.tensor_reduce(amv, wv, AX.X, ALU.max, apply_absolute_value=True)
                return v.tensor_single_scalar(amb, amb, 1e-30, ALU.max)

            cx.on("vector", step1)
            cx.on("scalar", lambda s: s.activation(scb, amb, AF.Ln))

            def step2(v):
                v.tensor_scalar(eb, scb, 1.0 / LN2, -0.5, ALU.mult, ALU.add)
                return v.tensor_copy(scb, eb)

            cx.on("vector", step2)
            cx.on("scalar", lambda s: s.activation(scb, scb, AF.Exp, scale=LN2,
                                                   bias=constv[:, 1:2]))
            if dwf_ap is not None:
                cx.on("vector", lambda v: v.tensor_tensor(scb, scb, dwf_ap,
                                                          ALU.mult))
            else:
                cx.on("vector", lambda v: v.tensor_scalar(scb, scb,
                                                          sscal[:, 10:11], None,
                                                          ALU.mult))

            def step3a(v):
                v.reciprocal(amb, scb)
                return v.tensor_tensor(wv, wv, amv.broadcast_to([128, nb, 32]),
                                       ALU.mult)

            cx.on("vector", step3a)
            cx.on("scalar", lambda s: s.activation(ab, wb, AF.Abs))

            def step3(v):
                v.tensor_single_scalar(ab, ab, 6.0, ALU.min)
                v.tensor_single_scalar(g2b, ab, 2.0, ALU.is_ge)
                v.tensor_single_scalar(sgb, ab, 4.0, ALU.is_ge)
                v.scalar_tensor_tensor(g2b, sgb, 0.5, g2b, ALU.mult, ALU.add)
                v.tensor_scalar(g2b, g2b, -1.0, 2.0, ALU.mult, ALU.add)
                v.tensor_tensor(ib, ab, g2b, ALU.mult)
                v.reciprocal(g2b, g2b)
                v.tensor_tensor(ab, ib, g2b, ALU.mult)
                return v.tensor_single_scalar(sgb, wb, 0.0, ALU.is_ge)

            cx.on("vector", step3)

            def step4(v):
                v.tensor_scalar(sgb, sgb, 2.0, -1.0, ALU.mult, ALU.add)
                av = ab.rearrange("p (b q) -> p b q", q=32)
                v.tensor_tensor(ab, ab, sgb, ALU.mult)
                return v.tensor_tensor(wv, av, scv.broadcast_to([128, nb, 32]),
                                       ALU.mult)

            cx.on("vector", step4)

        cx.scope("PA")
        ar = Arena()
        WQKVT = ar.al("wqkvt", [128, 8 * 3 * DM])
        pa_stage = ar.al("pa_stage", [128, DM])
        pa_xtc = ar.al("pa_xtc", [128, 8 * TB])
        pa_cos = ar.al("pa_cos", [128, S])
        pa_sin = ar.al("pa_sin", [128, S])
        pa_ev = ar.al("pa_ev", [128, TB])
        pa_t1 = ar.al("pa_t1", [128, TB])
        pa_t2 = ar.al("pa_t2", [128, TB])
        pa_vt = ar.al("pa_vt", [128, 512])

        cx.on("sync", lambda e: (e.dma_start(out=pa_cos[:], in_=c_cos[:]),
                                 e.dma_start(out=pa_sin[:], in_=c_sin[:]))[-1],
              dma=True)
        wt_prep_sbuf(w_qkv, 3 * DM, DM, WQKVT, 3 * DM, pa_stage)

        pair_mask = [(i // 2) * 2 + (1 - i % 2) for i in range(32)]
        for c in range(NT):
            for i in range(4):
                cx.on("sync", dma(pa_stage[:, 0:DM],
                                  x_in[c * TB + i * 128:c * TB + (i + 1) * 128, :]),
                      dma=True)

                def trx(t):
                    for kt in range(8):
                        last = t.transpose(pb[kt][:, 0:128],
                                           pa_stage[:, kt * 128:kt * 128 + 128],
                                           ident[:])
                    return last

                def cpx(s, i=i):
                    for kt in range(8):
                        last = s.copy(
                            pa_xtc[:, kt * TB + i * 128:kt * TB + (i + 1) * 128],
                            pb[kt][:, 0:128])
                    return last

                cx.on("tensor", trx)
                cx.on("scalar", cpx)
            cx.on("sync", lambda e, c=c: [
                e.dma_start(out=sXT[pt * 128:(pt + 1) * 128, c * TB:(c + 1) * TB],
                            in_=pa_xtc[:, pt * TB:(pt + 1) * TB])
                for pt in range(8)][-1], dma=True)
            for m in range(24):
                def mm(t, m=m):
                    for k in range(8):
                        last = t.matmul(
                            pb[m % 2][:, 0:TB],
                            WQKVT[:, k * 3 * DM + m * 128:k * 3 * DM + (m + 1) * 128],
                            pa_xtc[:, k * TB:(k + 1) * TB],
                            start=(k == 0), stop=(k == 7), **MM)
                    return last

                cx.on("tensor", mm)
                cx.on("scalar", lambda s, m=m: s.copy(pa_ev[:], pb[m % 2][:, 0:TB]))
                if m < 16:
                    def rope(v, c=c):
                        v.tensor_tensor(pa_t1[:], pa_ev[:],
                                        pa_cos[:, c * TB:(c + 1) * TB], ALU.mult)
                        v.stream_shuffle(pa_t2[:], pa_ev[:], pair_mask)
                        v.tensor_tensor(pa_t2[:], pa_t2[:],
                                        pa_sin[:, c * TB:(c + 1) * TB], ALU.mult)
                        return v.tensor_add(pa_ev[:], pa_t1[:], pa_t2[:])

                    cx.on("vector", rope)
                    qk = sQ if m < 8 else sK
                    hm = (m % 8) * 2
                    cx.on("sync", lambda e, qk=qk, hm=hm, c=c: (
                        e.dma_start(out=qk[hm, :, c * TB:(c + 1) * TB],
                                    in_=pa_ev[0:64, :]),
                        e.dma_start(out=qk[hm + 1, :, c * TB:(c + 1) * TB],
                                    in_=pa_ev[64:128, :]))[-1], dma=True)
                else:
                    h2 = (m - 16) * 2

                    def trv(t):
                        for i in range(4):
                            last = t.transpose(pb[4 + i][:, 0:128],
                                               pa_ev[:, i * 128:(i + 1) * 128],
                                               ident[:])
                        return last

                    def cpv(s):
                        for i in range(4):
                            last = s.copy(pa_vt[:, i * 128:(i + 1) * 128],
                                          pb[4 + i][:, 0:128])
                        return last

                    cx.on("tensor", trv)
                    cx.on("scalar", cpv)

                    def stv(e, h2=h2, c=c):
                        for i in range(4):
                            kcg = c * 4 + i
                            e.dma_start(out=sVT[h2, kcg],
                                        in_=pa_vt[:, i * 128:i * 128 + 64])
                            last = e.dma_start(out=sVT[h2 + 1, kcg],
                                               in_=pa_vt[:, i * 128 + 64:
                                                         (i + 1) * 128])
                        return last

                    cx.on("sync", stv, dma=True)

        ckpt("PA")

        cx.scope("PB")
        ar = Arena()
        WOT = ar.al("wot", [128, 8 * DM])
        pb_stage = ar.al("pb_stage", [128, DM])
        kbuf = ar.al("kbuf", [64, 2 * S])
        qbuf = ar.al("qbuf", [64, 2 * TB])
        vtb = ar.al("vtb", [128, 2 * 32 * 65])
        expT = ar.al("expT", [128, 2 * 4 * TB])
        attn_sb = ar.al("attn_sb", [128, 8 * TB])
        rsb = ar.al("rsb", [128, 8 * TB])
        r2t8 = ar.al("r2t8", [128, 8 * TB])
        xres8 = ar.al("xres8", [128, 8 * TB])
        rbc64 = ar.al("rbc64", [64, TB])

        wt_prep_sbuf(w_o, DM, DM, WOT, DM, pb_stage)
        cx.on("vector", lambda v: v.memset(vtb[:], 1.0))

        NPair = NT * NH
        hld = [0] * NPair
        xrt = [0] * NT
        radd_mark = [0] * NT

        def hload(e, i):
            j, h = i // NH, i % NH
            nk = 4 * (j + 1)
            kb = i % 2
            e.dma_start(out=qbuf[:, kb * TB:kb * TB + TB],
                        in_=sQ[h, :, j * TB:(j + 1) * TB])
            e.dma_start(out=kbuf[:, kb * S:kb * S + nk * 128],
                        in_=sK[h, :, 0:nk * 128])
            for kc0 in range(0, nk, 8):
                kn = min(8, nk - kc0)
                e.dma_start(
                    out=vtb.ap().rearrange("p (b kc c) -> p b kc c", b=2, c=65)
                    [:, kb, kc0:kc0 + kn, 0:64],
                    in_=sVT[h, kc0:kc0 + kn].rearrange("kc p c -> p kc c"))

        hld[0] = cx.dma_async(lambda e: hload(e, 0), after=cx.mark())
        hld[1] = cx.dma_async(lambda e: hload(e, 1))
        for j in range(NT):
            nk = 4 * (j + 1)
            if j >= 1:
                xrt[j] = cx.dma_async(lambda e, j=j: [
                    e.dma_start(out=xres8[:, m * TB:(m + 1) * TB],
                                in_=sXT[m * 128:(m + 1) * 128,
                                        j * TB:(j + 1) * TB])
                    for m in range(8)], after=radd_mark[j - 1])
            for h in range(NH):
                i = j * NH + h
                ko, qo, vo = (i % 2) * S, (i % 2) * TB, (i % 2) * 32 * 65
                ng = nk // 4

                def mm_s(t, g, ko=ko, qo=qo):
                    for ii in range(4):
                        last = t.matmul(
                            pb[ii][:, 0:TB],
                            kbuf[:, ko + (g * 4 + ii) * 128:
                                 ko + (g * 4 + ii + 1) * 128],
                            qbuf[:, qo:qo + TB], start=True, stop=True, **MM)
                    return last

                def ex(s, g):
                    eb = (g % 2) * 4 * TB
                    for ii in range(4):
                        last = s.activation(
                            expT[:, eb + ii * TB:eb + (ii + 1) * TB],
                            pb[ii][:, 0:TB], AF.Exp, scale=0.125)
                    return last

                def mm_av(t, g, nk=nk, vo=vo):
                    eb = (g % 2) * 4 * TB
                    for ii in range(4):
                        last = t.matmul(
                            pb[4][0:65, 0:TB],
                            vtb[:, vo + (g * 4 + ii) * 65:
                                vo + (g * 4 + ii + 1) * 65],
                            expT[:, eb + ii * TB:eb + (ii + 1) * TB],
                            start=(g * 4 + ii == 0),
                            stop=(g * 4 + ii == nk - 1), **MM)
                    return last

                cx.on("tensor", lambda t: mm_s(t, 0), pre_dma=hld[i])
                for g in range(ng):
                    if g == 0:
                        cx.on("scalar", lambda s: ex(s, 0))
                    else:
                        cx.par([("scalar", lambda s, g=g: ex(s, g)),
                                ("tensor", lambda t, g=g: mm_av(t, g - 1))])
                    if g == j:
                        eb = (g % 2) * 4 * TB
                        cx.on("vector", lambda v, eb=eb: v.tensor_tensor(
                            expT[:, eb:eb + 4 * TB], expT[:, eb:eb + 4 * TB],
                            maskt[:], ALU.mult))
                    if g + 1 < ng:
                        cx.on("tensor", lambda t, g=g: mm_s(t, g + 1))
                cx.on("tensor", lambda t: mm_av(t, ng - 1))
                if i + 2 < NPair:
                    hld[i + 2] = cx.dma_async(
                        lambda e, i=i: hload(e, i + 2), after=cx.mark())
                cx.on("vector", lambda v: v.reciprocal(mrow[:], pb[4][64:65, 0:TB]))
                cx.on("tensor", lambda t: t.matmul(pb[5][0:64, 0:TB],
                                                   ones1r[0:1, 0:64], mrow[:],
                                                   start=True, stop=True, **MM))
                cx.on("scalar", lambda s: s.copy(rbc64[:], pb[5][0:64, 0:TB]))
                pt, sub = h // 2, (h % 2) * 64
                cx.on("vector", lambda v, pt=pt, sub=sub: v.tensor_tensor(
                    attn_sb[sub:sub + 64, pt * TB:(pt + 1) * TB],
                    pb[4][0:64, 0:TB], rbc64[:], ALU.mult))
            if "ATT" in debug:
                cx.on("sync", lambda e, j=j: [
                    e.dma_start(out=sATT[pt * 128:(pt + 1) * 128,
                                         j * TB:(j + 1) * TB],
                                in_=attn_sb[:, pt * TB:(pt + 1) * TB])
                    for pt in range(8)], dma=True)
            if j == 0:
                cx.on("sync", lambda e: [
                    e.dma_start(out=xres8[:, m * TB:(m + 1) * TB],
                                in_=sXT[m * 128:(m + 1) * 128, 0:TB])
                    for m in range(8)][-1], dma=True)

            def mm_o(t):
                for m in range(8):
                    for k in range(8):
                        last = t.matmul(
                            pb[m][:, 0:TB],
                            WOT[:, k * DM + m * 128:k * DM + (m + 1) * 128],
                            attn_sb[:, k * TB:(k + 1) * TB],
                            start=(k == 0), stop=(k == 7), **MM)
                return last

            cx.on("tensor", mm_o)

            def radd(v):
                for m in range(8):
                    last = v.tensor_add(rsb[:, m * TB:(m + 1) * TB],
                                        pb[m][:, 0:TB],
                                        xres8[:, m * TB:(m + 1) * TB])
                return last

            cx.on("vector", radd, pre_dma=xrt[j] if j >= 1 else None)
            radd_mark[j] = cx.mark()

            def rsq(s):
                for m in range(8):
                    last = s.activation(r2t8[:, m * TB:(m + 1) * TB],
                                        rsb[:, m * TB:(m + 1) * TB], AF.Square)
                return last

            cx.on("scalar", rsq)

            def mm_st(t):
                for m in range(8):
                    t.matmul(pb[6][0:1, 0:TB], ones128[:, 0:1],
                             rsb[:, m * TB:(m + 1) * TB], start=(m == 0),
                             stop=(m == 7), **MM)
                    last = t.matmul(pb[7][0:1, 0:TB], ones128[:, 0:1],
                                    r2t8[:, m * TB:(m + 1) * TB], start=(m == 0),
                                    stop=(m == 7), **MM)
                return last

            cx.on("tensor", mm_st)
            emit_ln(rsb, sX1, j, max_slot=0)

        ckpt("PB")

        cx.scope("PC0")
        exchange(0, 0, 0)
        ar = Arena()
        c0_x = ar.al("c0_x", [128, 2 * S])
        c0_i = ar.al("c0_i", [128, S], I32)
        ld_t = [0] * 8
        qmark = [0] * 8
        pc0_st = [0] * 8
        ld_t[0] = cx.dma_async(lambda e: e.dma_start(
            out=c0_x[:, 0:S], in_=sX1[0:128, :]), after=cx.mark())
        for pt in range(8):
            if pt + 1 < 8:
                ld_t[pt + 1] = cx.dma_async(
                    lambda e, pt=pt: e.dma_start(
                        out=c0_x[:, ((pt + 1) % 2) * S:((pt + 1) % 2) * S + S],
                        in_=sX1[(pt + 1) * 128:(pt + 2) * 128, :]),
                    after=qmark[pt - 1] if pt >= 1 else None,
                    after_d2=pc0_st[pt - 1] if pt >= 1 else None)

            def qdq(v, pt=pt):
                b = (pt % 2) * S
                v.tensor_scalar(c0_i[:], c0_x[:, b:b + S], sscal[:, 1:2], None,
                                ALU.mult)
                return v.tensor_scalar(c0_x[:, b:b + S], c0_i[:],
                                       sscal[:, 0:1], None, ALU.mult)

            cx.on("vector", qdq, pre_dma=ld_t[pt])
            qmark[pt] = cx.mark()
            pc0_st[pt] = cx.dma_async(lambda e, pt=pt: e.dma_start(
                out=sXQ[pt * 128:(pt + 1) * 128, :],
                in_=c0_x[:, (pt % 2) * S:(pt % 2) * S + S]), after=qmark[pt])

        ckpt("PC0")

        cx.scope("PC1")
        cx.on("vector", lambda v: v.memset(tmp128[:], 0.0), pre_dma=cx.dn)
        ar = Arena()
        p1_w = ar.al("p1_w", [128, 2 * DM])
        p1_i = ar.al("p1_i", [128, DM], I32)
        p1_am = ar.al("p1_am", [128, 32])
        p1_rs = ar.al("p1_rs", [128, 32])
        xq_sb = ar.al("xq_sb", [128, 8 * S])
        p1_st = ar.al("p1_st", [128, 2 * 8 * 128])
        ev_off = ar.off
        p1_ev = ar.al("p1_ev", [128, 2 * S])
        p1_wt = nc.alloc_sbuf_tensor_at("p1_wt", [128, 2 * 8 * 128], F32,
                                        offset=ARENA0 + ev_off)

        def lpbq_quant(v, wt_, it_, amt, rst, nb):
            wv = wt_.rearrange("p (b q) -> p b q", q=32)
            iv = it_.rearrange("p (b q) -> p b q", q=32)
            amv = amt.rearrange("p (b o) -> p b o", o=1)
            rsv = rst.rearrange("p (b o) -> p b o", o=1)
            v.tensor_reduce(amv, wv, AX.X, ALU.max, apply_absolute_value=True)
            v.tensor_scalar(amt, amt, 1.0 / 7.0, 1e-12, ALU.mult, ALU.max)
            v.reciprocal(rst, amt)
            v.tensor_tensor(iv, wv, rsv.broadcast_to([128, nb, 32]), ALU.mult)
            return v.tensor_tensor(wv, iv, amv.broadcast_to([128, nb, 32]),
                                   ALU.mult)

        def lpbq_prep(src, dstT, n_rows, n_cols, wbuf, ibuf, ambuf, rsbuf,
                      wtbuf, carry=None):
            nmt, nkt, nb = n_rows // 128, n_cols // 128, n_cols // 32
            ld_t = [0] * nmt
            tile_trm = [0] * nmt
            st_t = [0] * nmt
            prev_st = carry if carry else [None, None]
            ld_t[0] = cx.dma_async(lambda e: e.dma_start(
                out=wbuf[:, 0:n_cols], in_=src[0:128, :]), after=cx.mark())
            for m in range(nmt):
                b = (m % 2) * n_cols
                wb = (m % 2) * nkt * 128
                if m + 1 < nmt:
                    ld_t[m + 1] = cx.dma_async(lambda e, m=m: e.dma_start(
                        out=wbuf[:, ((m + 1) % 2) * n_cols:
                                 ((m + 1) % 2) * n_cols + n_cols],
                        in_=src[(m + 1) * 128:(m + 2) * 128, :]),
                        after=tile_trm[m - 1] if m >= 1 else None)
                cx.on("vector", lambda v, b=b, nb=nb, ncc=n_cols: lpbq_quant(
                    v, wbuf[:, b:b + ncc], ibuf[:, 0:ncc], ambuf[:, 0:nb],
                    rsbuf[:, 0:nb], nb), pre_dma=ld_t[m])
                for g in range(nkt // 8):
                    def tr(t, b=b, g=g):
                        for kk in range(8):
                            kt = g * 8 + kk
                            last = t.transpose(
                                pb[kk][:, 0:128],
                                wbuf[:, b + kt * 128:b + kt * 128 + 128],
                                ident[:])
                        return last

                    cx.on("tensor", tr)

                    def cp(s, wb=wb, g=g):
                        for kk in range(8):
                            last = s.copy(
                                wtbuf[:, wb + (g * 8 + kk) * 128:
                                      wb + (g * 8 + kk + 1) * 128],
                                pb[kk][:, 0:128])
                        return last

                    cx.on("scalar", cp,
                          pre_dma=(st_t[m - 2] if m >= 2 else prev_st[m])
                          if g == 0 else None)
                tile_trm[m] = cx.mark()
                st_t[m] = cx.dma_async(
                    lambda e, m=m, wb=wb, nkt=nkt, dstT=dstT: [
                        e.dma_start(
                            out=dstT[kt, :, m * 128:(m + 1) * 128],
                            in_=wtbuf[:, wb + kt * 128:wb + (kt + 1) * 128])
                        for kt in range(nkt)],
                    after=tile_trm[m])
            return [st_t[nmt - 2], st_t[nmt - 1]]

        st1 = lpbq_prep(w_up, sWUT, FF, DM, p1_w, p1_i, p1_am, p1_rs, p1_wt)
        lpbq_prep(w_gate, sWGT, FF, DM, p1_w, p1_i, p1_am, p1_rs, p1_wt,
                  carry=st1)

        cx.on("sync", lambda e: [
            e.dma_start(out=xq_sb[:, pt * S:(pt + 1) * S],
                        in_=sXQ[pt * 128:(pt + 1) * 128, :]) for pt in range(8)][-1],
            dma=True)

        def big_mm(dstD, srcWT, max_slot, carry=None):
            NM = FF // 128
            ld_t = [0] * NM
            mm_mark = [0] * NM
            ev_mark = [0] * NM
            st_t = [0] * NM
            prev_mm, prev_st = carry if carry else (None, [None, None])

            def ld(e, m, srcWT=srcWT):
                b = (m % 2) * 1024
                for k in range(8):
                    e.dma_start(out=p1_st[:, b + k * 128:b + (k + 1) * 128],
                                in_=srcWT[k, :, m * 128:(m + 1) * 128])

            ld_t[0] = cx.dma_async(lambda e: ld(e, 0), after=prev_mm)
            for m in range(NM):
                if m + 1 < NM:
                    anchor = mm_mark[m - 1] if m >= 1 else prev_mm
                    ld_t[m + 1] = cx.dma_async(
                        lambda e, m=m: ld(e, m + 1), after=anchor)

                def mm(t, m=m):
                    b = (m % 2) * 1024
                    for k in range(8):
                        for tt in range(8):
                            last = t.matmul(pb[tt][:, 0:TB],
                                            p1_st[:, b + k * 128:b + (k + 1) * 128],
                                            xq_sb[:, k * S + tt * TB:
                                                  k * S + (tt + 1) * TB],
                                            start=(k == 0), stop=(k == 7), **MM)
                    return last

                cx.on("tensor", mm, pre_dma=ld_t[m])
                mm_mark[m] = cx.mark()

                def evac(s, m=m):
                    b = (m % 2) * S
                    for tt in range(8):
                        last = s.copy(p1_ev[:, b + tt * TB:b + (tt + 1) * TB],
                                      pb[tt][:, 0:TB])
                    return last

                cx.on("scalar", evac,
                      pre_dma=st_t[m - 2] if m >= 2 else prev_st[m])

                def mx(v, m=m):
                    b = (m % 2) * S
                    v.tensor_reduce(tmp128[:], p1_ev[:, b:b + S], AX.X, ALU.max,
                                    apply_absolute_value=True)
                    return v.tensor_tensor(smax[:, max_slot:max_slot + 1],
                                           smax[:, max_slot:max_slot + 1],
                                           tmp128[:], ALU.max)

                cx.on("vector", mx)
                ev_mark[m] = cx.mark()
                st_t[m] = cx.dma_async(
                    lambda e, m=m: e.dma_start(
                        out=dstD[m * 128:(m + 1) * 128, :],
                        in_=p1_ev[:, (m % 2) * S:(m % 2) * S + S]),
                    after=ev_mark[m])
            return (mm_mark[NM - 1], [st_t[NM - 2], st_t[NM - 1]])

        up_tail = big_mm(sUP, sWUT, 1)
        big_mm(sGA, sWGT, 2, carry=up_tail)

        ckpt("PC1")

        cx.scope("PC2")
        exchange(1, 1, 2)
        exchange(2, 2, 4)
        ar = Arena()
        c2_u = ar.al("c2_u", [128, 2 * S])
        c2_g = ar.al("c2_g", [128, 2 * S])
        c2_i = ar.al("c2_i", [128, S], I32)
        c2_s = ar.al("c2_s", [128, S])
        NM2 = FF // 128
        ld_t = [0] * NM2
        q3m = [0] * NM2
        stg1 = [0] * NM2

        def ld2(e, mt):
            b = (mt % 2) * S
            e.dma_start(out=c2_u[:, b:b + S], in_=sUP[mt * 128:(mt + 1) * 128, :])
            e.dma_start(out=c2_g[:, b:b + S], in_=sGA[mt * 128:(mt + 1) * 128, :])

        ld_t[0] = cx.dma_async(lambda e: ld2(e, 0), after_d2=cx.dn)
        for mt in range(NM2):
            b = (mt % 2) * S
            if mt + 1 < NM2:
                ld_t[mt + 1] = cx.dma_async(
                    lambda e, mt=mt: ld2(e, mt + 1),
                    after=q3m[mt - 1] if mt >= 1 else None,
                    after_d2=stg1[mt - 1] if mt >= 1 else None)
            cu, cg = c2_u[:, b:b + S], c2_g[:, b:b + S]

            def q2(v, cu=cu, cg=cg):
                v.tensor_scalar(c2_i[:], cu, sscal[:, 3:4], None, ALU.mult)
                v.tensor_scalar(cu, c2_i[:], sscal[:, 2:3], None, ALU.mult)
                v.tensor_scalar(c2_i[:], cg, sscal[:, 5:6], None, ALU.mult)
                return v.tensor_scalar(cg, c2_i[:], sscal[:, 4:5], None, ALU.mult)

            cx.on("vector", q2, pre_dma=ld_t[mt])
            q2m = cx.mark()
            cx.dma_async(lambda e, mt=mt, cu=cu: e.dma_start(
                out=sUPQ[mt * 128:(mt + 1) * 128, :], in_=cu), after=q2m)
            cx.on("scalar", lambda s, cg=cg: s.activation(c2_s[:], cg, AF.Sigmoid))

            def q3(v, cg=cg):
                v.tensor_scalar(c2_i[:], c2_s[:], 65536.0, None, ALU.mult)
                v.tensor_scalar(c2_s[:], c2_i[:], 65535.0, 1.0 / 65536.0,
                                ALU.min, ALU.mult)
                v.tensor_tensor(cg, cg, c2_s[:], ALU.mult)
                v.tensor_reduce(tmp128[:], cg, AX.X, ALU.max,
                                apply_absolute_value=True)
                return v.tensor_tensor(smax[:, 3:4], smax[:, 3:4], tmp128[:],
                                       ALU.max)

            cx.on("vector", q3)
            q3m[mt] = cx.mark()
            stg1[mt] = cx.dma_async(lambda e, mt=mt, cg=cg: e.dma_start(
                out=sG1[mt * 128:(mt + 1) * 128, :], in_=cg), after=q3m[mt])

        ckpt("PC2")

        cx.scope("PC3")
        exchange(3, 3, 6)
        ld_t = [0] * NM2
        q4m = [0] * NM2
        stpr = [0] * NM2

        def ld3(e, mt):
            b = (mt % 2) * S
            e.dma_start(out=c2_g[:, b:b + S], in_=sG1[mt * 128:(mt + 1) * 128, :])
            e.dma_start(out=c2_u[:, b:b + S], in_=sUPQ[mt * 128:(mt + 1) * 128, :])

        ld_t[0] = cx.dma_async(lambda e: ld3(e, 0), after_d2=cx.dn)
        for mt in range(NM2):
            b = (mt % 2) * S
            if mt + 1 < NM2:
                ld_t[mt + 1] = cx.dma_async(
                    lambda e, mt=mt: ld3(e, mt + 1),
                    after=q4m[mt - 1] if mt >= 1 else None,
                    after_d2=stpr[mt - 1] if mt >= 1 else None)
            cu, cg = c2_u[:, b:b + S], c2_g[:, b:b + S]

            def q4(v, cu=cu, cg=cg):
                v.tensor_scalar(c2_i[:], cg, sscal[:, 7:8], None, ALU.mult)
                v.tensor_scalar(cg, c2_i[:], sscal[:, 6:7], None, ALU.mult)
                v.tensor_tensor(cg, cg, cu, ALU.mult)
                v.tensor_reduce(tmp128[:], cg, AX.X, ALU.max,
                                apply_absolute_value=True)
                return v.tensor_tensor(smax[:, 4:5], smax[:, 4:5], tmp128[:],
                                       ALU.max)

            cx.on("vector", q4, pre_dma=ld_t[mt])
            q4m[mt] = cx.mark()
            stpr[mt] = cx.dma_async(lambda e, mt=mt, cg=cg: e.dma_start(
                out=sPROD[mt * 128:(mt + 1) * 128, :], in_=cg), after=q4m[mt])

        ckpt("PC3")

        cx.scope("PC4")
        exchange(4, 4, 8)
        cx.on("vector", lambda v: v.memset(tmp128[:], 0.0), pre_dma=cx.dn)
        ar = Arena()
        c4_w = ar.al("c4_w", [128, 2 * FF])
        c4_i = ar.al("c4_i", [128, FF], I32)
        c4_am = ar.al("c4_am", [128, 96])
        c4_rs = ar.al("c4_rs", [128, 96])
        st_off = ar.off
        c4_st = ar.al("c4_st", [128, 8 * DM])
        c4_wt = nc.alloc_sbuf_tensor_at("c4_wt", [128, 2 * 24 * 128], F32,
                                        offset=ARENA0 + st_off)
        c4_pq = ar.al("c4_pq", [128, 8 * TB])
        c4_oi = ar.al("c4_oi", [128, TB], I32)
        c4_rsb = ar.al("c4_rsb", [128, 8 * TB])
        c4_r2 = ar.al("c4_r2", [128, 8 * TB])
        c4_xr = ar.al("c4_xr", [128, 8 * TB])

        lpbq_prep(w_down, sWDT, DM, FF, c4_w, c4_i, c4_am, c4_rs, c4_wt)
        d2_c4 = cx.dn

        for tt in range(NT):
            for kg in range(3):
                def ld(e, tt=tt, kg=kg):
                    for kk in range(8):
                        k = kg * 8 + kk
                        e.dma_start(out=c4_st[:, kk * DM:(kk + 1) * DM],
                                    in_=sWDT[k, :, :])
                        last = e.dma_start(
                            out=c4_pq[:, kk * TB:(kk + 1) * TB],
                            in_=sPROD[k * 128:(k + 1) * 128,
                                      tt * TB:(tt + 1) * TB])
                    return last

                cx.on("sync", ld, dma=True,
                      pre_dma=d2_c4 if (tt == 0 and kg == 0) else None)

                def oq(v):
                    for kk in range(8):
                        sl = c4_pq[:, kk * TB:(kk + 1) * TB]
                        v.tensor_scalar(c4_oi[:], sl, sscal[:, 9:10], None,
                                        ALU.mult)
                        last = v.tensor_scalar(sl, c4_oi[:], sscal[:, 8:9], None,
                                               ALU.mult)
                    return last

                cx.on("vector", oq)

                def mm(t, kg=kg):
                    for kk in range(8):
                        k = kg * 8 + kk
                        for m in range(8):
                            last = t.matmul(
                                pb[m][:, 0:TB],
                                c4_st[:, kk * DM + m * 128:kk * DM + (m + 1) * 128],
                                c4_pq[:, kk * TB:(kk + 1) * TB],
                                start=(k == 0), stop=(k == 23), **MM)
                    return last

                cx.on("tensor", mm)
            cx.on("sync", lambda e, tt=tt: [
                e.dma_start(out=c4_xr[:, m * TB:(m + 1) * TB],
                            in_=sX1[m * 128:(m + 1) * 128, tt * TB:(tt + 1) * TB])
                for m in range(8)][-1], dma=True)

            def radd(v):
                for m in range(8):
                    last = v.tensor_add(c4_rsb[:, m * TB:(m + 1) * TB],
                                        pb[m][:, 0:TB],
                                        c4_xr[:, m * TB:(m + 1) * TB])
                return last

            cx.on("vector", radd)

            def rsq(s):
                for m in range(8):
                    last = s.activation(c4_r2[:, m * TB:(m + 1) * TB],
                                        c4_rsb[:, m * TB:(m + 1) * TB], AF.Square)
                return last

            cx.on("scalar", rsq)

            def mm_st(t):
                for m in range(8):
                    t.matmul(pb[6][0:1, 0:TB], ones128[:, 0:1],
                             c4_rsb[:, m * TB:(m + 1) * TB], start=(m == 0),
                             stop=(m == 7), **MM)
                    last = t.matmul(pb[7][0:1, 0:TB], ones128[:, 0:1],
                                    c4_r2[:, m * TB:(m + 1) * TB], start=(m == 0),
                                    stop=(m == 7), **MM)
                return last

            cx.on("tensor", mm_st)
            emit_ln(c4_rsb, sX2, tt)

        ckpt("PC4")

        cx.scope("PD")
        ar = Arena()
        d_wst = ar.al("d_wst", [128, DM])
        d_wt = ar.al("d_wt", [128, 8 * 128])
        x2_sb = ar.al("x2_sb", [128, 8 * S])
        d_st = ar.al("d_st", [128, 2 * 8 * 128])
        d_cb = ar.al("d_cb", [128, S + 3])
        d_acc = ar.al("d_acc", [128, S])
        xpt = ar.al("xpt", [128, 16 * 33])
        xps = ar.al("xps", [33, TB])

        for m in range(2 * DI // 128):
            cx.on("sync", dma(d_wst[:, 0:DM], w_ip[m * 128:(m + 1) * 128, :]),
                  dma=True)
            for kt0 in range(0, 8, 4):
                kts = list(range(kt0, kt0 + 4))

                def tr(t, kts=kts):
                    for kt in kts:
                        last = t.transpose(pb[4 + kt % 4][:, 0:128],
                                           d_wst[:, kt * 128:kt * 128 + 128],
                                           ident[:])
                    return last

                def cp(s, kts=kts):
                    for kt in kts:
                        last = s.copy(d_wt[:, kt * 128:(kt + 1) * 128],
                                      pb[4 + kt % 4][:, 0:128])
                    return last

                cx.on("tensor", tr)
                cx.on("scalar", cp)
            cx.on("sync", lambda e, m=m: [
                e.dma_start(out=sWIPT[kt, :, m * 128:(m + 1) * 128],
                            in_=d_wt[:, kt * 128:(kt + 1) * 128])
                for kt in range(8)][-1], dma=True)

        cx.on("sync", lambda e: [
            e.dma_start(out=x2_sb[:, pt * S:(pt + 1) * S],
                        in_=sX2[pt * 128:(pt + 1) * 128, :]) for pt in range(8)][-1],
            dma=True)

        NMD = 2 * DI // 128
        ld_t = [0] * NMD
        mm_mark = [0] * NMD
        st_t = [0] * NMD

        def ldst(e, m):
            b = (m % 2) * 1024
            for k in range(8):
                e.dma_start(out=d_st[:, b + k * 128:b + (k + 1) * 128],
                            in_=sWIPT[k, :, m * 128:(m + 1) * 128])

        ld_t[0] = cx.dma_async(lambda e: ldst(e, 0))
        for m in range(NMD):
            if m + 1 < NMD:
                ld_t[m + 1] = cx.dma_async(
                    lambda e, m=m: ldst(e, m + 1),
                    after=mm_mark[m - 1] if m >= 1 else None)

            def mm(t, m=m):
                b = (m % 2) * 1024
                for k in range(8):
                    for tt in range(8):
                        last = t.matmul(pb[tt][:, 0:TB],
                                        d_st[:, b + k * 128:b + (k + 1) * 128],
                                        x2_sb[:, k * S + tt * TB:k * S + (tt + 1) * TB],
                                        start=(k == 0), stop=(k == 7), **MM)
                return last

            cx.on("tensor", mm, pre_dma=ld_t[m])
            mm_mark[m] = cx.mark()
            if m < 16:
                db = m

                def evac(s):
                    for tt in range(8):
                        last = s.copy(d_cb[:, 3 + tt * TB:3 + (tt + 1) * TB],
                                      pb[tt][:, 0:TB])
                    return last

                cx.on("scalar", evac,
                      pre_dma=st_t[m - 1] if m >= 1 else None)

                def conv(v, db=db):
                    v.memset(d_cb[:, 0:3], 0.0)
                    v.tensor_scalar(d_acc[:], d_cb[:, 0:S],
                                    pvec[:, 272 + db * 4:273 + db * 4], None,
                                    ALU.mult)
                    for kk in range(1, 4):
                        last = v.scalar_tensor_tensor(
                            d_acc[:], d_cb[:, kk:kk + S],
                            pvec[:, 272 + db * 4 + kk:273 + db * 4 + kk],
                            d_acc[:], ALU.mult, ALU.add)
                    return last

                cx.on("vector", conv)
                cx.on("scalar", lambda s, db=db: s.activation(
                    d_cb[:, 0:S], d_acc[:], AF.Silu,
                    bias=pvec[:, 368 + db:369 + db]))
                st_t[m] = cx.dma_async(lambda e, db=db: e.dma_start(
                    out=sXC[db * 128:(db + 1) * 128, :], in_=d_cb[:, 0:S]),
                    after=cx.mark())
            else:
                db = m - 16

                def zsilu(s):
                    for tt in range(8):
                        last = s.activation(d_acc[:, tt * TB:(tt + 1) * TB],
                                            pb[tt][:, 0:TB], AF.Silu)
                    return last

                cx.on("scalar", zsilu,
                      pre_dma=st_t[m - 1] if m >= 1 else None)
                st_t[m] = cx.dma_async(lambda e, db=db: e.dma_start(
                    out=sSZ[db * 128:(db + 1) * 128, :], in_=d_acc[:, 0:S]),
                    after=cx.mark())

        cx.on("vector", lambda v: v.memset(tmp128[:], 0.0), pre_dma=cx.dn)
        for kt in range(16):
            cx.on("sync", dma(d_wst[0:33, 0:128], w_xp[:, kt * 128:(kt + 1) * 128]),
                  dma=True)
            cx.on("tensor", lambda t: t.transpose(pb[4][0:128, 0:33],
                                                  d_wst[0:33, 0:128],
                                                  ident[0:33, 0:33]))
            cx.on("scalar", lambda s, kt=kt: s.copy(xpt[:, kt * 33:(kt + 1) * 33],
                                                    pb[4][0:128, 0:33]))
        for tt in range(8):
            cx.on("sync", lambda e, tt=tt: [
                e.dma_start(out=d_acc[:, k * 256:(k + 1) * 256],
                            in_=sXC[k * 128:(k + 1) * 128,
                                    tt * TB:tt * TB + 256])
                for k in range(16)][-1], dma=True)
            cx.on("sync", lambda e, tt=tt: [
                e.dma_start(out=d_cb[:, k * 256:(k + 1) * 256],
                            in_=sXC[k * 128:(k + 1) * 128,
                                    tt * TB + 256:(tt + 1) * TB])
                for k in range(16)][-1], dma=True)

            def mmx(t, tt=tt):
                for k in range(16):
                    t.matmul(pb[6][0:33, 0:256], xpt[:, k * 33:(k + 1) * 33],
                             d_acc[:, k * 256:(k + 1) * 256],
                             start=(k == 0), stop=(k == 15), **MM)
                    last = t.matmul(pb[7][0:33, 0:256],
                                    xpt[:, k * 33:(k + 1) * 33],
                                    d_cb[:, k * 256:(k + 1) * 256],
                                    start=(k == 0), stop=(k == 15), **MM)
                return last

            cx.on("tensor", mmx)

            def xpev(s):
                s.copy(xps[:, 0:256], pb[6][0:33, 0:256])
                return s.copy(xps[:, 256:512], pb[7][0:33, 0:256])

            cx.on("scalar", xpev)
            cx.on("sync", lambda e, tt=tt: e.dma_start(
                out=sXP[:, tt * TB:(tt + 1) * TB], in_=xps[:]), dma=True)

        ckpt("PD")

        cx.scope("PE")
        ar = Arena()
        e_bbc = ar.al("e_bbc", [128, 16 * TB])
        e_cbc = ar.al("e_cbc", [128, 16 * TB])
        e_da = ar.al("e_da", [128, 16 * TB])
        e_m = ar.al("e_m", [128, 16 * TB])
        e_dtb = ar.al("e_dtb", [128, TB])
        e_row = ar.al("e_row", [1, 4 * TB])
        e_dt = ar.al("e_dt", [128, TB])
        e_sp1 = ar.al("e_sp1", [128, TB])
        e_u = ar.al("e_u", [128, TB])
        e_xc = ar.al("e_xc", [128, 2 * TB])
        e_sz = ar.al("e_sz", [128, 2 * TB])
        e_y = ar.al("e_y", [128, 2 * TB])
        e_st = ar.al("e_st", [128, 16 * 16])

        cx.on("vector", lambda v: v.memset(e_st[:], 0.0))
        pe_ld = [0] * 16
        pe_gm = [0] * 16
        pe_st = [0] * 16
        for c in range(NT):
            cx.on("sync", lambda e, c=c: e.dma_start(
                out=e_row[0:1, 0:TB], in_=sXP[0:1, c * TB:(c + 1) * TB]), dma=True)
            cx.on("tensor", lambda t: t.matmul(pb[4][:, 0:TB], ones1r[:],
                                               e_row[0:1, 0:TB], start=True,
                                               stop=True, **MM))
            cx.on("scalar", lambda s: s.copy(e_dtb[:], pb[4][:, 0:TB]))
            for grp, dst in ((1, e_bbc), (17, e_cbc)):
                for n0 in range(0, 16, 4):
                    cx.on("sync", lambda e, c=c, grp=grp, n0=n0: e.dma_start(
                        out=e_row.ap().rearrange("p (n t) -> p n t", n=4),
                        in_=sXP[grp + n0:grp + n0 + 4, c * TB:(c + 1) * TB]),
                        dma=True)

                    def bmm(t):
                        for i in range(4):
                            last = t.matmul(pb[i][:, 0:TB], ones1r[:],
                                            e_row[0:1, i * TB:(i + 1) * TB],
                                            start=True, stop=True, **MM)
                        return last

                    def bcp(s, n0=n0, dst=dst):
                        for i in range(4):
                            last = s.copy(dst[:, (n0 + i) * TB:(n0 + i + 1) * TB],
                                          pb[i][:, 0:TB])
                        return last

                    cx.on("tensor", bmm)
                    cx.on("scalar", bcp)
            for db in range(16):
                def ldx(e, db=db, c=c):
                    b = (db % 2) * TB
                    e.dma_start(out=e_xc[:, b:b + TB],
                                in_=sXC[db * 128:(db + 1) * 128,
                                        c * TB:(c + 1) * TB])
                    e.dma_start(out=e_sz[:, b:b + TB],
                                in_=sSZ[db * 128:(db + 1) * 128,
                                        c * TB:(c + 1) * TB])

                if db == 0:
                    pe_ld[0] = cx.dma_async(ldx, after=pe_gm[14] or None)
                if db + 1 < 16:
                    anc = pe_gm[db - 1] if db >= 1 else (pe_gm[15] or None)
                    pe_ld[db + 1] = cx.dma_async(
                        lambda e, db=db, c=c: ldx(e, db + 1, c), after=anc)

                def sp(s, db=db):
                    s.activation(e_sp1[:], e_dtb[:], AF.Abs,
                                 scale=pvec[:, 336 + db:337 + db],
                                 bias=pvec[:, 352 + db:353 + db])
                    s.activation(e_sp1[:], e_sp1[:], AF.Exp, scale=-1.0)
                    s.activation(e_sp1[:], e_sp1[:], AF.Ln, bias=1.0)
                    return s.activation(e_dt[:], e_dtb[:], AF.Relu,
                                        scale=pvec[:, 336 + db:337 + db],
                                        bias=pvec[:, 352 + db:353 + db])

                cx.on("scalar", sp)

                def mkdt(v, db=db):
                    b = (db % 2) * TB
                    v.tensor_add(e_dt[:], e_dt[:], e_sp1[:])
                    return v.tensor_tensor(e_u[:], e_dt[:], e_xc[:, b:b + TB],
                                           ALU.mult)

                cx.on("vector", mkdt, pre_dma=pe_ld[db])

                def dA(s, db=db):
                    for n in range(16):
                        last = s.activation(e_da[:, n * TB:(n + 1) * TB], e_dt[:],
                                            AF.Exp,
                                            scale=pvec[:, db * 16 + n:
                                                       db * 16 + n + 1])
                    return last

                def dbx(v):
                    return v.tensor_tensor(
                        e_m.ap().rearrange("p (n t) -> p n t", n=16),
                        e_u.ap().rearrange("p (o t) -> p o t", o=1)
                        .broadcast_to([128, 16, TB]),
                        e_bbc.ap().rearrange("p (n t) -> p n t", n=16),
                        ALU.mult)

                cx.par([("scalar", dA), ("vector", dbx)])

                def scan(v, db=db):
                    for n in range(16):
                        sl = slice(n * TB, (n + 1) * TB)
                        last = v.tensor_tensor_scan(
                            e_m[:, sl], e_da[:, sl], e_m[:, sl],
                            e_st[:, db * 16 + n:db * 16 + n + 1],
                            ALU.mult, ALU.add)
                    return last

                cx.on("vector", scan)
                cx.on("vector", lambda v, db=db: v.tensor_copy(
                    e_st[:, db * 16:(db + 1) * 16],
                    e_m.ap().rearrange("p (n t) -> p n t", n=16)[:, :, TB - 1:TB]))

                def redv(v):
                    v.tensor_tensor(e_m[:, 0:8 * TB], e_m[:, 0:8 * TB],
                                    e_cbc[:, 0:8 * TB], ALU.mult)
                    v.tensor_tensor(e_m[:, 0:4 * TB], e_m[:, 0:4 * TB],
                                    e_m[:, 4 * TB:8 * TB], ALU.add)
                    v.tensor_tensor(e_m[:, 0:2 * TB], e_m[:, 0:2 * TB],
                                    e_m[:, 2 * TB:4 * TB], ALU.add)
                    return v.tensor_tensor(e_m[:, 0:TB], e_m[:, 0:TB],
                                           e_m[:, TB:2 * TB], ALU.add)

                def redg(g):
                    o = 8 * TB
                    g.tensor_tensor(e_m[:, o:o + 8 * TB], e_m[:, o:o + 8 * TB],
                                    e_cbc[:, o:o + 8 * TB], ALU.mult)
                    g.tensor_tensor(e_m[:, o:o + 4 * TB], e_m[:, o:o + 4 * TB],
                                    e_m[:, o + 4 * TB:o + 8 * TB], ALU.add)
                    g.tensor_tensor(e_m[:, o:o + 2 * TB], e_m[:, o:o + 2 * TB],
                                    e_m[:, o + 2 * TB:o + 4 * TB], ALU.add)
                    return g.tensor_tensor(e_m[:, o:o + TB], e_m[:, o:o + TB],
                                           e_m[:, o + TB:o + 2 * TB], ALU.add)

                cx.par([("vector", redv), ("gpsimd", redg)])

                def gate(v, db=db):
                    b = (db % 2) * TB
                    v.tensor_tensor(e_y[:, b:b + TB], e_m[:, 0:TB],
                                    e_m[:, 8 * TB:9 * TB], ALU.add)
                    v.scalar_tensor_tensor(e_y[:, b:b + TB], e_xc[:, b:b + TB],
                                           pvec[:, 256 + db:257 + db],
                                           e_y[:, b:b + TB], ALU.mult, ALU.add)
                    return v.tensor_tensor(e_y[:, b:b + TB], e_y[:, b:b + TB],
                                           e_sz[:, b:b + TB], ALU.mult)

                cx.on("vector", gate,
                      pre_dma=pe_st[db - 2] if db >= 2
                      else (pe_st[14 + db] or None))
                pe_gm[db] = cx.mark()
                pe_st[db] = cx.dma_async(lambda e, db=db, c=c: e.dma_start(
                    out=sG[db * 128:(db + 1) * 128, c * TB:(c + 1) * TB],
                    in_=e_y[:, (db % 2) * TB:(db % 2) * TB + TB]),
                    after=pe_gm[db])

        ckpt("PE")

        cx.scope("PF")
        cx.on("vector", lambda v: v.memset(tmp128[:], 0.0), pre_dma=cx.dn)
        ar = Arena()
        WOPT = ar.al("wopt", [128, 16 * DM])
        f_wst = ar.al("f_wst", [128, DI])
        f_g16 = ar.al("f_g16", [128, 16 * TB])
        f_rsb = ar.al("f_rsb", [128, 8 * TB])
        f_r2 = ar.al("f_r2", [128, 8 * TB])
        f_xr = ar.al("f_xr", [128, 8 * TB])

        for m in range(8):
            cx.on("sync", dma(f_wst[:, 0:DI], w_op[m * 128:(m + 1) * 128, :]),
                  dma=True)
            for kt0 in range(0, 16, 4):
                kts = list(range(kt0, kt0 + 4))

                def tr(t, kts=kts):
                    for kt in kts:
                        last = t.transpose(pb[4 + kt % 4][:, 0:128],
                                           f_wst[:, kt * 128:kt * 128 + 128],
                                           ident[:])
                    return last

                def cp(s, kts=kts, m=m):
                    for kt in kts:
                        last = s.copy(WOPT[:, kt * DM + m * 128:kt * DM + (m + 1) * 128],
                                      pb[4 + kt % 4][:, 0:128])
                    return last

                cx.on("tensor", tr)
                cx.on("scalar", cp)

        pf_ld = [0] * NT
        pf_rm = [0] * NT
        pf_st = [0] * NT

        def gld(e, tt):
            for k in range(16):
                e.dma_start(
                    out=f_g16[:, k * TB:(k + 1) * TB],
                    in_=sG[k * 128:(k + 1) * 128, tt * TB:(tt + 1) * TB])
            for m in range(8):
                e.dma_start(
                    out=f_xr[:, m * TB:(m + 1) * TB],
                    in_=sX2[m * 128:(m + 1) * 128, tt * TB:(tt + 1) * TB])

        pf_ld[0] = cx.dma_async(lambda e: gld(e, 0))
        for tt in range(NT):
            def mm(t):
                for k in range(16):
                    for m in range(8):
                        last = t.matmul(pb[m][:, 0:TB],
                                        WOPT[:, k * DM + m * 128:k * DM + (m + 1) * 128],
                                        f_g16[:, k * TB:(k + 1) * TB],
                                        start=(k == 0), stop=(k == 15), **MM)
                return last

            cx.on("tensor", mm, pre_dma=pf_ld[tt])
            mmk = cx.mark()

            def radd(v):
                for m in range(8):
                    last = v.tensor_add(f_rsb[:, m * TB:(m + 1) * TB],
                                        pb[m][:, 0:TB],
                                        f_xr[:, m * TB:(m + 1) * TB])
                return last

            cx.on("vector", radd,
                  pre_dma=pf_st[tt - 1] if tt >= 1 else None)
            pf_rm[tt] = cx.mark()
            if tt + 1 < NT:
                pf_ld[tt + 1] = cx.dma_async(
                    lambda e, tt=tt: gld(e, tt + 1), after=pf_rm[tt])

            def rsq(s):
                for m in range(8):
                    last = s.activation(f_r2[:, m * TB:(m + 1) * TB],
                                        f_rsb[:, m * TB:(m + 1) * TB], AF.Square)
                return last

            cx.on("scalar", rsq)

            def mm_st(t):
                for m in range(8):
                    t.matmul(pb[6][0:1, 0:TB], ones128[:, 0:1],
                             f_rsb[:, m * TB:(m + 1) * TB], start=(m == 0),
                             stop=(m == 7), **MM)
                    last = t.matmul(pb[7][0:1, 0:TB], ones128[:, 0:1],
                                    f_r2[:, m * TB:(m + 1) * TB], start=(m == 0),
                                    stop=(m == 7), **MM)
                return last

            cx.on("tensor", mm_st)
            emit_ln(f_rsb, None, tt)
            pf_st[tt] = cx.dma_async(lambda e, tt=tt: [
                e.dma_start(out=sX3[m * 128:(m + 1) * 128,
                                    tt * TB:(tt + 1) * TB],
                            in_=f_rsb[:, m * TB:(m + 1) * TB])
                for m in range(8)], after=cx.mark())

        ckpt("PF")

        cx.scope("PG")
        cx.on("vector", lambda v: v.memset(tmp128[:], 0.0), pre_dma=cx.dn)
        ar = Arena()
        WLT = ar.al("wlt", [128, 8 * DM])
        g_w = ar.al("g_w", [128, DM])
        g_i = ar.al("g_i", [128, S], I32)
        g_am = ar.al("g_am", [128, 128])
        g_sc = ar.al("g_sc", [128, 128])
        g_e = ar.al("g_e", [128, 128], I32)
        g_dw = ar.al("g_dw", [128, 32])
        g_a = ar.al("g_a", [128, S])
        g_g2 = ar.al("g_g2", [128, S])
        g_sg = ar.al("g_sg", [128, S])
        g_x3 = ar.al("g_x3", [128, 8 * TB])
        g_xt = ar.al("g_xt", [128, 4 * DM])
        g_rsb = ar.al("g_rsb", [128, 8 * TB])
        g_r2 = ar.al("g_r2", [128, 8 * TB])

        for m in range(8):
            cx.on("sync", dma(g_dw[:, 0:32], dw_l1[m * 128:(m + 1) * 128, :]),
                  dma=True)
            cx.on("scalar", lambda s: s.activation(g_dw[:, 0:32], g_dw[:, 0:32],
                                                   AF.Tanh, scale=4.0))
            cx.on("scalar", lambda s: s.activation(g_dw[:, 0:32], g_dw[:, 0:32],
                                                   AF.Exp, scale=0.5 * LN2))
            cx.on("sync", dma(g_w[:, 0:DM], w_l1[m * 128:(m + 1) * 128, :]),
                  dma=True)
            mxfp4_emit(g_w[:, 0:DM], g_i[:, 0:DM], g_am[:, 0:32], g_sc[:, 0:32],
                       g_e[:, 0:32], g_dw[:, 0:32], g_a[:, 0:DM], g_g2[:, 0:DM],
                       g_sg[:, 0:DM], DM)

            def trw(t):
                for kt in range(8):
                    last = t.transpose(pb[kt][:, 0:128],
                                       g_w[:, kt * 128:kt * 128 + 128], ident[:])
                return last

            def cpw(s, m=m):
                for kt in range(8):
                    last = s.copy(WLT[:, kt * DM + m * 128:kt * DM + (m + 1) * 128],
                                  pb[kt][:, 0:128])
                return last

            cx.on("tensor", trw)
            cx.on("scalar", cpw)

        def tr8(src_t, src_base, dst_t, dst_base, src_stride, dst_stride):
            def tr(t):
                for kt in range(8):
                    last = t.transpose(
                        pb[kt][:, 0:128],
                        src_t[:, src_base + kt * src_stride:
                              src_base + kt * src_stride + 128], ident[:])
                return last

            def cp(s):
                for kt in range(8):
                    last = s.copy(dst_t[:, dst_base + kt * dst_stride:
                                        dst_base + kt * dst_stride + 128],
                                  pb[kt][:, 0:128])
                return last

            cx.on("tensor", tr)
            cx.on("scalar", cp)

        for c in range(NT):
            cx.on("sync", lambda e, c=c: [
                e.dma_start(out=g_x3[:, pt * TB:(pt + 1) * TB],
                            in_=sX3[pt * 128:(pt + 1) * 128, c * TB:(c + 1) * TB])
                for pt in range(8)][-1], dma=True)
            for i in range(4):
                tr8(g_x3, i * 128, g_xt, i * DM, TB, 128)
            mxfp4_emit(g_xt[:, 0:4 * DM], g_i[:, 0:4 * DM], g_am[:, 0:128],
                       g_sc[:, 0:128], g_e[:, 0:128], None, g_a[:, 0:4 * DM],
                       g_g2[:, 0:4 * DM], g_sg[:, 0:4 * DM], 4 * DM)
            for i in range(4):
                tr8(g_xt, i * DM, g_x3, i * 128, 128, TB)

            def mm(t):
                for k in range(8):
                    for m in range(8):
                        last = t.matmul(pb[m][:, 0:TB],
                                        WLT[:, k * DM + m * 128:k * DM + (m + 1) * 128],
                                        g_x3[:, k * TB:(k + 1) * TB],
                                        start=(k == 0), stop=(k == 7), **MM)
                return last

            cx.on("tensor", mm)
            cx.on("sync", lambda e, c=c: [
                e.dma_start(out=g_r2[:, m * TB:(m + 1) * TB],
                            in_=sX3[m * 128:(m + 1) * 128, c * TB:(c + 1) * TB])
                for m in range(8)][-1], dma=True)

            def radd(v):
                for m in range(8):
                    last = v.scalar_tensor_tensor(
                        g_rsb[:, m * TB:(m + 1) * TB], pb[m][:, 0:TB],
                        pvec[:, 384 + m:385 + m], g_r2[:, m * TB:(m + 1) * TB],
                        ALU.add, ALU.add)
                return last

            cx.on("vector", radd)

            def rsq(s):
                for m in range(8):
                    last = s.activation(g_r2[:, m * TB:(m + 1) * TB],
                                        g_rsb[:, m * TB:(m + 1) * TB], AF.Square)
                return last

            cx.on("scalar", rsq)

            def mm_st(t):
                for m in range(8):
                    t.matmul(pb[6][0:1, 0:TB], ones128[:, 0:1],
                             g_rsb[:, m * TB:(m + 1) * TB], start=(m == 0),
                             stop=(m == 7), **MM)
                    last = t.matmul(pb[7][0:1, 0:TB], ones128[:, 0:1],
                                    g_r2[:, m * TB:(m + 1) * TB], start=(m == 0),
                                    stop=(m == 7), **MM)
                return last

            cx.on("tensor", mm_st)
            emit_ln(g_rsb, None, c)
            for i in range(4):
                tr8(g_rsb, i * 128, g_xt, i * DM, TB, 128)
            cx.on("sync", lambda e, c=c: [
                e.dma_start(out=out_d[c * TB + i * 128:c * TB + (i + 1) * 128, :],
                            in_=g_xt[:, i * DM:(i + 1) * DM]) for i in range(4)][-1],
                dma=True)

        ckpt("PG")

        cx.stopped = False
        for name in debug:
            cx.on("sync", dma(dbg[name], dbg_src[name]), dma=True)

        cx.final_wait("sync")

    auto_end = 16512 + (212863 - nc.sbuf_bytes_remaining)
    assert auto_end <= ARENA0, f"auto sbuf end {auto_end} exceeds ARENA0 {ARENA0}"
    return nc, dbg


_BUILT = {}


def _get_built(debug=(), stop_at=None):
    key = (tuple(debug), stop_at)
    if key not in _BUILT:
        _BUILT[key] = build(debug, stop_at)
    return _BUILT[key]


def kernel(**inputs):
    nc, _ = _get_built(DEBUG, STOP_AT)
    consts = _host_constants()
    x = np.ascontiguousarray(np.asarray(inputs["L0_attn_x"], np.float32))
    base = {
        "w_qkv": np.ascontiguousarray(inputs["L0_attn_W_qkv"], np.float32),
        "w_o": np.ascontiguousarray(inputs["L0_attn_W_o"], np.float32),
        "w_gate": np.ascontiguousarray(inputs["L0_ffn_w_gate"], np.float32),
        "w_up": np.ascontiguousarray(inputs["L0_ffn_w_up"], np.float32),
        "w_down": np.ascontiguousarray(inputs["L0_ffn_w_down"], np.float32),
        "w_ip": np.ascontiguousarray(inputs["L1_attn_in_proj_w"], np.float32),
        "conv_w": np.ascontiguousarray(
            np.asarray(inputs["L1_attn_conv_w"], np.float32).reshape(DI, 4)),
        "conv_b": np.ascontiguousarray(
            np.asarray(inputs["L1_attn_conv_b"], np.float32).reshape(DI, 1)),
        "w_xp": np.ascontiguousarray(inputs["L1_attn_x_proj_w"], np.float32),
        "dt_w": np.ascontiguousarray(
            np.asarray(inputs["L1_attn_dt_proj_w"], np.float32).reshape(DI, 1)),
        "dt_b": np.ascontiguousarray(
            np.asarray(inputs["L1_attn_dt_proj_b"], np.float32).reshape(DI, 1)),
        "a_log": np.ascontiguousarray(inputs["L1_attn_A_log"], np.float32),
        "d_vec": np.ascontiguousarray(
            np.asarray(inputs["L1_attn_D"], np.float32).reshape(DI, 1)),
        "w_op": np.ascontiguousarray(inputs["L1_attn_out_proj_w"], np.float32),
        "w_l1": np.ascontiguousarray(inputs["L1_ffn_weight"], np.float32),
        "b_l1": np.ascontiguousarray(
            np.asarray(inputs["L1_ffn_bias"], np.float32).reshape(DM, 1)),
        "dw_l1": np.ascontiguousarray(inputs["L1_ffn_delta_w"], np.float32),
        "da_l1": np.ascontiguousarray(
            np.asarray(inputs["L1_ffn_delta_act"], np.float32).reshape(1, 1)),
    }
    base.update(consts)
    in_maps = []
    for b in range(NCORES):
        m = dict(base)
        m["x"] = np.ascontiguousarray(x[b])
        in_maps.append(m)
    res = run_bass_kernel_spmd(nc, in_maps, core_ids=list(range(NCORES)))
    out = np.stack([np.asarray(res.results[b]["out"]) for b in range(NCORES)],
                   axis=0)
    if DEBUG:
        kernel._dbg = [{k: np.asarray(res.results[b]["dbg_" + k]) for k in DEBUG}
                       for b in range(NCORES)]
    return out
